# version 36
# baseline (speedup 1.0000x reference)
import contextlib
import math

import numpy as np
import ml_dtypes

import concourse.bass as bass
import concourse.mybir as mybir
from concourse.bass import IndirectOffsetOnAxis
from concourse.bass_utils import run_bass_kernel_spmd
from bass_rust import AP

F32 = mybir.dt.float32
BF16 = mybir.dt.bfloat16
I32 = mybir.dt.int32
AF = mybir.ActivationFunctionType
ALU = mybir.AluOpType
AX = mybir.AxisListType
BF = ml_dtypes.bfloat16

N_CORES = 8


def make_cfg(S=2048, D=1024, F=4096, H=16, DH=64, E=8, R=16, SEQS=2,
             TC=512, QC=512, HPG=4):
    c = dict(S=S, D=D, F=F, H=H, DH=DH, E=E, R=R, SEQS=SEQS, NB=32, MAXD=128,
             TC=TC, QC=QC, HPG=HPG)
    c["NDC"] = D // 128
    c["NFC"] = F // 128
    c["NTT"] = S // 128
    c["NTC"] = S // TC
    c["NQC"] = S // QC
    c["NKC"] = S // 128
    c["NHG"] = H // HPG
    c["LT"] = 2 * S - 1
    c["LTP"] = ((2 * S - 1 + 511) // 512) * 512
    c["WBR"] = 2 * S - 128
    c["NMQ"] = H * DH // 128
    c["HPT"] = 128 // DH
    assert TC % 128 == 0 and QC % 128 == 0 and QC <= 512
    assert H % HPG == 0 and HPG % c["HPT"] == 0
    return c


CFG = make_cfg()


def rel_bucket_rev(S, NB=32, MAXD=128):
    half = NB // 2
    max_exact = half // 2
    coeff = (half - max_exact) / math.log(MAXD / max_exact)
    delta = np.arange(S - 1, -S, -1)
    arel = np.abs(delta)
    base = np.where(delta > 0, half, 0)
    distant = max_exact + (np.log(np.maximum(arel, 1).astype(np.float32) / max_exact)
                           * coeff).astype(np.int32)
    distant = np.minimum(distant, half - 1)
    return (base + np.where(arel < max_exact, arel, distant)).astype(np.int64)


def host_prep(cfg, w):
    S, D, F, H, DH, E, R, NB = (cfg[k] for k in "S D F H DH E R NB".split())
    out = {}
    out["qkv_wT"] = np.ascontiguousarray(w["qkv_w"].T)
    out["o_wT"] = np.ascontiguousarray(w["o_w"].T)
    out["w1T"] = np.ascontiguousarray(w["w1"].T)
    out["w3T"] = np.ascontiguousarray(w["w3"].T)
    out["w2T"] = np.ascontiguousarray(w["w2"].T)
    out["gate_wT"] = np.ascontiguousarray(w["gate_w"].T)
    out["emb"] = np.ascontiguousarray(w["emb_weight"])
    out["norm_w"] = np.ascontiguousarray(w["norm_w"].reshape(1, D))

    a1, a3 = w["a1"], w["a3"]
    a13p = np.zeros((D, 768), np.float32)
    a13n = np.zeros((D, 768), np.float32)
    for wi, a in ((0, a1), (1, a3)):
        for e in range(E):
            b = wi * E + e
            base = (b // 3) * 128 + 32 * (b % 3)
            a13p[:, base:base + R] = a[e].T
            if e > 0:
                a13n[:, base + 16:base + 16 + R] = a[e - 1].T
    out["a13P_pos"], out["a13P_neg"] = a13p, a13n

    b1, b3 = w["b1"], w["b3"]
    b13 = np.zeros((2, E, 32, F), np.float32)
    for wi, b in ((0, b1), (1, b3)):
        for e in range(E):
            b13[wi, e, 0:R] = b[e].T
            if e > 0:
                b13[wi, e, 16:16 + R] = b[e - 1].T
    out["b13P"] = np.ascontiguousarray(b13.reshape(2 * E * 32, F))

    out["a2T"] = np.ascontiguousarray(w["a2"].reshape(E * R, F).T)

    b2 = w["b2"]
    b2p = np.zeros((384, D), np.float32)
    for e in range(E):
        k, j = divmod(e, 3)
        b2p[128 * k + 32 * j: 128 * k + 32 * j + R] = b2[e].T
    out["b2T_pad"] = b2p

    LT, LTP = cfg["LT"], cfg["LTP"]
    bt = rel_bucket_rev(S, NB, cfg["MAXD"])
    ohr = np.zeros((NB, LTP), BF)
    ohr[bt, np.arange(LT)] = BF(1.0)
    out["ohr"] = ohr
    out["gidx"] = (127 - np.arange(128, dtype=np.int32)).reshape(128, 1)
    out["id_bf"] = np.eye(128, dtype=BF)
    NMQ = cfg["NMQ"]
    sel = np.zeros((128, NMQ * 16), BF)
    for mc in range(NMQ):
        sel[0:64, mc * 16 + 2 * mc] = BF(1.0)
        sel[64:128, mc * 16 + 2 * mc + 1] = BF(1.0)
    out["sels"] = sel
    sel8 = np.zeros((E, E * 128), BF)
    for e in range(E):
        sel8[e, e * 128:(e + 1) * 128] = BF(1.0)
    out["sel8"] = sel8
    return out


class B:
    taps = False
    _pf_attn = False
    _vhgsem = "vhgl"
    phases = "lfa"
    aparts = "qvbxo"

    def __init__(self, cfg, dbg=False):
        self.cfg = cfg
        self.nc = bass.Bass("TRN2", target_bir_lowering=False, debug=dbg)
        for k, v in cfg.items():
            setattr(self, k, v)
        self.HDx = self.H * self.DH
        self.c = {}
        self.sems = {}
        self.st = None
        self._khgsem = {0: "khgl", 1: "khgl"}
        self._vhgsem = "vhgl"

    def sem(self, name):
        if name not in self.sems:
            self.sems[name] = self.st.enter_context(self.nc.semaphore(name))
            self.c[name] = 0
        return self.sems[name]

    def inc(self, instr, name, n=1):
        instr.then_inc(self.sem(name), n)
        self.c[name] += n
        return self.c[name]

    def cur(self, name):
        self.sem(name)
        return self.c[name]

    def wait(self, eng, name, value=None):
        if value is None:
            value = self.c[name]
        assert value <= self.c.get(name, 0), (name, value)
        if value > 0:
            eng.wait_ge(self.sems[name], value)

    def sb(self, n, sh, dt=F32):
        return self.st.enter_context(
            self.nc.sbuf_tensor(n, sh, dt, allow_name_mangling=True))

    def psb(self, n, sh, dt=F32):
        return self.st.enter_context(self.nc.psum_tensor(n, sh, dt))

    def barrier(self):
        self.drain_stores()
        self.nc.all_engine_barrier()
        self.c = {k: v for k, v in self.c.items() if not k.startswith("_")}

    def soft_barrier(self):
        self.drain_stores()
        for q in (self.sy, self.ge):
            for s in ("pe", "dve", "sc", "po"):
                if self.c.get(s, 0) > 0:
                    q.wait_ge(self.sems[s], self.c[s])

    def drain_stores(self):
        for name in list(self.c):
            if name.startswith(("sto", "dstg", "xnst", "qkst", "w13l", "b13l",
                                "w2l", "qkvwl", "vwl", "khgl", "vhgl", "qsll",
                                "xrl", "brwg", "xld", "tapst")):
                if self.c[name] > 0:
                    self.wait(self.sy, name, self.c[name])

    def dw(self):
        if self.c.get("dve", 0) > 0:
            self.ve.wait_ge(self.sems["dve"], self.c["dve"])

    def sw(self):
        if self.c.get("sc", 0) > 0:
            self.se.wait_ge(self.sems["sc"], self.c["sc"])

    def gw(self):
        if self.c.get("po", 0) > 0:
            self.ge.wait_ge(self.sems["po"], self.c["po"])

    def build(self):
        nc, cfg = self.nc, self.cfg
        S, D, F, H, DH, E, R, NB = (cfg[k] for k in "S D F H DH E R NB".split())
        NDC, NFC, NTT, TC, NTC = (cfg[k] for k in "NDC NFC NTT TC NTC".split())
        LTP, NMQ, SEQS = cfg["LTP"], cfg["NMQ"], cfg["SEQS"]
        HDx = self.HDx

        din = lambda n, sh, dt=F32: nc.dram_tensor(n, sh, dt, kind="ExternalInput").ap()
        self.x_sh = din("x_sh", [SEQS, S, D])
        self.qkv_wT = din("qkv_wT", [D, 3 * HDx])
        self.o_wT = din("o_wT", [self.HDx, D])
        self.w1T = din("w1T", [D, F])
        self.w3T = din("w3T", [D, F])
        self.w2T = din("w2T", [F, D])
        self.gate_wT = din("gate_wT", [D, E])
        self.emb = din("emb", [NB, H])
        self.norm_w = din("norm_w", [1, D])
        self.a13P_pos = din("a13P_pos", [D, 768])
        self.a13P_neg = din("a13P_neg", [D, 768])
        self.b13P = din("b13P", [2 * E * 32, F])
        self.a2T = din("a2T", [F, E * R])
        self.b2T_pad = din("b2T_pad", [384, D])
        self.ohr = din("ohr", [NB, LTP], BF16)
        self.gidx = din("gidx", [128, 1], I32)
        self.id_bf = din("id_bf", [128, 128], BF16)
        self.sels = din("sels", [128, NMQ * 16], BF16)
        self.sel8 = din("sel8", [E, E * 128], BF16)
        self.out_sh = nc.dram_tensor("out_sh", [SEQS, S, D], F32,
                                     kind="ExternalOutput").ap()

        dscr = lambda n, sh, dt=BF16: nc.dram_tensor(n, sh, dt, kind="Internal")
        self.xn_hbm = dscr("xn_hbm", [S, D])
        self.nx_hbm = dscr("nx_hbm", [S, D])
        self.q_hbm = dscr("q_hbm", [HDx, S])
        self.k_hbm = dscr("k_hbm", [HDx, S])
        self.v_hbm = dscr("v_hbm", [S, HDx])
        self.ffn_hbm = dscr("ffn_hbm", [S, D])
        self.ebr_hbm = dscr("ebr_hbm", [H, LTP])
        self.w1_bf = dscr("w1_bf", [D, F])
        self.w3_bf = dscr("w3_bf", [D, F])
        self.w2_bf = dscr("w2_bf", [F, D])
        self.qkvT_bf = dscr("qkvT_bf", [D, 3 * HDx])
        self.owT_bf = dscr("owT_bf", [self.HDx, D])
        self.b13_bf = dscr("b13_bf", [2 * E * 32, F])
        if self.taps:
            self.tap_r13 = dscr("tap_r13", [128, 8 * S])
            self.tap_wT = dscr("tap_wT", [E, S])
            self.tap_adm = dscr("tap_adm", [128, cfg["NMQ"] * S])
            self.tap_wact = dscr("tap_wact", [cfg["NTC"], 128, cfg["NFC"] * cfg["TC"]])
            self.tap_pex = dscr("tap_pex", [128, cfg["QC"]])
            self.tap_ptil = dscr("tap_ptil", [128, cfg["QC"]])
            self.tap_av = dscr("tap_av", [65, cfg["QC"]])
            self.tap_brw = dscr("tap_brw", [128, cfg["WBR"]])
            self.tap_aw = dscr("tap_aw", [cfg["E"], 128, cfg["TC"]])
            self.tap_wb = dscr("tap_wb", [128, cfg["E"] * cfg["TC"]])
            self.tap_h1 = dscr("tap_h1", [cfg["E"], 128, cfg["TC"]], mybir.dt.float32)
            self.tap_h3 = dscr("tap_h3", [cfg["E"], 128, cfg["TC"]], mybir.dt.float32)
            self.tap_r2w = dscr("tap_r2w", [3, 128, cfg["TC"]])

        with contextlib.ExitStack() as st:
            self.st = st
            te, ve, se, sy, ge = nc.tensor, nc.vector, nc.scalar, nc.sync, nc.gpsimd
            self.te, self.ve, self.se, self.sy, self.ge = te, ve, se, sy, ge

            self.ones1 = self.sb("ones1", [1, 128], BF16)
            self.id_sb = self.sb("id_sb", [128, 128], BF16)
            self.sels_sb = self.sb("sels_sb", [128, NMQ * 16], BF16)
            self.sel8_sb = self.sb("sel8_sb", [8, 8 * 128], BF16)
            self.gidx_sb = st.enter_context(nc.sbuf_tensor("gidx_sb", [128, 1], I32))
            self.emb_sb = self.sb("emb_sb", [NB, H], BF16)
            self.nwb = self.sb("nwb", [128, D], BF16)
            self.gate_sb = self.sb("gate_sb", [128, NDC * E], BF16)
            self.a2T_sb = self.sb("a2T_sb", [128, NFC * E * R], BF16)
            self.b2p_sb = self.sb("b2p_sb", [128, 3 * D], BF16)
            self.biasmat = self.sb("biasmat", [128, H], F32)
            self.w_T = self.sb("w_T", [E, self.cfg["TC"]], BF16)
            self.qm2 = self.sb("qm2", [H, 2], F32)
            self.small = self.sb("small", [128, 32], F32)
            self.eps = self.sb("eps", [128, 4], F32)
            self.banks = [self.psb(f"bank{i}", [128, 512]) for i in range(8)]
            cfg_ = self.cfg
            self.khg0 = self.sb("khg0", [128, (cfg_["HPG"] // self.HPT) * self.S], BF16)
            self.vhg = self.sb("vhg", [128, self.NTT * cfg_["HPG"] * (self.DH + 1)], BF16)

            self.prelude()
            if "l" in self.phases:
                self.seq = 0
                self.phase_ln()
                self.soft_barrier()
            for seq in range(SEQS):
                self.seq = seq
                if "f" in self.phases:
                    with contextlib.ExitStack() as ffn_st:
                        outer = self.st
                        self.st = ffn_st
                        self.phase_ffn()
                        self.st = outer
                    self.soft_barrier()
                if "a" in self.phases:
                    with contextlib.ExitStack() as att_st:
                        outer = self.st
                        self.st = att_st
                        self.phase_attn()
                        self.st = outer
                    self.soft_barrier()
            if "sto0" in self.c:
                self.wait(sy, "sto0")
                self.wait(sy, "sto1")
            self.drain_stores()
            if self.taps and "tapst" in self.c:
                self.wait(sy, "tapst")
            self.st = None
        return nc

    def prelude(self):
        nc, te, ve, se, sy, ge = self.nc, self.te, self.ve, self.se, self.sy, self.ge
        D, F, E, NB, H, NDC, NMQ, LTP = (self.D, self.F, self.E, self.NB, self.H,
                                         self.NDC, self.NMQ, self.LTP)
        banks = self.banks

        sy.dma_start(self.id_sb[:, :], self.id_bf).then_inc(self.sem("ldc"), 16)
        sy.dma_start(self.sels_sb[:, :], self.sels).then_inc(self.sem("ldc"), 16)
        sy.dma_start(self.sel8_sb[0:self.E, 0:self.E * 128], self.sel8).then_inc(self.sem("ldc"), 16)
        sy.dma_start(self.gidx_sb[:, :], self.gidx).then_inc(self.sem("ldc"), 16)
        self.c["ldc"] += 64

        ge.dma_start(self.emb_sb[:, :], self.emb).then_inc(self.sem("ldg"), 16)
        ge.dma_start(self.gate_sb[:, :].rearrange("p (c e) -> p c e", c=NDC),
                     self.gate_wT.rearrange("(c p) e -> p c e", p=128)
                     ).then_inc(self.sem("ldg"), 16)
        ge.dma_start(self.a2T_sb[:, :].rearrange("p (c n) -> p c n", c=self.NFC),
                     self.a2T.rearrange("(c p) n -> p c n", p=128)
                     ).then_inc(self.sem("ldg"), 16)
        ge.dma_start(self.b2p_sb[:, :].rearrange("p (k n) -> p k n", k=3),
                     self.b2T_pad.rearrange("(k p) n -> p k n", p=128)
                     ).then_inc(self.sem("ldg"), 16)
        self.c["ldg"] += 64
        nw_row = self.sb("nw_row", [1, D], BF16)
        ge.dma_start(nw_row[:, :], self.norm_w).then_inc(self.sem("ldg"), 16)
        self.c["ldg"] += 16

        for dst, src, nm in ((self.qkvT_bf, self.qkv_wT, "wcqkv"),
                             (self.w1_bf, self.w1T, "wcw1"),
                             (self.w3_bf, self.w3T, "wcw3"),
                             (self.b13_bf, self.b13P, "wcb13"),
                             (self.w2_bf, self.w2T, "wcw2"),
                             (self.owT_bf, self.o_wT, "wcow")):
            ge.dma_start(dst.ap(), src).then_inc(self.sem(nm), 16)
            self.c[nm] += 16

        self.wait(ve, "ldc")
        self.dw()
        self.inc(ve.memset(self.ones1[:, :], 1.0), "dve")
        self.dw()
        self.inc(ve.memset(self.eps[:, 0:1], 1e-5), "dve")
        self.dw()
        self.inc(ve.memset(self.eps[:, 1:2], 1e-6), "dve")
        self.dw()
        self.inc(ve.memset(self.eps[:, 2:3], 1e-30), "dve")
        self.dw()
        self.inc(ve.memset(self.eps[:, 3:4], -2.0794415416798357), "dve")
        vb = self.vhg
        vones = AP(vb[:, :].tensor, vb[:, :].offset + self.DH,
                   [list(vb[:, :].ap[0]),
                    [self.DH + 1, self.NTT * self.cfg["HPG"]], [1, 1]])
        self.dw()
        self.inc(ve.memset(vones, 1.0), "dve")
        self.c["_vones"] = self.cur("dve")
        self.c["_preeps"] = self.cur("dve")

        self.wait(te, "ldg")
        self.wait(te, "dve")
        for j in range(D // 512):
            bk = banks[j % 2]
            if j >= 2:
                self.wait(te, "dve", self.c[f"_nwb{j % 2}"])
            mm = te.matmul(bk[:, :], self.ones1[:, :], nw_row[:, j * 512:(j + 1) * 512],
                      start=True, stop=True)
            self.inc(mm, "pe")
            self.wait(ve, "pe")
            self.dw()
            self.c[f"_nwb{j % 2}"] = self.inc(
                ve.tensor_copy(self.nwb[:, j * 512:(j + 1) * 512], bk[:, :]), "dve")

        self.a13p_sb = self.sb("a13p_sb", [128, NDC * 768], BF16)
        pre_st = contextlib.ExitStack()
        outer_st = self.st
        self.st = pre_st
        ohr_sb = self.sb("ohr_sb", [NB, LTP], BF16)
        sy.dma_start(ohr_sb[:, :], self.ohr).then_inc(self.sem("ldo"), 16)
        self.c["ldo"] += 16
        ebr_stage = self.sb("ebr_stage", [H, LTP], BF16)
        self.wait(te, "ldo")
        self.wait(te, "dve")
        for j in range(LTP // 512):
            bk = banks[j % 2]
            if j >= 2:
                self.wait(te, "sc", self.c[f"_ebr{j % 2}"])
            mm = te.matmul(bk[0:H, :], self.emb_sb[:, :],
                      ohr_sb[:, j * 512:(j + 1) * 512], start=True, stop=True)
            self.inc(mm, "pe")
            self.wait(se, "pe")
            self.sw()
            self.c[f"_ebr{j % 2}"] = self.inc(
                se.activation(ebr_stage[:, j * 512:(j + 1) * 512], bk[0:H, :],
                              AF.Exp), "sc")
        self.wait(sy, "sc")
        sy.dma_start(self.ebr_hbm.ap(), ebr_stage[:, :]).then_inc(self.sem("ebr"), 16)
        self.c["ebr"] += 16

        a13_stage = self.sb("a13_stage", [128, 1536], F32)
        for dc in range(NDC):
            if dc > 0:
                self.wait(sy, "dve")
            sy.dma_start(a13_stage[:, 0:768],
                         self.a13P_pos[dc * 128:(dc + 1) * 128, :]
                         ).then_inc(self.sem("a13"), 16)
            sy.dma_start(a13_stage[:, 768:1536],
                         self.a13P_neg[dc * 128:(dc + 1) * 128, :]
                         ).then_inc(self.sem("a13"), 16)
            self.c["a13"] += 32
            self.wait(ve, "a13")
            self.dw()
            self.inc(ve.scalar_tensor_tensor(
                self.a13p_sb[:, dc * 768:(dc + 1) * 768],
                a13_stage[:, 768:1536], -1.0, a13_stage[:, 0:768],
                op0=ALU.mult, op1=ALU.add), "dve")
        self.c["_a13p"] = self.cur("dve")

        for b in (4, 5, 6):
            self.dw()
            self.inc(ve.memset(banks[b][:, :], 0.0), "dve")
        self.wait(sy, "ebr", 16)
        self.soft_barrier()
        self.st = outer_st
        pre_st.close()

    def phase_ln(self):
        nc, te, ve, se, sy = self.nc, self.te, self.ve, self.se, self.sy
        D, NTT, seq = self.D, self.NTT, self.seq
        ln_st = contextlib.ExitStack()
        outer_st = self.st
        self.st = ln_st
        xload = [self.sb(f"xload{i}", [128, D]) for i in range(2)]
        xn_stage = [self.sb(f"xn_stage{i}", [128, D], BF16) for i in range(2)]
        nx_stage = [self.sb(f"nx_stage{i}", [128, D], BF16) for i in range(2)]
        sqdump = [self.sb(f"sqdump{i}", [128, D]) for i in range(2)]
        stat = [self.sb(f"stat{i}", [128, 8]) for i in range(2)]
        self.st = outer_st
        nwb = self.nwb

        for tt in range(NTT):
            slot = tt % 2
            stt = stat[slot]
            if f"_lnxv{slot}" in self.c:
                self.wait(sy, "dve", self.c[f"_lnxv{slot}"])
                self.wait(sy, "sc", self.c[f"_lnxs{slot}"])
            sy.dma_start(xload[slot][:, :], self.x_sh[seq, tt * 128:(tt + 1) * 128, :]
                         ).then_inc(self.sem(f"xld{slot}"), 16)
            self.c[f"xld{slot}"] += 16
            self.wait(ve, f"xld{slot}")
            if f"_lnsc{slot}" in self.c:
                self.wait(ve, "sc", self.c[f"_lnsc{slot}"])
            self.dw()
            self.inc(ve.tensor_reduce(stt[:, 0:1], xload[slot][:, :],
                                      axis=AX.X, op=ALU.add), "dve")
            self.dw()
            self.inc(ve.tensor_scalar(stt[:, 0:1], stt[:, 0:1], 1.0 / D, None,
                                      op0=ALU.mult), "dve")
            mn_done = self.cur("dve")
            self.wait(se, f"xld{slot}")
            self.wait(se, "dve", mn_done)
            self.sw()
            self.inc(se.activation(sqdump[slot][:, :], xload[slot][:, :], AF.Square,
                                   accum_out=stt[:, 1:2]), "sc")
            sq_done = self.cur("sc")
            self.c[f"_lnxs{slot}"] = sq_done
            self.wait(ve, "sc", sq_done)
            self.dw()
            self.inc(ve.scalar_tensor_tensor(stt[:, 2:3], stt[:, 0:1], 0.0,
                                             stt[:, 0:1], op0=ALU.bypass,
                                             op1=ALU.mult), "dve")
            self.dw()
            self.inc(ve.scalar_tensor_tensor(stt[:, 2:3], stt[:, 1:2], 1.0 / D,
                                             stt[:, 2:3], op0=ALU.mult,
                                             op1=ALU.subtract), "dve")
            var_done = self.cur("dve")
            self.wait(se, "dve", var_done)
            self.sw()
            self.inc(se.activation(stt[:, 3:4], stt[:, 2:3], AF.Ln,
                                   bias=self.eps[:, 0:1], scale=1.0), "sc")
            self.sw()
            self.inc(se.activation(stt[:, 4:5], stt[:, 3:4], AF.Exp,
                                   bias=0.0, scale=-1.0), "sc")
            self.sw()
            self.inc(se.activation(stt[:, 3:4], stt[:, 3:4], AF.Exp,
                                   bias=0.0, scale=-0.5), "sc")
            rcp_done = self.cur("sc")
            self.wait(ve, "sc", rcp_done)
            self.dw()
            self.inc(ve.scalar_tensor_tensor(stt[:, 5:6], stt[:, 2:3], 0.0,
                                             stt[:, 4:5], op0=ALU.bypass,
                                             op1=ALU.mult), "dve")
            u_done = self.cur("dve")
            self.wait(se, "dve", u_done)
            self.sw()
            self.inc(se.activation(stt[:, 6:7], stt[:, 5:6], AF.Ln,
                                   bias=self.eps[:, 1:2], scale=1.0), "sc")
            self.sw()
            self.inc(se.activation(stt[:, 6:7], stt[:, 6:7], AF.Exp,
                                   bias=0.0, scale=-0.5), "sc")
            s_done = self.cur("sc")
            self.c[f"_lnsc{slot}"] = s_done
            self.wait(ve, "sc", s_done)
            stk = f"_xnst{slot}"
            if stk in self.c:
                self.wait(ve, f"xnst{slot}", self.c[stk])
            self.dw()
            self.inc(ve.tensor_scalar(xn_stage[slot][:, :], xload[slot][:, :],
                                      stt[:, 0:1], stt[:, 3:4],
                                      op0=ALU.subtract, op1=ALU.mult), "dve")
            self.dw()
            self.inc(ve.scalar_tensor_tensor(nx_stage[slot][:, :],
                                             xn_stage[slot][:, :], stt[:, 6:7],
                                             nwb[:, :], op0=ALU.mult,
                                             op1=ALU.mult), "dve")
            self.c[f"_ln{slot}"] = self.cur("dve")
            self.c[f"_lnxv{slot}"] = self.cur("dve")
            self.wait(sy, "dve", self.c[f"_ln{slot}"])

            self.inc(sy.dma_start(self.xn_hbm.ap()[tt * 128:(tt + 1) * 128, :],
                                  xn_stage[slot][:, :]), f"xnst{slot}", 16)
            self.inc(sy.dma_start(self.nx_hbm.ap()[tt * 128:(tt + 1) * 128, :],
                                  nx_stage[slot][:, :]), f"xnst{slot}", 16)
            self.c[stk] = self.cur(f"xnst{slot}")
            self.c[f"_xnsT{tt}"] = (slot, self.cur(f"xnst{slot}"))
        ln_st.close()

    def transpose_in(self, src_hbm, dst_sb, sem_name):
        S, NDC = self.S, self.NDC
        for dc in range(NDC):
            self.inc(self.sy.dma_start_transpose(
                dst_sb[:, dc * S:(dc + 1) * S],
                src_hbm.ap()[:, dc * 128:(dc + 1) * 128]), sem_name, 16)

    def phase_ffn(self):
        nc, te, ve, se, sy, ge = self.nc, self.te, self.ve, self.se, self.sy, self.ge
        S, D, F, E, R, H, DH = self.S, self.D, self.F, self.E, self.R, self.H, self.DH
        NDC, NFC, NTT, TC, NTC = self.NDC, self.NFC, self.NTT, self.TC, self.NTC
        NMQ, HDx = self.NMQ, self.HDx
        banks, seq = self.banks, self.seq
        sq = f"s{seq}"
        CD = min(512, D)
        C4 = min(512, S)
        NT4 = S // C4
        CV = min(512, HDx)
        NVC = HDx // CV

        r13 = self.sb("r13", [128, 6 * TC], BF16)
        wact = self.sb("wact", [128, NFC * TC], BF16)
        aw = [self.sb(f"aw{i}", [128, TC], BF16) for i in range(6)]
        tsum = [self.sb(f"tsum{i}", [128, TC], BF16) for i in range(4)]
        sring = [self.sb(f"sring{i}", [128, TC], BF16) for i in range(4)]
        h3ring = [self.sb(f"h3ring{i}", [128, TC], BF16) for i in range(4)]
        wb_sb = self.sb("wb_sb", [128, E * TC], BF16)
        r2w_sb = [self.sb(f"r2w{i}", [128, TC], BF16) for i in range(3)]
        w13t = [self.sb(f"w13t{i}", [128, 2 * NDC * 128], BF16) for i in range(4)]
        b13t = [self.sb(f"b13t{i}", [128, 2 * E * 128], BF16) for i in range(4)]
        w2t = [self.sb(f"w2t{i}", [128, CD], BF16) for i in range(3)]
        dstage = [self.sb(f"dstage{i}", [128, CD], BF16) for i in range(2)]
        gate_f = self.sb("gate_f", [128, 32], F32)
        wt_bf = self.sb("wt_bf", [128, E], BF16)
        nxt = [self.sb(f"nxt{i}", [128, NDC * TC], BF16) for i in range(2)]
        xnt = [self.sb(f"xnt{i}", [128, NDC * C4], BF16) for i in range(2)]
        qkvw = [self.sb(f"qkvw{i}", [128, NDC * 128], BF16) for i in range(2)]
        vw = self.sb("vw", [128, NDC * CV], BF16)
        qkv_stage = [self.sb(f"qkvs{i}", [128, C4], BF16) for i in range(2)]
        sq_sb = [self.sb("sqsb0", [128, C4], BF16)]

        self.wait(te, "sc")
        self.wait(te, "dve")

        def load_nx(tc_i):
            if tc_i >= NTC:
                return
            sl = tc_i % 2
            key = f"_nxu{sl}"
            if key in self.c:
                self.wait(sy, "pe", self.c[key])
            for tt_ in range(tc_i * (TC // 128), (tc_i + 1) * (TC // 128)):
                slt, cnt = self.c[f"_xnsT{tt_}"]
                self.wait(sy, f"xnst{slt}", cnt)
            for dc in range(NDC):
                self.inc(sy.dma_start_transpose(
                    nxt[sl][:, dc * TC:(dc + 1) * TC],
                    self.nx_hbm.ap()[tc_i * TC:(tc_i + 1) * TC, dc * 128:(dc + 1) * 128]),
                    f"nxl{sl}", 16)
            self.c[f"_nxc{sl}"] = self.cur(f"nxl{sl}")

        def load_xn(t4_i):
            if t4_i >= NT4:
                return
            sl = t4_i % 2
            key = f"_xnu{sl}"
            if key in self.c:
                self.wait(sy, "pe", self.c[key])
            for tt_ in range(t4_i * (C4 // 128), (t4_i + 1) * (C4 // 128)):
                slt, cnt = self.c[f"_xnsT{tt_}"]
                self.wait(sy, f"xnst{slt}", cnt)
            for dc in range(NDC):
                self.inc(sy.dma_start_transpose(
                    xnt[sl][:, dc * C4:(dc + 1) * C4],
                    self.xn_hbm.ap()[t4_i * C4:(t4_i + 1) * C4, dc * 128:(dc + 1) * 128]),
                    f"xnl{sl}", 16)
            self.c[f"_xnc{sl}"] = self.cur(f"xnl{sl}")

        qk_order = []
        for _t4 in range(NT4):
            qk_order.extend(range(2 * NMQ))
        self._qkpf = dict(nload=0)

        def issue_qkvw():
            k = self._qkpf["nload"]
            if k >= len(qk_order):
                return
            mc_i = qk_order[k]
            sl = k % 2
            key = f"_qwu{sl}"
            if k == 0:
                self.wait(sy, "wcqkv", 16)
            if key in self.c:
                self.wait(sy, "pe", self.c[key])
            self.inc(sy.dma_start(
                qkvw[sl][:, :].rearrange("p (c n) -> p c n", c=NDC),
                self.qkvT_bf.ap().rearrange("(c p) n -> p c n", p=128)[:, :, mc_i * 128:(mc_i + 1) * 128]),
                f"qwl{sl}", 16)
            self.c[f"_qwc{sl}"] = self.cur(f"qwl{sl}")
            self._qkpf["nload"] = k + 1

        qkv_units = []
        for t4 in range(NT4):
            for mc in range(2 * NMQ):
                qkv_units.append(("qk", t4, mc))
            for vc in range(NVC):
                for tt in range(C4 // 128):
                    qkv_units.append(("v", t4, t4 * (C4 // 128) + tt, vc))
        self._qkv_state = dict(idx=0, wu=0, t4loaded=-1)

        def emit_qkv_unit():
            st_ = self._qkv_state
            if st_["idx"] >= len(qkv_units):
                return False
            u = qkv_units[st_["idx"]]
            bk = banks[7]
            if u[0] == "qk":
                _, t4, mc = u
                kq = self._qkv_state.setdefault("nqk", 0)
                self._qkv_state["nqk"] = kq + 1
                if mc == 0:
                    load_xn(t4 + 1)
                wsl = kq % 2
                self.wait(te, f"xnl{t4 % 2}", self.c[f"_xnc{t4 % 2}"])
                self.wait(te, f"qwl{wsl}", self.c[f"_qwc{wsl}"])
                if "_qkv7" in self.c:
                    self.wait(te, "dve", self.c["_qkv7"])
                    self.wait(te, "sc", self.c["_qkv7sc"])
                for dc in range(NDC):
                    mm = te.matmul(bk[:, 0:C4], qkvw[wsl][:, dc * 128:(dc + 1) * 128],
                              xnt[t4 % 2][:, dc * C4:(dc + 1) * C4],
                              start=(dc == 0), stop=(dc == NDC - 1))
                self.c[f"_qwu{wsl}"] = self.inc(mm, "pe")
                pq = self.c[f"_qwu{wsl}"]
                self.c[f"_xnu{t4 % 2}"] = pq
                issue_qkvw()
                isq = mc < NMQ
                est = st_["idx"] % 2
                self.wait(ve, "pe", pq)
                if f"_qst{est}" in self.c:
                    self.wait(ve, f"qkst{est}", self.c[f"_qst{est}"])
                self.dw()
                if isq:
                    self.inc(ve.tensor_scalar(qkv_stage[est][:, :], bk[:, 0:C4], 0.125,
                                              None, op0=ALU.mult), "dve")
                else:
                    self.inc(ve.tensor_copy(qkv_stage[est][:, :], bk[:, 0:C4]), "dve")
                self.c["_qkv7"] = self.cur("dve")
                self.wait(sy, "dve")
                dst = self.q_hbm if isq else self.k_hbm
                r0 = (mc % NMQ) * 128
                self.inc(sy.dma_start(dst.ap()[r0:r0 + 128, t4 * C4:(t4 + 1) * C4],
                                      qkv_stage[est][:, :]), f"qkst{est}", 16)
                self.c[f"_qst{est}"] = self.cur(f"qkst{est}")
                ssl = 0
                self.wait(se, "pe", pq)
                self.wait(se, "dve", self.c["_qkv7"])
                if f"_sqr{ssl}" in self.c:
                    self.wait(se, "pe", self.c[f"_sqr{ssl}"])
                self.sw()
                self.c["_qkv7sc"] = self.inc(
                    se.activation(sq_sb[ssl][:, :], bk[:, 0:C4], AF.Square), "sc")
                sel = self.sels_sb
                self.wait(te, "sc", self.c["_qkv7sc"])
                self.wait(te, "dve", self.c["_qkv7"])
                mm = te.matmul(bk[0:H, 0:C4], sel[:, (mc % NMQ) * 16:(mc % NMQ) * 16 + H],
                          sq_sb[ssl][:, :], start=True, stop=True)
                self.c[f"_sqr{ssl}"] = self.inc(mm, "pe")
                self.wait(ve, "pe", self.c[f"_sqr{ssl}"])
                ii = 0 if isq else 1
                self.dw()
                self.inc(ve.tensor_reduce(self.small[0:H, 8 + ii:9 + ii], bk[0:H, 0:C4],
                                          axis=AX.X, op=ALU.max), "dve")
                self.dw()
                if t4 == 0 and mc % NMQ == 0:
                    self.inc(ve.tensor_copy(self.qm2[0:H, ii:ii + 1],
                                            self.small[0:H, 8 + ii:9 + ii]), "dve")
                else:
                    self.inc(ve.tensor_tensor(self.qm2[0:H, ii:ii + 1],
                                              self.qm2[0:H, ii:ii + 1],
                                              self.small[0:H, 8 + ii:9 + ii],
                                              op=ALU.max), "dve")
                self.c["_qkv7"] = self.cur("dve")
            else:
                _, t4, tt, vc = u
                load_vw(vc)
                self.wait(te, f"xnl{t4 % 2}", self.c[f"_xnc{t4 % 2}"])
                self.wait(te, "vwl", self.cur("vwl"))
                if "_qkv7" in self.c:
                    self.wait(te, "dve", self.c["_qkv7"])
                    self.wait(te, "sc", self.c["_qkv7sc"])
                lt = tt - t4 * (C4 // 128)
                for dc in range(NDC):
                    mm = te.matmul(bk[:, 0:CV],
                              xnt[t4 % 2][:, dc * C4 + lt * 128: dc * C4 + lt * 128 + 128],
                              vw[:, dc * CV:(dc + 1) * CV],
                              start=(dc == 0), stop=(dc == NDC - 1))
                pq = self.inc(mm, "pe")
                self.c[f"_xnu{t4 % 2}"] = pq
                self.c["_vwu"] = pq
                est = st_["idx"] % 2
                self.wait(ve, "pe", pq)
                if f"_qst{est}" in self.c:
                    self.wait(ve, f"qkst{est}", self.c[f"_qst{est}"])
                self.dw()
                self.inc(ve.tensor_copy(qkv_stage[est][:, 0:CV], bk[:, 0:CV]), "dve")
                self.c["_qkv7"] = self.cur("dve")
                self.wait(sy, "dve")
                self.inc(sy.dma_start(
                    self.v_hbm.ap()[tt * 128:(tt + 1) * 128, vc * CV:(vc + 1) * CV],
                    qkv_stage[est][:, 0:CV]), f"qkst{est}", 16)
                self.c[f"_qst{est}"] = self.cur(f"qkst{est}")
            st_["idx"] += 1
            return True

        self._vwst = dict(loaded=None)

        def load_vw(vc_i):
            if self._vwst["loaded"] == vc_i:
                return
            if self._vwst["loaded"] is None:
                self.wait(sy, "wcqkv", 16)
            if "_vwu" in self.c:
                self.wait(sy, "pe", self.c["_vwu"])
            self.inc(sy.dma_start(
                vw[:, :].rearrange("p (c n) -> p c n", c=NDC),
                self.qkvT_bf.ap().rearrange("(c p) n -> p c n", p=128)[:, :, 2 * HDx + vc_i * CV: 2 * HDx + (vc_i + 1) * CV]),
                "vwl", 16)
            self._vwst["loaded"] = vc_i

        load_nx(0)
        load_xn(0)
        issue_qkvw()
        issue_qkvw()

        lane = {0: (0, 1), 1: (2, 3)}
        r2w_banks = (4, 5, 6)
        for tc in range(NTC):
            t0 = tc * TC
            sl = tc % 2
            load_nx(tc + 1)
            if self.taps and self.cur("tapst") > 0:
                self.wait(ve, "tapst", self.cur("tapst"))
            self.wait(te, f"nxl{sl}", self.c[f"_nxc{sl}"])
            if "_devac" in self.c:
                self.wait(te, "dve", self.c["_devac"])
            if "_r2wevac2" in self.c:
                self.wait(te, "dve", self.c["_r2wevac2"])
            if "_laneA0" in self.c:
                self.wait(te, "sc", self.c["_laneA0"])
            for lt in range(TC // 128):
                bk = banks[6]
                if "_gate_done" in self.c:
                    self.wait(te, "dve", self.c["_gate_done"])
                if "_gate_sc" in self.c:
                    self.wait(te, "sc", self.c["_gate_sc"])
                for dc in range(NDC):
                    mm = te.matmul(bk[:, 0:E],
                              nxt[sl][:, dc * TC + lt * 128: dc * TC + (lt + 1) * 128],
                              self.gate_sb[:, dc * E:(dc + 1) * E],
                              start=(dc == 0), stop=(dc == NDC - 1))
                self.inc(mm, "pe")
                self.wait(ve, "pe")
                self.wait(ve, "sc")
                self.dw()
                self.inc(ve.tensor_reduce(gate_f[:, 0:1], bk[:, 0:E], axis=AX.X,
                                          op=ALU.max), "dve")
                self.dw()
                self.inc(ve.tensor_scalar(gate_f[:, 1:2], gate_f[:, 0:1], -1.0, None,
                                          op0=ALU.mult), "dve")
                self.wait(se, "dve")
                self.sw()
                self.c["_gate_sc"] = self.inc(
                    se.activation(gate_f[:, 2:2 + E], bk[:, 0:E], AF.Exp,
                                  bias=gate_f[:, 1:2], scale=1.0), "sc")
                self.wait(ve, "sc")
                eg = gate_f[:, 2:2 + E]
                self.dw()
                self.inc(ve.tensor_reduce(gate_f[:, 10:11], eg, axis=AX.X, op=ALU.max), "dve")
                self.dw()
                self.inc(ve.tensor_scalar(gate_f[:, 12:12 + E], eg, gate_f[:, 10:11],
                                          None, op0=ALU.is_equal), "dve")
                self.dw()
                self.inc(ve.tensor_scalar(gate_f[:, 20:20 + E], gate_f[:, 12:12 + E],
                                          -1.0, 1.0, op0=ALU.mult, op1=ALU.add), "dve")
                self.dw()
                self.inc(ve.tensor_tensor(gate_f[:, 20:20 + E], eg, gate_f[:, 20:20 + E],
                                          op=ALU.mult), "dve")
                self.dw()
                self.inc(ve.tensor_reduce(gate_f[:, 11:12], gate_f[:, 20:20 + E],
                                          axis=AX.X, op=ALU.max), "dve")
                self.dw()
                self.inc(ve.tensor_scalar(gate_f[:, 20:20 + E], gate_f[:, 20:20 + E],
                                          gate_f[:, 11:12], None, op0=ALU.is_equal), "dve")
                self.dw()
                self.inc(ve.tensor_tensor(gate_f[:, 12:12 + E], gate_f[:, 12:12 + E],
                                          gate_f[:, 20:20 + E], op=ALU.add), "dve")
                self.dw()
                self.inc(ve.scalar_tensor_tensor(gate_f[:, 30:31], gate_f[:, 10:11], 0.0,
                                                 gate_f[:, 11:12], op0=ALU.bypass,
                                                 op1=ALU.add), "dve")
                self.dw()
                self.inc(ve.reciprocal(gate_f[:, 31:32], gate_f[:, 30:31]), "dve")
                self.dw()
                self.inc(ve.tensor_tensor(gate_f[:, 12:12 + E], eg, gate_f[:, 12:12 + E],
                                          op=ALU.mult), "dve")
                self.dw()
                self.inc(ve.tensor_scalar(wt_bf[:, :], gate_f[:, 12:12 + E],
                                          gate_f[:, 31:32], None, op0=ALU.mult), "dve")
                self.c["_gate_done"] = self.cur("dve")
                bkt = banks[6][:, :].bitcast(BF16)
                self.wait(te, "dve")
                mm = te.transpose(bkt[0:E, 0:128], wt_bf[:, :], self.id_sb[:, :])
                self.inc(mm, "pe")
                self.wait(ve, "pe")
                self.dw()
                self.c["_wTe"] = self.inc(ve.tensor_copy(
                    self.w_T[:, lt * 128: (lt + 1) * 128], bkt[0:E, 0:128]), "dve")
                self.c["_gate_done"] = self.cur("dve")
                emit_qkv_unit()
            for j in range(6):
                bk = banks[j % 2]
                key = f"_r13b{j % 2}"
                if key in self.c:
                    self.wait(te, "dve", self.c[key])
                if tc == 0 and j < 2 and f"_lane{'AB'[j % 2]}" not in self.c:
                    pass
                for dc in range(NDC):
                    mm = te.matmul(bk[:, 0:TC],
                              self.a13p_sb[:, dc * 768 + j * 128: dc * 768 + (j + 1) * 128],
                              nxt[sl][:, dc * TC: dc * TC + TC],
                              start=(dc == 0), stop=(dc == NDC - 1))
                self.inc(mm, "pe")
                self.wait(ve, "pe")
                self.dw()
                self.c[key] = self.inc(
                    ve.tensor_scalar(r13[:, j * TC: (j + 1) * TC],
                                     bk[:, 0:TC], 2.0, None, op0=ALU.mult), "dve")
                emit_qkv_unit()
            self.wait(te, "dve")
            self.wait(te, "sc")
            for e in range(E):
                bk = banks[6]
                key = "_wb6"
                if key in self.c:
                    self.wait(te, "dve", self.c[key])
                mm = te.matmul(bk[:, 0:TC], self.sel8_sb[0:E, e * 128:(e + 1) * 128],
                          self.w_T[0:E, 0:TC], start=True, stop=True)
                self.inc(mm, "pe")
                self.wait(ve, "pe")
                self.dw()
                self.c[key] = self.inc(
                    ve.tensor_copy(wb_sb[:, e * TC:(e + 1) * TC], bk[:, 0:TC]), "dve")

            for b in r2w_banks:
                self.dw()
                self.wait(ve, "pe", self.c.get("_r2wfree", 0))
                self.inc(ve.memset(banks[b][:, 0:TC], 0.0), "dve")

            def issue_w13(fcp_i, ch_i):
                fc_i = 2 * fcp_i + ch_i
                if fc_i >= NFC:
                    return
                sl2 = ch_i * 2 + fcp_i % 2
                k1 = f"_w13u{sl2}"
                if fc_i == 0:
                    self.wait(sy, "wcw1", 16)
                    self.wait(sy, "wcw3", 16)
                    self.wait(sy, "wcb13", 16)
                if k1 in self.c:
                    self.wait(sy, "pe", self.c[k1])
                self.inc(sy.dma_start(
                    w13t[sl2][:, 0:NDC * 128].rearrange("p (c n) -> p c n", c=NDC),
                    self.w1_bf.ap().rearrange("(c p) n -> p c n", p=128)[:, :, fc_i * 128:(fc_i + 1) * 128]),
                    f"w13l{sl2}", 16)
                self.inc(sy.dma_start(
                    w13t[sl2][:, NDC * 128:].rearrange("p (c n) -> p c n", c=NDC),
                    self.w3_bf.ap().rearrange("(c p) n -> p c n", p=128)[:, :, fc_i * 128:(fc_i + 1) * 128]),
                    f"w13l{sl2}", 16)
                b13src = self.b13_bf.ap().rearrange("(w p) n -> p w n", p=32)[:, :, fc_i * 128:(fc_i + 1) * 128]
                for ro in (0, 32, 64):
                    self.inc(sy.dma_start(
                        b13t[sl2][ro:ro + 32, :].rearrange("p (w n) -> p w n", w=2 * E),
                        b13src), f"w13l{sl2}", 16)

            issue_w13(0, 0)
            issue_w13(0, 1)
            w13cnt = {i: self.cur(f"w13l{i}") for i in range(4)}
            for fcp in range(NFC // 2):
                for ch in (0, 1):
                    fc = 2 * fcp + ch
                    A, Bk = lane[ch]
                    wsl = ch * 2 + fcp % 2
                    issue_w13(fcp + 1, ch)
                    self.wait(te, f"w13l{wsl}", w13cnt[wsl])
                    nsl = ch * 2 + (fcp + 1) % 2
                    w13cnt[nsl] = self.cur(f"w13l{nsl}")
                    if f"_laneA{ch}" in self.c:
                        self.wait(te, "sc", self.c[f"_laneA{ch}"])
                    if f"_laneB{ch}" in self.c:
                        self.wait(te, "dve", self.c[f"_laneB{ch}"])
                    for dc in range(NDC):
                        mm = te.matmul(banks[A][:, 0:TC], w13t[wsl][:, dc * 128:(dc + 1) * 128],
                                  nxt[sl][:, dc * TC: dc * TC + TC],
                                  start=(dc == 0), stop=(dc == NDC - 1))
                    for dc in range(NDC):
                        mm = te.matmul(banks[Bk][:, 0:TC],
                                  w13t[wsl][:, NDC * 128 + dc * 128: NDC * 128 + (dc + 1) * 128],
                                  nxt[sl][:, dc * TC: dc * TC + TC],
                                  start=(dc == 0), stop=(dc == NDC - 1))
                    self.c[f"_w13u{wsl}"] = self.inc(mm, "pe")
                r2w_pending = []

                def emit_r2w():
                    for (fc_, e_, slot_) in r2w_pending:
                        kb_, jj_ = divmod(e_, 3)
                        self.wait(te, "dve", self.c[f"_awd{slot_}"])
                        mm_ = te.matmul(
                            banks[r2w_banks[kb_]][32 * jj_:32 * jj_ + 16, 0:TC],
                            self.a2T_sb[:, fc_ * E * R + e_ * R: fc_ * E * R + (e_ + 1) * R],
                            aw[slot_][:, :], start=False, stop=False,
                            skip_group_check=True, tile_position=(0, 32 * jj_))
                        self.c[f"_awr{slot_}"] = self.inc(mm_, "pe")
                    r2w_pending.clear()

                for e in range(E):
                    pA = {}
                    pB = {}
                    for ch in (0, 1):
                        A, Bk = lane[ch]
                        wsl = ch * 2 + fcp % 2
                        b1i, b3i = e, E + e
                        jt1, ro1 = b1i // 3, 32 * (b1i % 3)
                        jt3, ro3 = b3i // 3, 32 * (b3i % 3)
                        if e > 0:
                            self.wait(te, "sc", self.c[f"_silu{ch}"])
                        mm = te.matmul(banks[A][:, 0:TC],
                                  b13t[wsl][ro1:ro1 + 32,
                                            b1i * 128: b1i * 128 + 128],
                                  r13[ro1:ro1 + 32, jt1 * TC: (jt1 + 1) * TC],
                                  start=False, stop=False, skip_group_check=True)
                        pA[ch] = self.inc(mm, "pe")
                        if e > 0:
                            self.wait(te, "dve", self.c[f"_h3w{ch}"])
                        mm = te.matmul(banks[Bk][:, 0:TC],
                                  b13t[wsl][ro3:ro3 + 32,
                                            (E + e) * 128: (E + e) * 128 + 128],
                                  r13[ro3:ro3 + 32, jt3 * TC: (jt3 + 1) * TC],
                                  start=False, stop=False, skip_group_check=True)
                        pB[ch] = self.inc(mm, "pe")
                        if e == E - 1:
                            self.c[f"_w13u{wsl}"] = self.cur("pe")
                    emit_r2w()
                    for ch in (0, 1):
                        srA = sring[ch * 2 + e % 2]
                        self.wait(se, "pe", pA[ch])
                        if f"_awtt{ch}{e % 2}" in self.c:
                            self.wait(se, "dve", self.c[f"_awtt{ch}{e % 2}"])
                        self.sw()
                        self.c[f"_silu{ch}"] = self.inc(
                            se.activation(srA[:, :], banks[lane[ch][0]][:, 0:TC],
                                          AF.Silu), "sc")
                    for ch in (0, 1):
                        h3A = h3ring[ch * 2 + e % 2]
                        self.wait(ve, "pe", pB[ch])
                        self.dw()
                        self.c[f"_h3w{ch}"] = self.inc(
                            ve.tensor_tensor(h3A[:, :], banks[lane[ch][1]][:, 0:TC],
                                             wb_sb[:, e * TC:(e + 1) * TC],
                                             op=ALU.mult), "dve")
                    for ch in (0, 1):
                        fc = 2 * fcp + ch
                        srA = sring[ch * 2 + e % 2]
                        h3A = h3ring[ch * 2 + e % 2]
                        awslot = ch * 3 + e % 3
                        if f"_awr{awslot}" in self.c:
                            self.wait(ve, "pe", self.c[f"_awr{awslot}"])
                        if f"_awp{awslot}" in self.c:
                            self.wait(ve, "po", self.c[f"_awp{awslot}"])
                        self.wait(ve, "sc", self.c[f"_silu{ch}"])
                        self.dw()
                        self.c[f"_aw{ch}"] = self.inc(
                            ve.tensor_tensor(aw[awslot][:, :], srA[:, :], h3A[:, :],
                                             op=ALU.mult), "dve")
                        self.c[f"_awd{awslot}"] = self.c[f"_aw{ch}"]
                        self.c[f"_awtt{ch}{e % 2}"] = self.c[f"_aw{ch}"]
                        r2w_pending.append((fc, e, awslot))
                        tb = ch * 2
                        aprev = aw[ch * 3 + (e - 1) % 3]
                        if e % 2 == 1:
                            self.wait(ge, "dve", self.c[f"_aw{ch}"])
                        if e == 1 or e == 5:
                            self.inc(ge.tensor_tensor(tsum[tb + (e // 4)][:, :],
                                                      aprev[:, :], aw[awslot][:, :],
                                                      op=ALU.add), "po")
                        if e == 3:
                            self.inc(ge.tensor_tensor(tsum[tb + 1][:, :], aprev[:, :],
                                                      aw[awslot][:, :], op=ALU.add), "po")
                            self.gw()
                            self.inc(ge.tensor_tensor(tsum[tb][:, :], tsum[tb][:, :],
                                                      tsum[tb + 1][:, :], op=ALU.add), "po")
                        if e == 7:
                            wv = wact[:, fc * TC:(fc + 1) * TC]
                            self.inc(ge.tensor_tensor(wv, aprev[:, :], aw[awslot][:, :],
                                                      op=ALU.add), "po")
                            self.gw()
                            self.inc(ge.tensor_tensor(wv, wv, tsum[tb + 1][:, :],
                                                      op=ALU.add), "po")
                            self.gw()
                            self.c[f"_wact{fc}"] = self.inc(
                                ge.tensor_tensor(wv, wv, tsum[tb][:, :], op=ALU.add),
                                "po")
                            self.c[f"_laneB{ch}"] = self.c[f"_h3w{ch}"]
                            self.c[f"_laneA{ch}"] = self.c[f"_silu{ch}"]
                        if e % 2 == 1:
                            self.c[f"_awp{awslot}"] = self.cur("po")
                            self.c[f"_awp{ch * 3 + (e - 1) % 3}"] = self.cur("po")
                emit_r2w()
                emit_qkv_unit()

            self.wait(ve, "pe")
            for kb in range(3):
                self.dw()
                self.c[f"_r2wevac{kb}"] = self.inc(
                    ve.tensor_scalar(r2w_sb[kb][:, :], banks[r2w_banks[kb]][:, 0:TC],
                                     2.0, None, op0=ALU.mult), "dve")
            self.c["_wb6"] = self.c["_r2wevac2"]

            if self.taps:
                self.wait(sy, "dve")
                for kb in range(3):
                    self.inc(sy.dma_start(self.tap_r2w.ap()[kb], r2w_sb[kb][:, :]), "tapst", 16)

            NTS = TC // 128
            self.wait(te, "sc")

            def issue_w2(gi):
                if gi >= (D // CD) * NFC:
                    return
                dcb2, fc2 = divmod(gi, NFC)
                sl2 = gi % 3
                key = f"_w2u{sl2}"
                if gi == 0:
                    self.wait(sy, "wcw2", 16)
                if key in self.c:
                    self.wait(sy, "pe", self.c[key])
                self.inc(sy.dma_start(
                    w2t[sl2][:, :],
                    self.w2_bf.ap()[fc2 * 128:(fc2 + 1) * 128, dcb2 * CD:(dcb2 + 1) * CD]),
                    f"w2l{sl2}", 16)
                self.c[f"_w2c{sl2}"] = self.cur(f"w2l{sl2}")

            issue_w2(0)
            issue_w2(1)
            for dcb in range(D // CD):
                for fc in range(NFC):
                    gi = dcb * NFC + fc
                    wslot = gi % 3
                    issue_w2(gi + 2)
                    self.wait(te, f"w2l{wslot}", self.c[f"_w2c{wslot}"])
                    self.wait(te, "po", self.c[f"_wact{fc}"])
                    if fc == 0 and dcb > 0:
                        self.wait(te, "dve", self.c["_devac"])
                    for ts_ in range(NTS):
                        mm = te.matmul(banks[ts_][:, 0:CD],
                                  wact[:, fc * TC + ts_ * 128: fc * TC + (ts_ + 1) * 128],
                                  w2t[wslot][:, :], start=(fc == 0), stop=False)
                    self.c[f"_w2u{wslot}"] = self.inc(mm, "pe")
                    if fc % 8 == 3:
                        emit_qkv_unit()
                self.wait(te, "dve", self.c["_r2wevac2"])
                for ts_ in range(NTS):
                    for kb in range(3):
                        mm = te.matmul(banks[ts_][:, 0:CD],
                                  r2w_sb[kb][:, ts_ * 128:(ts_ + 1) * 128],
                                  self.b2p_sb[:, kb * D + dcb * CD: kb * D + (dcb + 1) * CD],
                                  start=False, stop=(kb == 2))
                pev = self.inc(mm, "pe")
                for ts_ in range(NTS):
                    sl2 = ts_ % 2
                    self.wait(ve, "pe", pev)
                    stk = f"_dst{sl2}"
                    if stk in self.c:
                        self.wait(ve, f"dstg{sl2}", self.c[stk])
                    self.dw()
                    self.inc(ve.tensor_copy(dstage[sl2][:, :], banks[ts_][:, 0:CD]), "dve")
                    self.wait(sy, "dve")
                    self.inc(sy.dma_start(
                        self.ffn_hbm.ap()[t0 + ts_ * 128: t0 + (ts_ + 1) * 128,
                                          dcb * CD:(dcb + 1) * CD],
                        dstage[sl2][:, :]), f"dstg{sl2}", 16)
                    self.c[stk] = self.cur(f"dstg{sl2}")
                self.c["_devac"] = self.cur("dve")
            if self.taps:
                self.wait(sy, "po", self.cur("po"))
                self.inc(sy.dma_start(self.tap_wact.ap()[tc], wact[:, :]), "tapst", 16)
            self.c["_laneA0"] = self.cur("sc")
            self.c["_laneB0"] = self.cur("dve")
            self.c["_laneA1"] = self.cur("sc")
            self.c["_laneB1"] = self.cur("dve")
            self.c["_w13u0"] = self.cur("pe")
            self.c["_w13u2"] = self.cur("pe")
            self.c["_r2wfree"] = self.cur("pe")
        while emit_qkv_unit():
            pass

        if "a" in self.phases and "x" in self.aparts:
            HPG_, HPT_, DH_, NTT_, QC_ = (self.cfg["HPG"], self.HPT, self.DH,
                                          self.NTT, self.QC)
            for i in range(4):
                if f"qkst{i}" in self.c:
                    self.wait(ge, f"qkst{i}")
            if "_khgfree0" in self.c:
                self.wait(ge, "pe", self.c["_khgfree0"])
            self.inc(ge.dma_start(
                self.khg0[:, :].rearrange("p (c n) -> p c n", c=HPG_ // HPT_),
                self.k_hbm.ap().rearrange("(c p) n -> p c n", p=128)[:, 0:HPG_ // HPT_, :]),
                "khgp", 16)
            self.c["_khgc0"] = self.cur("khgp")
            self._khgsem[0] = "khgp"
            if "_vhgfree" in self.c:
                self.wait(ge, "pe", self.c["_vhgfree"])
            for hh2 in range(HPG_):
                vdst = AP(self.vhg[:, :].tensor, self.vhg[:, :].offset + hh2 * (DH_ + 1),
                          [list(self.vhg[:, :].ap[0]), [HPG_ * (DH_ + 1), NTT_], [1, DH_]])
                vsrc = self.v_hbm.ap().rearrange("(a p) n -> p a n", p=128)[
                    :, :, hh2 * DH_:(hh2 + 1) * DH_]
                self.inc(ge.dma_start(vdst, vsrc), "vhgp", 16)
            self.c["_vhgc"] = self.cur("vhgp")
            self._vhgsem = "vhgp"
            self._pf_attn = True

        if self.taps:
            self.wait(sy, "dve")
            self.inc(sy.dma_start(self.tap_wT.ap(), self.w_T[:, :]), "tapst", 16)
            self.inc(sy.dma_start(self.tap_r13.ap(), r13[:, :]), "tapst", 16)

    def phase_attn(self):
        nc, te, ve, se, sy, ge = self.nc, self.te, self.ve, self.se, self.sy, self.ge
        S, D, H, DH, E = self.S, self.D, self.H, self.DH, self.E
        NDC, NTT, NMQ, HPT = self.NDC, self.NTT, self.NMQ, self.HPT
        QC, NQC, NKC, HPG, NHG = self.QC, self.NQC, self.NKC, self.HPG, self.NHG
        WBR, LTP = self.WBR, self.LTP
        banks, seq = self.banks, self.seq
        HDx = self.HDx
        sq = f"s{seq}"
        C4 = min(512, S)
        NT4 = S // C4
        CD = min(512, D)

        adm = self.sb("adm", [128, NMQ * S], BF16)
        lrec_f = [self.sb(f"lrec_f{i}", [1, QC], F32) for i in range(HPG)]
        lrec = [self.sb(f"lrec{i}", [1, QC], BF16) for i in range(HPG)]
        lbc = [self.sb(f"lbc{i}", [64, QC], BF16) for i in range(HPG)]
        smb = self.sb("smb", [128, 16], BF16)
        nmrow = self.sb("nmrow", [1, H], BF16)
        owT_sb = self.sb("owT_sb", [128, NMQ * D], BF16)
        self.wait(sy, "wcow", 16)
        self.inc(sy.dma_start(owT_sb[:, :].rearrange("p (c n) -> p c n", c=NMQ),
                              self.owT_bf.ap().rearrange("(c p) n -> p c n", p=128)),
                 "owl", 16)
        xres = [self.sb(f"xres{i}", [128, CD]) for i in range(2)]
        fres = [self.sb(f"fres{i}", [128, CD], BF16) for i in range(2)]
        ores = [self.sb(f"ores{i}", [128, CD]) for i in range(2)]
        ost = [self.sb(f"ost{i}", [128, CD]) for i in range(2)]
        attn_outer_st = self.st

        if "b" not in self.aparts:
            self.c["_epi_done"] = self.cur("dve")
            self.c["_skipflash"] = 1
        sm = self.small
        if "b" in self.aparts:
            self.wait(ve, "sc")
            self.dw()
            self.inc(ve.tensor_tensor(sm[0:H, 0:1], self.qm2[0:H, 0:1],
                                      self.qm2[0:H, 1:2], op=ALU.mult), "dve")
            self.wait(se, "dve")
            self.sw()
            self.inc(se.activation(sm[0:H, 1:2], sm[0:H, 0:1], AF.Ln,
                                   bias=self.eps[0:H, 2:3], scale=1.0), "sc")
            self.sw()
            self.inc(se.activation(sm[0:H, 1:2], sm[0:H, 1:2], AF.Exp,
                                   bias=self.eps[0:H, 3:4], scale=0.5), "sc")
            self.wait(ve, "sc")
            self.dw()
            self.inc(ve.tensor_scalar(smb[0:H, 0:1], sm[0:H, 1:2], -1.0, None,
                                      op0=ALU.mult), "dve")
            bkt = banks[0][:, :].bitcast(BF16)
            self.wait(te, "dve")
            mm = te.transpose(bkt[0:16, 0:128], smb[:, :], self.id_sb[:, :])
            self.inc(mm, "pe")
            self.wait(ve, "pe")
            self.dw()
            self.inc(ve.tensor_copy(nmrow[:, :], bkt[0:1, 0:H]), "dve")
            self.wait(te, "dve")
            mm = te.matmul(banks[1][:, 0:H], self.ones1[:, :], nmrow[:, :], start=True, stop=True)
            self.inc(mm, "pe")
            self.wait(ve, "pe")
            self.dw()
            self.inc(ve.tensor_copy(self.biasmat[:, :], banks[1][:, 0:H]), "dve")
            self.c["_epi_done"] = self.cur("dve")

        fl_scope = contextlib.ExitStack()
        self.st = fl_scope
        vhg = self.vhg
        khg = [self.khg0, self.sb("khg1", [128, (HPG // HPT) * S], BF16)]
        qsl = [self.sb(f"qsl{i}", [128, (HPG // HPT) * S], BF16) for i in range(2)]

        def emit_qsl_load(hg_i):
            if hg_i >= NHG:
                return
            qs_ = hg_i % 2
            h0_ = hg_i * HPG
            kfk = f"_qslfree{qs_}"
            if kfk in self.c:
                self.wait(sy, "pe", self.c[kfk])
            self.inc(sy.dma_start(
                qsl[qs_][:, :].rearrange("p (c n) -> p c n", c=HPG // HPT),
                self.q_hbm.ap().rearrange("(c p) n -> p c n", p=128)[
                    :, h0_ // HPT: (h0_ + HPG) // HPT, :]),
                f"qsll{qs_}", 16)
            self.c[f"_qslc{qs_}"] = self.cur(f"qsll{qs_}")

        def emit_hg_loads(hg_i):
            if hg_i >= NHG:
                return
            sl_ = hg_i % 2
            h0_ = hg_i * HPG
            kfk = f"_khgfree{sl_}"
            if kfk in self.c:
                self.wait(sy, "pe", self.c[kfk])
            self.inc(sy.dma_start(
                khg[sl_][:, :].rearrange("p (c n) -> p c n", c=HPG // HPT),
                self.k_hbm.ap().rearrange("(c p) n -> p c n", p=128)[
                    :, h0_ // HPT: (h0_ + HPG) // HPT, :]),
                f"khgl{sl_}", 16)
            self.c[f"_khgc{sl_}"] = self.cur(f"khgl{sl_}")
            self._khgsem[sl_] = f"khgl{sl_}"

        def emit_vhg_loads(hg_i):
            if hg_i >= NHG:
                return
            h0_ = hg_i * HPG
            self.wait(sy, "pe", self.c["_vhgfree"])
            for hh2 in range(HPG):
                vdst = AP(vhg[:, :].tensor, vhg[:, :].offset + hh2 * (DH + 1),
                          [list(vhg[:, :].ap[0]), [HPG * (DH + 1), NTT], [1, DH]])
                vsrc = self.v_hbm.ap().rearrange("(a p) n -> p a n", p=128)[
                    :, :, (h0_ + hh2) * DH:(h0_ + hh2 + 1) * DH]
                self.inc(sy.dma_start(vdst, vsrc), "vhgl", 16)
            self.c["_vhgc"] = self.cur("vhgl")
            self._vhgsem = "vhgl"

        brw = [self.sb(f"brw{i}", [128, WBR], BF16) for i in range(HPG)]
        ptil = [self.sb(f"ptil{i}", [128, QC], BF16) for i in range(4)]
        self.st = attn_outer_st
        self.wait(ge, "ebr", 16)
        self.wait(ge, "pe")
        self.wait(ge, "dve")
        for nm in ("qkst0", "qkst1", "xnst0", "xnst1"):
            if nm in self.c:
                self.wait(ge, nm, self.c[nm])
                self.wait(sy, nm, self.c[nm])
        self.wait(sy, "pe")
        self.wait(sy, "dve")
        self.wait(sy, "sc")
        for i in range(4):
            if f"qkst{i}" in self.c:
                self.wait(sy, f"qkst{i}")
        if "x" in self.aparts and NHG > 0:
            emit_qsl_load(0)
        for hg in range(NHG if "x" in self.aparts else 0):
            h0 = hg * HPG
            sl = hg % 2
            qs = hg % 2
            if hg == 0 and not self._pf_attn:
                emit_hg_loads(0)
            emit_qsl_load(hg + 1)
            if hg > 0 and "_brwfree" in self.c:
                self.wait(ge, "dve", self.c["_brwfree"])
            for hh in range(HPG):
                h = h0 + hh
                win = AP(self.ebr_hbm.ap().tensor, 0, [[1, H * LTP], [1, 1]])
                self.inc(ge.indirect_dma_start(
                    brw[hh][:, :], None, win,
                    IndirectOffsetOnAxis(self.gidx_sb[:, :], 0),
                    element_offset=h * LTP), f"brwg{hh}", 16)
                self.c[f"_brwc{hh}"] = self.cur(f"brwg{hh}")
            for qc in range(NQC):
                q0 = qc * QC
                if qc == 0:
                    emit_hg_loads(hg + 1)
                self.wait(te, f"qsll{qs}", self.c[f"_qslc{qs}"])
                self.wait(te, self._khgsem[sl], self.c[f"_khgc{sl}"])
                self.wait(te, self._vhgsem, self.c["_vhgc"])
                self.wait(te, "dve", self.c["_epi_done"])
                self.wait(te, "dve", self.c["_vones"])
                tiles = [(kc, hh) for hh in range(HPG) for kc in range(NKC)]
                LA = 2
                for i in range(len(tiles) + LA):
                    if i < len(tiles):
                        kc, hh = tiles[i]
                        h = h0 + hh
                        sbk = i % 4
                        bk = banks[sbk]
                        key = f"_exp{sbk}"
                        if f"_pext{sbk}" in self.c:
                            self.wait(te, "dve", self.c[f"_pext{sbk}"])
                        if kc == 0 and "_devac" in self.c:
                            self.wait(te, "dve", self.c["_devac"])
                        if self.taps and "_tapw" in self.c:
                            self.wait(se, "tapst", self.c["_tapw"])
                            self.wait(ve, "tapst", self.c["_tapw"])
                        jt = hh // HPT
                        roff = 64 * (hh % HPT)
                        mm = te.matmul(bk[:, 0:QC],
                                  khg[sl][roff:roff + DH, jt * S + kc * 128: jt * S + (kc + 1) * 128],
                                  qsl[qs][roff:roff + DH, jt * S + q0: jt * S + q0 + QC],
                                  start=True, stop=True)
                        pmm = self.inc(mm, "pe")
                        self.wait(se, "pe", pmm)
                        if f"_pext{sbk}" in self.c:
                            self.wait(se, "dve", self.c[f"_pext{sbk}"])
                        self.sw()
                        self.c[key] = self.inc(
                            se.activation(bk[:, 0:QC], bk[:, 0:QC], AF.Exp,
                                          bias=self.biasmat[:, h:h + 1], scale=1.0),
                            "sc")
                        self.wait(ve, "sc", self.c[key])
                        c0 = (S - 128) - kc * 128 + q0
                        if kc == 0:
                            self.wait(ve, f"brwg{hh}", self.c[f"_brwc{hh}"])
                        if f"_ptu{sbk}" in self.c:
                            self.wait(ve, "pe", self.c[f"_ptu{sbk}"])
                        self.dw()
                        self.c[f"_pext{sbk}"] = self.inc(
                            ve.tensor_tensor(ptil[sbk][:, :], bk[:, 0:QC],
                                             brw[hh][:, c0:c0 + QC], op=ALU.mult),
                            "dve")
                    if i >= LA:
                        kc, hh = tiles[i - LA]
                        sbk = (i - LA) % 4
                        avb = banks[4 + hh]
                        self.wait(te, "dve", self.c[f"_pext{sbk}"])
                        mm = te.matmul(avb[0:DH + 1, 0:QC],
                                  vhg[:, (kc * HPG + hh) * (DH + 1): (kc * HPG + hh + 1) * (DH + 1)],
                                  ptil[sbk][:, :], start=(kc == 0), stop=(kc == NKC - 1))
                        self.c[f"_ptu{sbk}"] = self.inc(mm, "pe")
                if qc == NQC - 1:
                    self.c[f"_qslfree{qs}"] = self.cur("pe")
                if self.taps and seq == 1 and hg == 0 and qc == 0:
                    self.wait(ve, "pe")
                    self.dw()
                    self.inc(ve.tensor_copy(qkv_stage[0][0:65, 0:QC], banks[4][0:65, 0:QC]), "dve")
                    self.wait(sy, "dve")
                    self.inc(sy.dma_start(self.tap_av.ap(), qkv_stage[0][0:65, 0:QC]), "tapst", 16)
                for hh in range(HPG):
                    avb = banks[4 + hh]
                    self.wait(se, "pe")
                    if "_lbccp" in self.c:
                        self.wait(se, "dve", self.c["_lbccp"])
                    self.sw()
                    self.inc(se.activation(lrec_f[hh][:, :], avb[64:65, 0:QC], AF.Ln,
                                           bias=0.0, scale=1.0), "sc")
                    self.sw()
                    self.inc(se.activation(lrec[hh][:, :], lrec_f[hh][:, :], AF.Exp,
                                           bias=0.0, scale=-1.0), "sc")
                    self.c[f"_lr{hh}"] = self.cur("sc")
                for hh in range(HPG):
                    bb = banks[hh % 4]
                    self.wait(te, "sc", self.c[f"_lr{hh}"])
                    self.wait(te, "dve", self.c[f"_pext{hh % 4}"])
                    if hh == 0 and "_lbccp" in self.c:
                        self.wait(te, "dve", self.c["_lbccp"])
                    mm = te.matmul(bb[0:64, 0:QC], self.ones1[:, 0:64], lrec[hh][:, :],
                              start=True, stop=True)
                    self.c[f"_lbcmm{hh}"] = self.inc(mm, "pe")
                for hh in range(HPG):
                    h = h0 + hh
                    avb = banks[4 + hh]
                    bb = banks[hh % 4]
                    self.wait(ve, "pe", self.c[f"_lbcmm{hh}"])
                    self.dw()
                    self.inc(ve.tensor_copy(lbc[hh][:, :], bb[0:64, 0:QC]), "dve")
                    dst_r = 64 * (h % HPT)
                    dst_t = h // HPT
                    self.dw()
                    self.inc(ve.tensor_tensor(
                        adm[dst_r:dst_r + 64, dst_t * S + q0: dst_t * S + q0 + QC],
                        avb[0:64, 0:QC], lbc[hh][:, :], op=ALU.mult), "dve")
                self.c["_lbccp"] = self.cur("dve")
                self.c["_epi_done"] = self.cur("dve")
                if qc == NQC - 1:
                    self.c[f"_khgfree{sl}"] = self.cur("pe")
                    self.c["_vhgfree"] = self.cur("pe")
                    self.c["_brwfree"] = self.cur("dve")
                    emit_vhg_loads(hg + 1)
                for i4 in range(4):
                    self.c[f"_exp{i4}"] = self.cur("sc")
            self.c["_hg_done"] = self.cur("dve")
        self.c["_flash_free"] = self.cur("pe")
        self._pf_attn = False

        if self.taps:
            self.wait(sy, "dve")
            self.inc(sy.dma_start(self.tap_adm.ap(), adm[:, :]), "tapst", 16)
        self.wait(te, "sc")
        self.wait(te, "dve")
        fl_scope.close()
        self.wait(sy, "pe", self.c["_flash_free"])
        self.wait(ge, "pe", self.c["_flash_free"])

        if "l" in self.phases and seq + 1 < self.SEQS:
            saved = self.seq
            self.seq = seq + 1
            self.phase_ln()
            self.seq = saved
        self.wait(sy, "dve")
        for hh_ in range(HPG):
            if f"brwg{hh_}" in self.c:
                self.wait(sy, f"brwg{hh_}")
        for ts_ in range(self.TC // 128):
            if f"dstg{ts_}" in self.c:
                self.wait(sy, f"dstg{ts_}")
        for tt in range(NTT if "o" in self.aparts else 0):
            for dcb in range(D // CD):
                idx = tt * (D // CD) + dcb
                bk = banks[idx % 2]
                slot = idx % 2
                key = f"_ors{slot}"
                if key in self.c:
                    self.wait(ge, f"sto{slot}", self.c[key])
                self.inc(ge.dma_start(xres[slot][:, :],
                                      self.x_sh[seq, tt * 128:(tt + 1) * 128,
                                                dcb * CD:(dcb + 1) * CD]),
                         f"xrg{slot}", 16)
                self.inc(ge.dma_start(fres[slot][:, :],
                                      self.ffn_hbm.ap()[tt * 128:(tt + 1) * 128,
                                                        dcb * CD:(dcb + 1) * CD]),
                         f"xrg{slot}", 16)
                okey = f"_opev{slot}"
                if okey in self.c:
                    self.wait(te, "dve", self.c[okey])
                elif idx < 2:
                    self.wait(te, "dve")
                    self.wait(te, "sc")
                if idx == 0:
                    self.wait(te, "owl", self.cur("owl"))
                for j in range(NMQ):
                    mm = te.matmul(bk[:, 0:CD], adm[:, j * S + tt * 128: j * S + (tt + 1) * 128],
                              owT_sb[:, j * D + dcb * CD: j * D + (dcb + 1) * CD],
                              start=(j == 0), stop=(j == NMQ - 1))
                self.inc(mm, "pe")
                self.wait(ve, "pe")
                self.wait(ve, f"xrg{slot}")
                self.dw()
                self.inc(ve.scalar_tensor_tensor(ores[slot][:, :], bk[:, 0:CD], 0.0,
                                                 xres[slot][:, :], op0=ALU.bypass,
                                                 op1=ALU.add), "dve")
                self.c[okey] = self.cur("dve")
                self.dw()
                self.inc(ve.tensor_tensor(ost[slot][:, :], ores[slot][:, :],
                                          fres[slot][:, :], op=ALU.add), "dve")
                self.wait(sy, "dve")
                self.inc(sy.dma_start(
                    self.out_sh[seq, tt * 128:(tt + 1) * 128, dcb * CD:(dcb + 1) * CD],
                    ost[slot][:, :]), f"sto{slot}", 16)
                self.c[key] = self.cur(f"sto{slot}")


def kernel(L0_attn_x, L0_attn_emb_weight, L0_attn_qkv_w, L0_attn_o_w,
           L0_ffn_norm_w, L0_ffn_w1, L0_ffn_w3, L0_ffn_w2, L0_ffn_gate_w,
           L0_ffn_a1, L0_ffn_b1, L0_ffn_a3, L0_ffn_b3, L0_ffn_a2, L0_ffn_b2):
    cfg = CFG
    weights = dict(qkv_w=L0_attn_qkv_w, o_w=L0_attn_o_w, emb_weight=L0_attn_emb_weight,
                   norm_w=L0_ffn_norm_w, w1=L0_ffn_w1, w3=L0_ffn_w3, w2=L0_ffn_w2,
                   gate_w=L0_ffn_gate_w, a1=L0_ffn_a1, b1=L0_ffn_b1, a3=L0_ffn_a3,
                   b3=L0_ffn_b3, a2=L0_ffn_a2, b2=L0_ffn_b2)
    weights = {k: np.asarray(v, np.float32) for k, v in weights.items()}
    hp = host_prep(cfg, weights)
    x = np.asarray(L0_attn_x, np.float32)
    B = x.shape[0]
    per = B // N_CORES

    nc = B_build_cached(cfg)
    in_maps = []
    for c in range(N_CORES):
        m = dict(hp)
        m["x_sh"] = np.ascontiguousarray(x[c * per:(c + 1) * per])
        in_maps.append(m)
    res = run_bass_kernel_spmd(nc, in_maps, core_ids=list(range(N_CORES)))
    out = np.concatenate([r["out_sh"] for r in res.results], axis=0)
    return out


_cache = {}


def B_build_cached(cfg):
    key = tuple(sorted(cfg.items()))
    if key not in _cache:
        _cache[key] = B(cfg).build()
    return _cache[key]


# version 38
# speedup vs baseline: 1.0485x; 1.0485x over previous
import contextlib
import math

import numpy as np
import ml_dtypes

import concourse.bass as bass
import concourse.mybir as mybir
from concourse.bass import IndirectOffsetOnAxis
from concourse.bass_utils import run_bass_kernel_spmd
from bass_rust import AP

F32 = mybir.dt.float32
BF16 = mybir.dt.bfloat16
I32 = mybir.dt.int32
AF = mybir.ActivationFunctionType
ALU = mybir.AluOpType
AX = mybir.AxisListType
BF = ml_dtypes.bfloat16

N_CORES = 8


def make_cfg(S=2048, D=1024, F=4096, H=16, DH=64, E=8, R=16, SEQS=2,
             TC=512, QC=512, HPG=4):
    c = dict(S=S, D=D, F=F, H=H, DH=DH, E=E, R=R, SEQS=SEQS, NB=32, MAXD=128,
             TC=TC, QC=QC, HPG=HPG)
    c["NDC"] = D // 128
    c["NFC"] = F // 128
    c["NTT"] = S // 128
    c["NTC"] = S // TC
    c["NQC"] = S // QC
    c["NKC"] = S // 128
    c["NHG"] = H // HPG
    c["LT"] = 2 * S - 1
    c["LTP"] = ((2 * S - 1 + 511) // 512) * 512
    c["WBR"] = 2 * S - 128
    c["NMQ"] = H * DH // 128
    c["HPT"] = 128 // DH
    assert TC % 128 == 0 and QC % 128 == 0 and QC <= 512
    assert H % HPG == 0 and HPG % c["HPT"] == 0
    return c


CFG = make_cfg()


def rel_bucket_rev(S, NB=32, MAXD=128):
    half = NB // 2
    max_exact = half // 2
    coeff = (half - max_exact) / math.log(MAXD / max_exact)
    delta = np.arange(S - 1, -S, -1)
    arel = np.abs(delta)
    base = np.where(delta > 0, half, 0)
    distant = max_exact + (np.log(np.maximum(arel, 1).astype(np.float32) / max_exact)
                           * coeff).astype(np.int32)
    distant = np.minimum(distant, half - 1)
    return (base + np.where(arel < max_exact, arel, distant)).astype(np.int64)


def host_prep(cfg, w):
    S, D, F, H, DH, E, R, NB = (cfg[k] for k in "S D F H DH E R NB".split())
    out = {}
    out["qkv_wT"] = np.ascontiguousarray(w["qkv_w"].T)
    out["o_wT"] = np.ascontiguousarray(w["o_w"].T)
    out["w1T"] = np.ascontiguousarray(w["w1"].T)
    out["w3T"] = np.ascontiguousarray(w["w3"].T)
    out["w2T"] = np.ascontiguousarray(w["w2"].T)
    out["gate_wT"] = np.ascontiguousarray(w["gate_w"].T)
    out["emb"] = np.ascontiguousarray(w["emb_weight"])
    out["norm_w"] = np.ascontiguousarray(w["norm_w"].reshape(1, D))

    a1, a3 = w["a1"], w["a3"]
    a13p = np.zeros((D, 768), np.float32)
    a13n = np.zeros((D, 768), np.float32)
    for wi, a in ((0, a1), (1, a3)):
        for e in range(E):
            b = wi * E + e
            base = (b // 3) * 128 + 32 * (b % 3)
            a13p[:, base:base + R] = a[e].T
            if e > 0:
                a13n[:, base + 16:base + 16 + R] = a[e - 1].T
    out["a13P_pos"], out["a13P_neg"] = a13p, a13n

    b1, b3 = w["b1"], w["b3"]
    b13 = np.zeros((2, E, 32, F), np.float32)
    for wi, b in ((0, b1), (1, b3)):
        for e in range(E):
            b13[wi, e, 0:R] = b[e].T
            if e > 0:
                b13[wi, e, 16:16 + R] = b[e - 1].T
    out["b13P"] = np.ascontiguousarray(b13.reshape(2 * E * 32, F))

    out["a2T"] = np.ascontiguousarray(w["a2"].reshape(E * R, F).T)

    b2 = w["b2"]
    b2p = np.zeros((384, D), np.float32)
    for e in range(E):
        k, j = divmod(e, 3)
        b2p[128 * k + 32 * j: 128 * k + 32 * j + R] = b2[e].T
    out["b2T_pad"] = b2p

    LT, LTP = cfg["LT"], cfg["LTP"]
    bt = rel_bucket_rev(S, NB, cfg["MAXD"])
    ohr = np.zeros((NB, LTP), BF)
    ohr[bt, np.arange(LT)] = BF(1.0)
    out["ohr"] = ohr
    out["gidx"] = (127 - np.arange(128, dtype=np.int32)).reshape(128, 1)
    out["id_bf"] = np.eye(128, dtype=BF)
    NMQ = cfg["NMQ"]
    sel = np.zeros((128, NMQ * 16), BF)
    for mc in range(NMQ):
        sel[0:64, mc * 16 + 2 * mc] = BF(1.0)
        sel[64:128, mc * 16 + 2 * mc + 1] = BF(1.0)
    out["sels"] = sel
    sel8 = np.zeros((E, E * 128), BF)
    for e in range(E):
        sel8[e, e * 128:(e + 1) * 128] = BF(1.0)
    out["sel8"] = sel8
    return out


class B:
    taps = False
    _pf_attn = False
    _vhgsem = "vhgl"
    phases = "lfa"
    aparts = "qvbxo"

    def __init__(self, cfg, dbg=False):
        self.cfg = cfg
        self.nc = bass.Bass("TRN2", target_bir_lowering=False, debug=dbg)
        for k, v in cfg.items():
            setattr(self, k, v)
        self.HDx = self.H * self.DH
        self.c = {}
        self.sems = {}
        self.st = None
        self._khgsem = {0: "khgl", 1: "khgl"}
        self._vhgsem = "vhgl"

    def sem(self, name):
        if name not in self.sems:
            self.sems[name] = self.st.enter_context(self.nc.semaphore(name))
            self.c[name] = 0
        return self.sems[name]

    def inc(self, instr, name, n=1):
        instr.then_inc(self.sem(name), n)
        self.c[name] += n
        return self.c[name]

    def cur(self, name):
        self.sem(name)
        return self.c[name]

    def wait(self, eng, name, value=None):
        if value is None:
            value = self.c[name]
        assert value <= self.c.get(name, 0), (name, value)
        if value > 0:
            eng.wait_ge(self.sems[name], value)

    def sb(self, n, sh, dt=F32):
        return self.st.enter_context(
            self.nc.sbuf_tensor(n, sh, dt, allow_name_mangling=True))

    def psb(self, n, sh, dt=F32):
        return self.st.enter_context(self.nc.psum_tensor(n, sh, dt))

    def barrier(self):
        self.drain_stores()
        self.nc.all_engine_barrier()
        self.c = {k: v for k, v in self.c.items() if not k.startswith("_")}

    def soft_barrier(self):
        self.drain_stores()
        for q in (self.sy, self.ge):
            for s in ("pe", "dve", "sc", "po"):
                if self.c.get(s, 0) > 0:
                    q.wait_ge(self.sems[s], self.c[s])

    def drain_stores(self):
        for name in list(self.c):
            if name.startswith(("sto", "dstg", "xnst", "qkst", "w13l", "b13l",
                                "w2l", "qkvwl", "vwl", "khgl", "vhgl", "qsll",
                                "xrl", "brwg", "xld", "tapst")):
                if self.c[name] > 0:
                    self.wait(self.sy, name, self.c[name])

    def dw(self):
        if self.c.get("dve", 0) > 0:
            self.ve.wait_ge(self.sems["dve"], self.c["dve"])

    def sw(self):
        if self.c.get("sc", 0) > 0:
            self.se.wait_ge(self.sems["sc"], self.c["sc"])

    def gw(self):
        if self.c.get("po", 0) > 0:
            self.ge.wait_ge(self.sems["po"], self.c["po"])

    def build(self):
        nc, cfg = self.nc, self.cfg
        S, D, F, H, DH, E, R, NB = (cfg[k] for k in "S D F H DH E R NB".split())
        NDC, NFC, NTT, TC, NTC = (cfg[k] for k in "NDC NFC NTT TC NTC".split())
        LTP, NMQ, SEQS = cfg["LTP"], cfg["NMQ"], cfg["SEQS"]
        HDx = self.HDx

        din = lambda n, sh, dt=F32: nc.dram_tensor(n, sh, dt, kind="ExternalInput").ap()
        self.x_sh = din("x_sh", [SEQS, S, D])
        self.qkv_wT = din("qkv_wT", [D, 3 * HDx])
        self.o_wT = din("o_wT", [self.HDx, D])
        self.w1T = din("w1T", [D, F])
        self.w3T = din("w3T", [D, F])
        self.w2T = din("w2T", [F, D])
        self.gate_wT = din("gate_wT", [D, E])
        self.emb = din("emb", [NB, H])
        self.norm_w = din("norm_w", [1, D])
        self.a13P_pos = din("a13P_pos", [D, 768])
        self.a13P_neg = din("a13P_neg", [D, 768])
        self.b13P = din("b13P", [2 * E * 32, F])
        self.a2T = din("a2T", [F, E * R])
        self.b2T_pad = din("b2T_pad", [384, D])
        self.ohr = din("ohr", [NB, LTP], BF16)
        self.gidx = din("gidx", [128, 1], I32)
        self.id_bf = din("id_bf", [128, 128], BF16)
        self.sels = din("sels", [128, NMQ * 16], BF16)
        self.sel8 = din("sel8", [E, E * 128], BF16)
        self.out_sh = nc.dram_tensor("out_sh", [SEQS, S, D], F32,
                                     kind="ExternalOutput").ap()

        dscr = lambda n, sh, dt=BF16: nc.dram_tensor(n, sh, dt, kind="Internal")
        self.xn_hbm = dscr("xn_hbm", [S, D])
        self.nx_hbm = dscr("nx_hbm", [S, D])
        self.q_hbm = dscr("q_hbm", [HDx, S])
        self.k_hbm = dscr("k_hbm", [HDx, S])
        self.v_hbm = dscr("v_hbm", [S, HDx])
        self.ffn_hbm = dscr("ffn_hbm", [S, D])
        self.ebr_hbm = dscr("ebr_hbm", [H, LTP])
        self.w1_bf = dscr("w1_bf", [D, F])
        self.w3_bf = dscr("w3_bf", [D, F])
        self.w2_bf = dscr("w2_bf", [F, D])
        self.qkvT_bf = dscr("qkvT_bf", [D, 3 * HDx])
        self.owT_bf = dscr("owT_bf", [self.HDx, D])
        self.b13_bf = dscr("b13_bf", [2 * E * 32, F])
        if self.taps:
            self.tap_r13 = dscr("tap_r13", [128, 8 * S])
            self.tap_wT = dscr("tap_wT", [E, S])
            self.tap_adm = dscr("tap_adm", [128, cfg["NMQ"] * S])
            self.tap_wact = dscr("tap_wact", [cfg["NTC"], 128, cfg["NFC"] * cfg["TC"]])
            self.tap_pex = dscr("tap_pex", [128, cfg["QC"]])
            self.tap_ptil = dscr("tap_ptil", [128, cfg["QC"]])
            self.tap_av = dscr("tap_av", [65, cfg["QC"]])
            self.tap_brw = dscr("tap_brw", [128, cfg["WBR"]])
            self.tap_aw = dscr("tap_aw", [cfg["E"], 128, cfg["TC"]])
            self.tap_wb = dscr("tap_wb", [128, cfg["E"] * cfg["TC"]])
            self.tap_h1 = dscr("tap_h1", [cfg["E"], 128, cfg["TC"]], mybir.dt.float32)
            self.tap_h3 = dscr("tap_h3", [cfg["E"], 128, cfg["TC"]], mybir.dt.float32)
            self.tap_r2w = dscr("tap_r2w", [3, 128, cfg["TC"]])

        with contextlib.ExitStack() as st:
            self.st = st
            te, ve, se, sy, ge = nc.tensor, nc.vector, nc.scalar, nc.sync, nc.gpsimd
            self.te, self.ve, self.se, self.sy, self.ge = te, ve, se, sy, ge

            self.ones1 = self.sb("ones1", [1, 128], BF16)
            self.id_sb = self.sb("id_sb", [128, 128], BF16)
            self.sels_sb = self.sb("sels_sb", [128, NMQ * 16], BF16)
            self.sel8_sb = self.sb("sel8_sb", [8, 8 * 128], BF16)
            self.gidx_sb = st.enter_context(nc.sbuf_tensor("gidx_sb", [128, 1], I32))
            self.emb_sb = self.sb("emb_sb", [NB, H], BF16)
            self.nwb = self.sb("nwb", [128, D], BF16)
            self.gate_sb = self.sb("gate_sb", [128, NDC * E], BF16)
            self.a2T_sb = self.sb("a2T_sb", [128, NFC * E * R], BF16)
            self.b2p_sb = self.sb("b2p_sb", [128, 3 * D], BF16)
            self.biasmat = self.sb("biasmat", [128, H], F32)
            self.w_T = self.sb("w_T", [E, self.cfg["TC"]], BF16)
            self.qm2 = self.sb("qm2", [H, 2], F32)
            self.small = self.sb("small", [128, 32], F32)
            self.eps = self.sb("eps", [128, 4], F32)
            self.banks = [self.psb(f"bank{i}", [128, 512]) for i in range(8)]
            cfg_ = self.cfg
            self.khg0 = self.sb("khg0", [128, (cfg_["HPG"] // self.HPT) * self.S], BF16)
            self.vhg = self.sb("vhg", [128, self.NTT * cfg_["HPG"] * (self.DH + 1)], BF16)

            self.prelude()
            if "l" in self.phases:
                self.seq = 0
                self.phase_ln()
                self.soft_barrier()
            for seq in range(SEQS):
                self.seq = seq
                if "f" in self.phases:
                    with contextlib.ExitStack() as ffn_st:
                        outer = self.st
                        self.st = ffn_st
                        self.phase_ffn()
                        self.st = outer
                    self.soft_barrier()
                if "a" in self.phases:
                    with contextlib.ExitStack() as att_st:
                        outer = self.st
                        self.st = att_st
                        self.phase_attn()
                        self.st = outer
                    self.soft_barrier()
            if "sto0" in self.c:
                self.wait(sy, "sto0")
                self.wait(sy, "sto1")
            self.drain_stores()
            if self.taps and "tapst" in self.c:
                self.wait(sy, "tapst")
            self.st = None
        return nc

    def prelude(self):
        nc, te, ve, se, sy, ge = self.nc, self.te, self.ve, self.se, self.sy, self.ge
        D, F, E, NB, H, NDC, NMQ, LTP = (self.D, self.F, self.E, self.NB, self.H,
                                         self.NDC, self.NMQ, self.LTP)
        banks = self.banks

        sy.dma_start(self.id_sb[:, :], self.id_bf).then_inc(self.sem("ldc"), 16)
        sy.dma_start(self.sels_sb[:, :], self.sels).then_inc(self.sem("ldc"), 16)
        sy.dma_start(self.sel8_sb[0:self.E, 0:self.E * 128], self.sel8).then_inc(self.sem("ldc"), 16)
        sy.dma_start(self.gidx_sb[:, :], self.gidx).then_inc(self.sem("ldc"), 16)
        self.c["ldc"] += 64

        ge.dma_start(self.emb_sb[:, :], self.emb).then_inc(self.sem("ldg"), 16)
        ge.dma_start(self.gate_sb[:, :].rearrange("p (c e) -> p c e", c=NDC),
                     self.gate_wT.rearrange("(c p) e -> p c e", p=128)
                     ).then_inc(self.sem("ldg"), 16)
        ge.dma_start(self.a2T_sb[:, :].rearrange("p (c n) -> p c n", c=self.NFC),
                     self.a2T.rearrange("(c p) n -> p c n", p=128)
                     ).then_inc(self.sem("ldg"), 16)
        ge.dma_start(self.b2p_sb[:, :].rearrange("p (k n) -> p k n", k=3),
                     self.b2T_pad.rearrange("(k p) n -> p k n", p=128)
                     ).then_inc(self.sem("ldg"), 16)
        self.c["ldg"] += 64
        nw_row = self.sb("nw_row", [1, D], BF16)
        ge.dma_start(nw_row[:, :], self.norm_w).then_inc(self.sem("ldg"), 16)
        self.c["ldg"] += 16

        for dst, src, nm in ((self.qkvT_bf, self.qkv_wT, "wcqkv"),
                             (self.w1_bf, self.w1T, "wcw1"),
                             (self.w3_bf, self.w3T, "wcw3"),
                             (self.b13_bf, self.b13P, "wcb13"),
                             (self.w2_bf, self.w2T, "wcw2"),
                             (self.owT_bf, self.o_wT, "wcow")):
            ge.dma_start(dst.ap(), src).then_inc(self.sem(nm), 16)
            self.c[nm] += 16

        self.wait(ve, "ldc")
        self.dw()
        self.inc(ve.memset(self.ones1[:, :], 1.0), "dve")
        self.dw()
        self.inc(ve.memset(self.eps[:, 0:1], 1e-5), "dve")
        self.dw()
        self.inc(ve.memset(self.eps[:, 1:2], 1e-6), "dve")
        self.dw()
        self.inc(ve.memset(self.eps[:, 2:3], 1e-30), "dve")
        self.dw()
        self.inc(ve.memset(self.eps[:, 3:4], -2.0794415416798357), "dve")
        vb = self.vhg
        vones = AP(vb[:, :].tensor, vb[:, :].offset + self.DH,
                   [list(vb[:, :].ap[0]),
                    [self.DH + 1, self.NTT * self.cfg["HPG"]], [1, 1]])
        self.dw()
        self.inc(ve.memset(vones, 1.0), "dve")
        self.c["_vones"] = self.cur("dve")
        self.c["_preeps"] = self.cur("dve")

        self.wait(te, "ldg")
        self.wait(te, "dve")
        for j in range(D // 512):
            bk = banks[j % 2]
            if j >= 2:
                self.wait(te, "dve", self.c[f"_nwb{j % 2}"])
            mm = te.matmul(bk[:, :], self.ones1[:, :], nw_row[:, j * 512:(j + 1) * 512],
                      start=True, stop=True)
            self.inc(mm, "pe")
            self.wait(ve, "pe")
            self.dw()
            self.c[f"_nwb{j % 2}"] = self.inc(
                ve.tensor_copy(self.nwb[:, j * 512:(j + 1) * 512], bk[:, :]), "dve")

        self.a13p_sb = self.sb("a13p_sb", [128, NDC * 768], BF16)
        pre_st = contextlib.ExitStack()
        outer_st = self.st
        self.st = pre_st
        ohr_sb = self.sb("ohr_sb", [NB, LTP], BF16)
        sy.dma_start(ohr_sb[:, :], self.ohr).then_inc(self.sem("ldo"), 16)
        self.c["ldo"] += 16
        ebr_stage = self.sb("ebr_stage", [H, LTP], BF16)
        self.wait(te, "ldo")
        self.wait(te, "dve")
        for j in range(LTP // 512):
            bk = banks[j % 2]
            if j >= 2:
                self.wait(te, "sc", self.c[f"_ebr{j % 2}"])
            mm = te.matmul(bk[0:H, :], self.emb_sb[:, :],
                      ohr_sb[:, j * 512:(j + 1) * 512], start=True, stop=True)
            self.inc(mm, "pe")
            self.wait(se, "pe")
            self.sw()
            self.c[f"_ebr{j % 2}"] = self.inc(
                se.activation(ebr_stage[:, j * 512:(j + 1) * 512], bk[0:H, :],
                              AF.Exp), "sc")
        self.wait(sy, "sc")
        sy.dma_start(self.ebr_hbm.ap(), ebr_stage[:, :]).then_inc(self.sem("ebr"), 16)
        self.c["ebr"] += 16

        a13_stage = self.sb("a13_stage", [128, 1536], F32)
        for dc in range(NDC):
            if dc > 0:
                self.wait(sy, "dve")
            sy.dma_start(a13_stage[:, 0:768],
                         self.a13P_pos[dc * 128:(dc + 1) * 128, :]
                         ).then_inc(self.sem("a13"), 16)
            sy.dma_start(a13_stage[:, 768:1536],
                         self.a13P_neg[dc * 128:(dc + 1) * 128, :]
                         ).then_inc(self.sem("a13"), 16)
            self.c["a13"] += 32
            self.wait(ve, "a13")
            self.dw()
            self.inc(ve.scalar_tensor_tensor(
                self.a13p_sb[:, dc * 768:(dc + 1) * 768],
                a13_stage[:, 768:1536], -1.0, a13_stage[:, 0:768],
                op0=ALU.mult, op1=ALU.add), "dve")
        self.c["_a13p"] = self.cur("dve")

        for b in (4, 5, 6):
            self.dw()
            self.inc(ve.memset(banks[b][:, :], 0.0), "dve")
        self.wait(sy, "ebr", 16)
        self.soft_barrier()
        self.st = outer_st
        pre_st.close()

    def phase_ln(self):
        nc, te, ve, se, sy = self.nc, self.te, self.ve, self.se, self.sy
        D, NTT, seq = self.D, self.NTT, self.seq
        ln_st = contextlib.ExitStack()
        outer_st = self.st
        self.st = ln_st
        xload = [self.sb(f"xload{i}", [128, D]) for i in range(2)]
        xn_stage = [self.sb(f"xn_stage{i}", [128, D], BF16) for i in range(2)]
        nx_stage = [self.sb(f"nx_stage{i}", [128, D], BF16) for i in range(2)]
        sqdump = [self.sb(f"sqdump{i}", [128, D]) for i in range(2)]
        stat = [self.sb(f"stat{i}", [128, 8]) for i in range(2)]
        self.st = outer_st
        nwb = self.nwb

        for tt in range(NTT):
            slot = tt % 2
            stt = stat[slot]
            if f"_lnxv{slot}" in self.c:
                self.wait(sy, "dve", self.c[f"_lnxv{slot}"])
                self.wait(sy, "sc", self.c[f"_lnxs{slot}"])
            sy.dma_start(xload[slot][:, :], self.x_sh[seq, tt * 128:(tt + 1) * 128, :]
                         ).then_inc(self.sem(f"xld{slot}"), 16)
            self.c[f"xld{slot}"] += 16
            self.wait(ve, f"xld{slot}")
            if f"_lnsc{slot}" in self.c:
                self.wait(ve, "sc", self.c[f"_lnsc{slot}"])
            self.dw()
            self.inc(ve.tensor_reduce(stt[:, 0:1], xload[slot][:, :],
                                      axis=AX.X, op=ALU.add), "dve")
            self.dw()
            self.inc(ve.tensor_scalar(stt[:, 0:1], stt[:, 0:1], 1.0 / D, None,
                                      op0=ALU.mult), "dve")
            mn_done = self.cur("dve")
            self.wait(se, f"xld{slot}")
            self.wait(se, "dve", mn_done)
            self.sw()
            self.inc(se.activation(sqdump[slot][:, :], xload[slot][:, :], AF.Square,
                                   accum_out=stt[:, 1:2]), "sc")
            sq_done = self.cur("sc")
            self.c[f"_lnxs{slot}"] = sq_done
            self.wait(ve, "sc", sq_done)
            self.dw()
            self.inc(ve.scalar_tensor_tensor(stt[:, 2:3], stt[:, 0:1], 0.0,
                                             stt[:, 0:1], op0=ALU.bypass,
                                             op1=ALU.mult), "dve")
            self.dw()
            self.inc(ve.scalar_tensor_tensor(stt[:, 2:3], stt[:, 1:2], 1.0 / D,
                                             stt[:, 2:3], op0=ALU.mult,
                                             op1=ALU.subtract), "dve")
            var_done = self.cur("dve")
            self.wait(se, "dve", var_done)
            self.sw()
            self.inc(se.activation(stt[:, 3:4], stt[:, 2:3], AF.Ln,
                                   bias=self.eps[:, 0:1], scale=1.0), "sc")
            self.sw()
            self.inc(se.activation(stt[:, 4:5], stt[:, 3:4], AF.Exp,
                                   bias=0.0, scale=-1.0), "sc")
            self.sw()
            self.inc(se.activation(stt[:, 3:4], stt[:, 3:4], AF.Exp,
                                   bias=0.0, scale=-0.5), "sc")
            rcp_done = self.cur("sc")
            self.wait(ve, "sc", rcp_done)
            self.dw()
            self.inc(ve.scalar_tensor_tensor(stt[:, 5:6], stt[:, 2:3], 0.0,
                                             stt[:, 4:5], op0=ALU.bypass,
                                             op1=ALU.mult), "dve")
            u_done = self.cur("dve")
            self.wait(se, "dve", u_done)
            self.sw()
            self.inc(se.activation(stt[:, 6:7], stt[:, 5:6], AF.Ln,
                                   bias=self.eps[:, 1:2], scale=1.0), "sc")
            self.sw()
            self.inc(se.activation(stt[:, 6:7], stt[:, 6:7], AF.Exp,
                                   bias=0.0, scale=-0.5), "sc")
            s_done = self.cur("sc")
            self.c[f"_lnsc{slot}"] = s_done
            self.wait(ve, "sc", s_done)
            stk = f"_xnst{slot}"
            if stk in self.c:
                self.wait(ve, f"xnst{slot}", self.c[stk])
            self.dw()
            self.inc(ve.tensor_scalar(xn_stage[slot][:, :], xload[slot][:, :],
                                      stt[:, 0:1], stt[:, 3:4],
                                      op0=ALU.subtract, op1=ALU.mult), "dve")
            self.dw()
            self.inc(ve.scalar_tensor_tensor(nx_stage[slot][:, :],
                                             xn_stage[slot][:, :], stt[:, 6:7],
                                             nwb[:, :], op0=ALU.mult,
                                             op1=ALU.mult), "dve")
            self.c[f"_ln{slot}"] = self.cur("dve")
            self.c[f"_lnxv{slot}"] = self.cur("dve")
            self.wait(sy, "dve", self.c[f"_ln{slot}"])

            self.inc(sy.dma_start(self.xn_hbm.ap()[tt * 128:(tt + 1) * 128, :],
                                  xn_stage[slot][:, :]), f"xnst{slot}", 16)
            self.inc(sy.dma_start(self.nx_hbm.ap()[tt * 128:(tt + 1) * 128, :],
                                  nx_stage[slot][:, :]), f"xnst{slot}", 16)
            self.c[stk] = self.cur(f"xnst{slot}")
            self.c[f"_xnsT{tt}"] = (slot, self.cur(f"xnst{slot}"))
        ln_st.close()

    def transpose_in(self, src_hbm, dst_sb, sem_name):
        S, NDC = self.S, self.NDC
        for dc in range(NDC):
            self.inc(self.sy.dma_start_transpose(
                dst_sb[:, dc * S:(dc + 1) * S],
                src_hbm.ap()[:, dc * 128:(dc + 1) * 128]), sem_name, 16)

    def phase_ffn(self):
        nc, te, ve, se, sy, ge = self.nc, self.te, self.ve, self.se, self.sy, self.ge
        S, D, F, E, R, H, DH = self.S, self.D, self.F, self.E, self.R, self.H, self.DH
        NDC, NFC, NTT, TC, NTC = self.NDC, self.NFC, self.NTT, self.TC, self.NTC
        NMQ, HDx = self.NMQ, self.HDx
        banks, seq = self.banks, self.seq
        sq = f"s{seq}"
        CD = min(512, D)
        C4 = min(512, S)
        NT4 = S // C4
        CV = min(512, HDx)
        NVC = HDx // CV

        r13 = self.sb("r13", [128, 6 * TC], BF16)
        wact = self.sb("wact", [128, NFC * TC], BF16)
        aw = [self.sb(f"aw{i}", [128, TC], BF16) for i in range(6)]
        tsum = [self.sb(f"tsum{i}", [128, TC], BF16) for i in range(4)]
        sring = [self.sb(f"sring{i}", [128, TC], BF16) for i in range(4)]
        h3ring = [self.sb(f"h3ring{i}", [128, TC], BF16) for i in range(4)]
        wb_sb = self.sb("wb_sb", [128, E * TC], BF16)
        r2w_sb = [self.sb(f"r2w{i}", [128, TC], BF16) for i in range(3)]
        w13t = [self.sb(f"w13t{i}", [128, 2 * NDC * 128], BF16) for i in range(4)]
        b13t = [self.sb(f"b13t{i}", [128, 2 * E * 128], BF16) for i in range(4)]
        w2t = [self.sb(f"w2t{i}", [128, CD], BF16) for i in range(3)]
        dstage = [self.sb(f"dstage{i}", [128, CD], BF16) for i in range(2)]
        gate_f = self.sb("gate_f", [128, 32], F32)
        wt_bf = self.sb("wt_bf", [128, E], BF16)
        nxt = [self.sb(f"nxt{i}", [128, NDC * TC], BF16) for i in range(2)]
        xnt = [self.sb(f"xnt{i}", [128, NDC * C4], BF16) for i in range(2)]
        qkvw = [self.sb(f"qkvw{i}", [128, NDC * 128], BF16) for i in range(2)]
        vw = self.sb("vw", [128, NDC * CV], BF16)
        qkv_stage = [self.sb(f"qkvs{i}", [128, C4], BF16) for i in range(2)]
        sq_sb = [self.sb("sqsb0", [128, C4], BF16)]

        self.wait(te, "sc")
        self.wait(te, "dve")

        def load_nx(tc_i):
            if tc_i >= NTC:
                return
            sl = tc_i % 2
            key = f"_nxu{sl}"
            if key in self.c:
                self.wait(sy, "pe", self.c[key])
            for tt_ in range(tc_i * (TC // 128), (tc_i + 1) * (TC // 128)):
                slt, cnt = self.c[f"_xnsT{tt_}"]
                self.wait(sy, f"xnst{slt}", cnt)
            for dc in range(NDC):
                self.inc(sy.dma_start_transpose(
                    nxt[sl][:, dc * TC:(dc + 1) * TC],
                    self.nx_hbm.ap()[tc_i * TC:(tc_i + 1) * TC, dc * 128:(dc + 1) * 128]),
                    f"nxl{sl}", 16)
            self.c[f"_nxc{sl}"] = self.cur(f"nxl{sl}")

        def load_xn(t4_i):
            if t4_i >= NT4:
                return
            sl = t4_i % 2
            key = f"_xnu{sl}"
            if key in self.c:
                self.wait(sy, "pe", self.c[key])
            for tt_ in range(t4_i * (C4 // 128), (t4_i + 1) * (C4 // 128)):
                slt, cnt = self.c[f"_xnsT{tt_}"]
                self.wait(sy, f"xnst{slt}", cnt)
            for dc in range(NDC):
                self.inc(sy.dma_start_transpose(
                    xnt[sl][:, dc * C4:(dc + 1) * C4],
                    self.xn_hbm.ap()[t4_i * C4:(t4_i + 1) * C4, dc * 128:(dc + 1) * 128]),
                    f"xnl{sl}", 16)
            self.c[f"_xnc{sl}"] = self.cur(f"xnl{sl}")

        qk_order = []
        for _t4 in range(NT4):
            qk_order.extend(range(2 * NMQ))
        self._qkpf = dict(nload=0)

        def issue_qkvw():
            k = self._qkpf["nload"]
            if k >= len(qk_order):
                return
            mc_i = qk_order[k]
            sl = k % 2
            key = f"_qwu{sl}"
            if k == 0:
                self.wait(sy, "wcqkv", 16)
            if key in self.c:
                self.wait(sy, "pe", self.c[key])
            self.inc(sy.dma_start(
                qkvw[sl][:, :].rearrange("p (c n) -> p c n", c=NDC),
                self.qkvT_bf.ap().rearrange("(c p) n -> p c n", p=128)[:, :, mc_i * 128:(mc_i + 1) * 128]),
                f"qwl{sl}", 16)
            self.c[f"_qwc{sl}"] = self.cur(f"qwl{sl}")
            self._qkpf["nload"] = k + 1

        qkv_units = []
        for t4 in range(NT4):
            for mc in range(2 * NMQ):
                qkv_units.append(("qk", t4, mc))
            for vc in range(NVC):
                for tt in range(C4 // 128):
                    qkv_units.append(("v", t4, t4 * (C4 // 128) + tt, vc))
        self._qkv_state = dict(idx=0, wu=0, t4loaded=-1)

        def emit_qkv_unit():
            st_ = self._qkv_state
            if st_["idx"] >= len(qkv_units):
                return False
            u = qkv_units[st_["idx"]]
            bk = banks[7]
            if u[0] == "qk":
                _, t4, mc = u
                kq = self._qkv_state.setdefault("nqk", 0)
                self._qkv_state["nqk"] = kq + 1
                if mc == 0:
                    load_xn(t4 + 1)
                wsl = kq % 2
                self.wait(te, f"xnl{t4 % 2}", self.c[f"_xnc{t4 % 2}"])
                self.wait(te, f"qwl{wsl}", self.c[f"_qwc{wsl}"])
                if "_qkv7" in self.c:
                    self.wait(te, "dve", self.c["_qkv7"])
                    self.wait(te, "sc", self.c["_qkv7sc"])
                for dc in range(NDC):
                    mm = te.matmul(bk[:, 0:C4], qkvw[wsl][:, dc * 128:(dc + 1) * 128],
                              xnt[t4 % 2][:, dc * C4:(dc + 1) * C4],
                              start=(dc == 0), stop=(dc == NDC - 1))
                self.c[f"_qwu{wsl}"] = self.inc(mm, "pe")
                pq = self.c[f"_qwu{wsl}"]
                self.c[f"_xnu{t4 % 2}"] = pq
                issue_qkvw()
                isq = mc < NMQ
                est = st_["idx"] % 2
                self.wait(ve, "pe", pq)
                if f"_qst{est}" in self.c:
                    self.wait(ve, f"qkst{est}", self.c[f"_qst{est}"])
                self.dw()
                if isq:
                    self.inc(ve.tensor_scalar(qkv_stage[est][:, :], bk[:, 0:C4], 0.125,
                                              None, op0=ALU.mult), "dve")
                else:
                    self.inc(ve.tensor_copy(qkv_stage[est][:, :], bk[:, 0:C4]), "dve")
                self.c["_qkv7"] = self.cur("dve")
                self.wait(sy, "dve")
                dst = self.q_hbm if isq else self.k_hbm
                r0 = (mc % NMQ) * 128
                self.inc(sy.dma_start(dst.ap()[r0:r0 + 128, t4 * C4:(t4 + 1) * C4],
                                      qkv_stage[est][:, :]), f"qkst{est}", 16)
                self.c[f"_qst{est}"] = self.cur(f"qkst{est}")
                ssl = 0
                self.wait(se, "pe", pq)
                self.wait(se, "dve", self.c["_qkv7"])
                if f"_sqr{ssl}" in self.c:
                    self.wait(se, "pe", self.c[f"_sqr{ssl}"])
                self.sw()
                self.c["_qkv7sc"] = self.inc(
                    se.activation(sq_sb[ssl][:, :], bk[:, 0:C4], AF.Square), "sc")
                sel = self.sels_sb
                self.wait(te, "sc", self.c["_qkv7sc"])
                self.wait(te, "dve", self.c["_qkv7"])
                mm = te.matmul(bk[0:H, 0:C4], sel[:, (mc % NMQ) * 16:(mc % NMQ) * 16 + H],
                          sq_sb[ssl][:, :], start=True, stop=True)
                self.c[f"_sqr{ssl}"] = self.inc(mm, "pe")
                self.wait(ve, "pe", self.c[f"_sqr{ssl}"])
                ii = 0 if isq else 1
                self.dw()
                self.inc(ve.tensor_reduce(self.small[0:H, 8 + ii:9 + ii], bk[0:H, 0:C4],
                                          axis=AX.X, op=ALU.max), "dve")
                self.dw()
                if t4 == 0 and mc % NMQ == 0:
                    self.inc(ve.tensor_copy(self.qm2[0:H, ii:ii + 1],
                                            self.small[0:H, 8 + ii:9 + ii]), "dve")
                else:
                    self.inc(ve.tensor_tensor(self.qm2[0:H, ii:ii + 1],
                                              self.qm2[0:H, ii:ii + 1],
                                              self.small[0:H, 8 + ii:9 + ii],
                                              op=ALU.max), "dve")
                self.c["_qkv7"] = self.cur("dve")
            else:
                _, t4, tt, vc = u
                load_vw(vc)
                self.wait(te, f"xnl{t4 % 2}", self.c[f"_xnc{t4 % 2}"])
                self.wait(te, "vwl", self.cur("vwl"))
                if "_qkv7" in self.c:
                    self.wait(te, "dve", self.c["_qkv7"])
                    self.wait(te, "sc", self.c["_qkv7sc"])
                lt = tt - t4 * (C4 // 128)
                for dc in range(NDC):
                    mm = te.matmul(bk[:, 0:CV],
                              xnt[t4 % 2][:, dc * C4 + lt * 128: dc * C4 + lt * 128 + 128],
                              vw[:, dc * CV:(dc + 1) * CV],
                              start=(dc == 0), stop=(dc == NDC - 1))
                pq = self.inc(mm, "pe")
                self.c[f"_xnu{t4 % 2}"] = pq
                self.c["_vwu"] = pq
                est = st_["idx"] % 2
                self.wait(ve, "pe", pq)
                if f"_qst{est}" in self.c:
                    self.wait(ve, f"qkst{est}", self.c[f"_qst{est}"])
                self.dw()
                self.inc(ve.tensor_copy(qkv_stage[est][:, 0:CV], bk[:, 0:CV]), "dve")
                self.c["_qkv7"] = self.cur("dve")
                self.wait(sy, "dve")
                self.inc(sy.dma_start(
                    self.v_hbm.ap()[tt * 128:(tt + 1) * 128, vc * CV:(vc + 1) * CV],
                    qkv_stage[est][:, 0:CV]), f"qkst{est}", 16)
                self.c[f"_qst{est}"] = self.cur(f"qkst{est}")
            st_["idx"] += 1
            return True

        self._vwst = dict(loaded=None)

        def load_vw(vc_i):
            if self._vwst["loaded"] == vc_i:
                return
            if self._vwst["loaded"] is None:
                self.wait(sy, "wcqkv", 16)
            if "_vwu" in self.c:
                self.wait(sy, "pe", self.c["_vwu"])
            self.inc(sy.dma_start(
                vw[:, :].rearrange("p (c n) -> p c n", c=NDC),
                self.qkvT_bf.ap().rearrange("(c p) n -> p c n", p=128)[:, :, 2 * HDx + vc_i * CV: 2 * HDx + (vc_i + 1) * CV]),
                "vwl", 16)
            self._vwst["loaded"] = vc_i

        load_nx(0)
        load_xn(0)
        issue_qkvw()
        issue_qkvw()

        lane = {0: (0, 1), 1: (2, 3)}
        r2w_banks = (4, 5, 6)
        for tc in range(NTC):
            t0 = tc * TC
            sl = tc % 2
            load_nx(tc + 1)
            if self.taps and self.cur("tapst") > 0:
                self.wait(ve, "tapst", self.cur("tapst"))
            self.wait(te, f"nxl{sl}", self.c[f"_nxc{sl}"])
            if "_devac" in self.c:
                self.wait(te, "dve", self.c["_devac"])
            if "_r2wevac2" in self.c:
                self.wait(te, "dve", self.c["_r2wevac2"])
            if "_laneA0" in self.c:
                self.wait(te, "sc", self.c["_laneA0"])
            for lt in range(TC // 128):
                bk = banks[6]
                if "_gate_done" in self.c:
                    self.wait(te, "dve", self.c["_gate_done"])
                if "_gate_sc" in self.c:
                    self.wait(te, "sc", self.c["_gate_sc"])
                for dc in range(NDC):
                    mm = te.matmul(bk[:, 0:E],
                              nxt[sl][:, dc * TC + lt * 128: dc * TC + (lt + 1) * 128],
                              self.gate_sb[:, dc * E:(dc + 1) * E],
                              start=(dc == 0), stop=(dc == NDC - 1))
                self.inc(mm, "pe")
                self.wait(ve, "pe")
                self.wait(ve, "sc")
                self.dw()
                self.inc(ve.tensor_reduce(gate_f[:, 0:1], bk[:, 0:E], axis=AX.X,
                                          op=ALU.max), "dve")
                self.dw()
                self.inc(ve.tensor_scalar(gate_f[:, 1:2], gate_f[:, 0:1], -1.0, None,
                                          op0=ALU.mult), "dve")
                self.wait(se, "dve")
                self.sw()
                self.c["_gate_sc"] = self.inc(
                    se.activation(gate_f[:, 2:2 + E], bk[:, 0:E], AF.Exp,
                                  bias=gate_f[:, 1:2], scale=1.0), "sc")
                self.wait(ve, "sc")
                eg = gate_f[:, 2:2 + E]
                self.dw()
                self.inc(ve.tensor_reduce(gate_f[:, 10:11], eg, axis=AX.X, op=ALU.max), "dve")
                self.dw()
                self.inc(ve.tensor_scalar(gate_f[:, 12:12 + E], eg, gate_f[:, 10:11],
                                          None, op0=ALU.is_equal), "dve")
                self.dw()
                self.inc(ve.tensor_scalar(gate_f[:, 20:20 + E], gate_f[:, 12:12 + E],
                                          -1.0, 1.0, op0=ALU.mult, op1=ALU.add), "dve")
                self.dw()
                self.inc(ve.tensor_tensor(gate_f[:, 20:20 + E], eg, gate_f[:, 20:20 + E],
                                          op=ALU.mult), "dve")
                self.dw()
                self.inc(ve.tensor_reduce(gate_f[:, 11:12], gate_f[:, 20:20 + E],
                                          axis=AX.X, op=ALU.max), "dve")
                self.dw()
                self.inc(ve.tensor_scalar(gate_f[:, 20:20 + E], gate_f[:, 20:20 + E],
                                          gate_f[:, 11:12], None, op0=ALU.is_equal), "dve")
                self.dw()
                self.inc(ve.tensor_tensor(gate_f[:, 12:12 + E], gate_f[:, 12:12 + E],
                                          gate_f[:, 20:20 + E], op=ALU.add), "dve")
                self.dw()
                self.inc(ve.scalar_tensor_tensor(gate_f[:, 30:31], gate_f[:, 10:11], 0.0,
                                                 gate_f[:, 11:12], op0=ALU.bypass,
                                                 op1=ALU.add), "dve")
                self.dw()
                self.inc(ve.reciprocal(gate_f[:, 31:32], gate_f[:, 30:31]), "dve")
                self.dw()
                self.inc(ve.tensor_tensor(gate_f[:, 12:12 + E], eg, gate_f[:, 12:12 + E],
                                          op=ALU.mult), "dve")
                self.dw()
                self.inc(ve.tensor_scalar(wt_bf[:, :], gate_f[:, 12:12 + E],
                                          gate_f[:, 31:32], None, op0=ALU.mult), "dve")
                self.c["_gate_done"] = self.cur("dve")
                bkt = banks[6][:, :].bitcast(BF16)
                self.wait(te, "dve")
                mm = te.transpose(bkt[0:E, 0:128], wt_bf[:, :], self.id_sb[:, :])
                self.inc(mm, "pe")
                self.wait(ve, "pe")
                self.dw()
                self.c["_wTe"] = self.inc(ve.tensor_copy(
                    self.w_T[:, lt * 128: (lt + 1) * 128], bkt[0:E, 0:128]), "dve")
                self.c["_gate_done"] = self.cur("dve")
                emit_qkv_unit()
            for j in range(6):
                bk = banks[j % 2]
                key = f"_r13b{j % 2}"
                if key in self.c:
                    self.wait(te, "dve", self.c[key])
                if tc == 0 and j < 2 and f"_lane{'AB'[j % 2]}" not in self.c:
                    pass
                for dc in range(NDC):
                    mm = te.matmul(bk[:, 0:TC],
                              self.a13p_sb[:, dc * 768 + j * 128: dc * 768 + (j + 1) * 128],
                              nxt[sl][:, dc * TC: dc * TC + TC],
                              start=(dc == 0), stop=(dc == NDC - 1))
                self.inc(mm, "pe")
                self.wait(ve, "pe")
                self.dw()
                self.c[key] = self.inc(
                    ve.tensor_scalar(r13[:, j * TC: (j + 1) * TC],
                                     bk[:, 0:TC], 2.0, None, op0=ALU.mult), "dve")
                emit_qkv_unit()
            self.wait(te, "dve")
            self.wait(te, "sc")
            for e in range(E):
                bk = banks[6]
                key = "_wb6"
                if key in self.c:
                    self.wait(te, "dve", self.c[key])
                mm = te.matmul(bk[:, 0:TC], self.sel8_sb[0:E, e * 128:(e + 1) * 128],
                          self.w_T[0:E, 0:TC], start=True, stop=True)
                self.inc(mm, "pe")
                self.wait(ve, "pe")
                self.dw()
                self.c[key] = self.inc(
                    ve.tensor_copy(wb_sb[:, e * TC:(e + 1) * TC], bk[:, 0:TC]), "dve")

            for b in r2w_banks:
                self.dw()
                self.wait(ve, "pe", self.c.get("_r2wfree", 0))
                self.inc(ve.memset(banks[b][:, 0:TC], 0.0), "dve")

            def issue_w13(fcp_i, ch_i):
                fc_i = 2 * fcp_i + ch_i
                if fc_i >= NFC:
                    return
                sl2 = ch_i * 2 + fcp_i % 2
                k1 = f"_w13u{sl2}"
                if fc_i == 0:
                    self.wait(sy, "wcw1", 16)
                    self.wait(sy, "wcw3", 16)
                    self.wait(sy, "wcb13", 16)
                if k1 in self.c:
                    self.wait(sy, "pe", self.c[k1])
                self.inc(sy.dma_start(
                    w13t[sl2][:, 0:NDC * 128].rearrange("p (c n) -> p c n", c=NDC),
                    self.w1_bf.ap().rearrange("(c p) n -> p c n", p=128)[:, :, fc_i * 128:(fc_i + 1) * 128]),
                    f"w13l{sl2}", 16)
                self.inc(sy.dma_start(
                    w13t[sl2][:, NDC * 128:].rearrange("p (c n) -> p c n", c=NDC),
                    self.w3_bf.ap().rearrange("(c p) n -> p c n", p=128)[:, :, fc_i * 128:(fc_i + 1) * 128]),
                    f"w13l{sl2}", 16)
                b13src = self.b13_bf.ap().rearrange("(w p) n -> p w n", p=32)[:, :, fc_i * 128:(fc_i + 1) * 128]
                for ro in (0, 32, 64):
                    self.inc(sy.dma_start(
                        b13t[sl2][ro:ro + 32, :].rearrange("p (w n) -> p w n", w=2 * E),
                        b13src), f"w13l{sl2}", 16)

            issue_w13(0, 0)
            issue_w13(0, 1)
            w13cnt = {i: self.cur(f"w13l{i}") for i in range(4)}
            for fcp in range(NFC // 2):
                for ch in (0, 1):
                    fc = 2 * fcp + ch
                    A, Bk = lane[ch]
                    wsl = ch * 2 + fcp % 2
                    issue_w13(fcp + 1, ch)
                    self.wait(te, f"w13l{wsl}", w13cnt[wsl])
                    nsl = ch * 2 + (fcp + 1) % 2
                    w13cnt[nsl] = self.cur(f"w13l{nsl}")
                    if f"_laneA{ch}" in self.c:
                        self.wait(te, "sc", self.c[f"_laneA{ch}"])
                    if f"_laneB{ch}" in self.c:
                        self.wait(te, "dve", self.c[f"_laneB{ch}"])
                    for dc in range(NDC):
                        mm = te.matmul(banks[A][:, 0:TC], w13t[wsl][:, dc * 128:(dc + 1) * 128],
                                  nxt[sl][:, dc * TC: dc * TC + TC],
                                  start=(dc == 0), stop=(dc == NDC - 1))
                    for dc in range(NDC):
                        mm = te.matmul(banks[Bk][:, 0:TC],
                                  w13t[wsl][:, NDC * 128 + dc * 128: NDC * 128 + (dc + 1) * 128],
                                  nxt[sl][:, dc * TC: dc * TC + TC],
                                  start=(dc == 0), stop=(dc == NDC - 1))
                    self.c[f"_w13u{wsl}"] = self.inc(mm, "pe")
                r2w_pending = []

                def emit_r2w():
                    for (fc_, e_, slot_) in r2w_pending:
                        kb_, jj_ = divmod(e_, 3)
                        self.wait(te, "po", self.c[f"_awd{slot_}"])
                        mm_ = te.matmul(
                            banks[r2w_banks[kb_]][32 * jj_:32 * jj_ + 16, 0:TC],
                            self.a2T_sb[:, fc_ * E * R + e_ * R: fc_ * E * R + (e_ + 1) * R],
                            aw[slot_][:, :], start=False, stop=False,
                            skip_group_check=True, tile_position=(0, 32 * jj_))
                        self.c[f"_awr{slot_}"] = self.inc(mm_, "pe")
                    r2w_pending.clear()

                for e in range(E):
                    pA = {}
                    pB = {}
                    for ch in (0, 1):
                        A, Bk = lane[ch]
                        wsl = ch * 2 + fcp % 2
                        b1i, b3i = e, E + e
                        jt1, ro1 = b1i // 3, 32 * (b1i % 3)
                        jt3, ro3 = b3i // 3, 32 * (b3i % 3)
                        if e > 0:
                            self.wait(te, "sc", self.c[f"_silu{ch}"])
                        mm = te.matmul(banks[A][:, 0:TC],
                                  b13t[wsl][ro1:ro1 + 32,
                                            b1i * 128: b1i * 128 + 128],
                                  r13[ro1:ro1 + 32, jt1 * TC: (jt1 + 1) * TC],
                                  start=False, stop=False, skip_group_check=True)
                        pA[ch] = self.inc(mm, "pe")
                        if e > 0:
                            self.wait(te, "dve", self.c[f"_h3w{ch}"])
                        mm = te.matmul(banks[Bk][:, 0:TC],
                                  b13t[wsl][ro3:ro3 + 32,
                                            (E + e) * 128: (E + e) * 128 + 128],
                                  r13[ro3:ro3 + 32, jt3 * TC: (jt3 + 1) * TC],
                                  start=False, stop=False, skip_group_check=True)
                        pB[ch] = self.inc(mm, "pe")
                        if e == E - 1:
                            self.c[f"_w13u{wsl}"] = self.cur("pe")
                    emit_r2w()
                    for ch in (0, 1):
                        srA = sring[ch * 2 + e % 2]
                        self.wait(se, "pe", pA[ch])
                        if f"_awtt{ch}{e % 2}" in self.c:
                            self.wait(se, "po", self.c[f"_awtt{ch}{e % 2}"])
                        self.sw()
                        self.c[f"_silu{ch}"] = self.inc(
                            se.activation(srA[:, :], banks[lane[ch][0]][:, 0:TC],
                                          AF.Silu), "sc")
                    for ch in (0, 1):
                        h3A = h3ring[ch * 2 + e % 2]
                        self.wait(ve, "pe", pB[ch])
                        if f"_aw{ch}" in self.c:
                            self.wait(ve, "po", self.c[f"_aw{ch}"])
                        self.dw()
                        self.c[f"_h3w{ch}"] = self.inc(
                            ve.tensor_tensor(h3A[:, :], banks[lane[ch][1]][:, 0:TC],
                                             wb_sb[:, e * TC:(e + 1) * TC],
                                             op=ALU.mult), "dve")
                    for ch in (0, 1):
                        fc = 2 * fcp + ch
                        srA = sring[ch * 2 + e % 2]
                        h3A = h3ring[ch * 2 + e % 2]
                        awslot = ch * 3 + e % 3
                        if f"_awr{awslot}" in self.c:
                            self.wait(ge, "pe", self.c[f"_awr{awslot}"])
                        self.wait(ge, "sc", self.c[f"_silu{ch}"])
                        self.wait(ge, "dve", self.c[f"_h3w{ch}"])
                        self.gw()
                        self.c[f"_aw{ch}"] = self.inc(
                            ge.tensor_tensor(aw[awslot][:, :], srA[:, :], h3A[:, :],
                                             op=ALU.mult), "po")
                        self.c[f"_awd{awslot}"] = self.c[f"_aw{ch}"]
                        self.c[f"_awtt{ch}{e % 2}"] = self.c[f"_aw{ch}"]
                        r2w_pending.append((fc, e, awslot))
                        tb = ch * 2
                        aprev = aw[ch * 3 + (e - 1) % 3]
                        if e == 1 or e == 5:
                            self.inc(ge.tensor_tensor(tsum[tb + (e // 4)][:, :],
                                                      aprev[:, :], aw[awslot][:, :],
                                                      op=ALU.add), "po")
                        if e == 3:
                            self.inc(ge.tensor_tensor(tsum[tb + 1][:, :], aprev[:, :],
                                                      aw[awslot][:, :], op=ALU.add), "po")
                            self.gw()
                            self.inc(ge.tensor_tensor(tsum[tb][:, :], tsum[tb][:, :],
                                                      tsum[tb + 1][:, :], op=ALU.add), "po")
                        if e == 7:
                            wv = wact[:, fc * TC:(fc + 1) * TC]
                            self.inc(ge.tensor_tensor(wv, aprev[:, :], aw[awslot][:, :],
                                                      op=ALU.add), "po")
                            self.gw()
                            self.inc(ge.tensor_tensor(wv, wv, tsum[tb + 1][:, :],
                                                      op=ALU.add), "po")
                            self.gw()
                            self.c[f"_wact{fc}"] = self.inc(
                                ge.tensor_tensor(wv, wv, tsum[tb][:, :], op=ALU.add),
                                "po")
                            self.c[f"_laneB{ch}"] = self.c[f"_h3w{ch}"]
                            self.c[f"_laneA{ch}"] = self.c[f"_silu{ch}"]
                        if e % 2 == 1:
                            self.c[f"_awp{awslot}"] = self.cur("po")
                            self.c[f"_awp{ch * 3 + (e - 1) % 3}"] = self.cur("po")
                emit_r2w()
                emit_qkv_unit()

            self.wait(ve, "pe")
            for kb in range(3):
                self.dw()
                self.c[f"_r2wevac{kb}"] = self.inc(
                    ve.tensor_scalar(r2w_sb[kb][:, :], banks[r2w_banks[kb]][:, 0:TC],
                                     2.0, None, op0=ALU.mult), "dve")
            self.c["_wb6"] = self.c["_r2wevac2"]

            if self.taps:
                self.wait(sy, "dve")
                for kb in range(3):
                    self.inc(sy.dma_start(self.tap_r2w.ap()[kb], r2w_sb[kb][:, :]), "tapst", 16)

            NTS = TC // 128
            self.wait(te, "sc")

            def issue_w2(gi):
                if gi >= (D // CD) * NFC:
                    return
                dcb2, fc2 = divmod(gi, NFC)
                sl2 = gi % 3
                key = f"_w2u{sl2}"
                if gi == 0:
                    self.wait(sy, "wcw2", 16)
                if key in self.c:
                    self.wait(sy, "pe", self.c[key])
                self.inc(sy.dma_start(
                    w2t[sl2][:, :],
                    self.w2_bf.ap()[fc2 * 128:(fc2 + 1) * 128, dcb2 * CD:(dcb2 + 1) * CD]),
                    f"w2l{sl2}", 16)
                self.c[f"_w2c{sl2}"] = self.cur(f"w2l{sl2}")

            issue_w2(0)
            issue_w2(1)
            for dcb in range(D // CD):
                for fc in range(NFC):
                    gi = dcb * NFC + fc
                    wslot = gi % 3
                    issue_w2(gi + 2)
                    self.wait(te, f"w2l{wslot}", self.c[f"_w2c{wslot}"])
                    self.wait(te, "po", self.c[f"_wact{fc}"])
                    if fc == 0 and dcb > 0:
                        self.wait(te, "dve", self.c["_devac"])
                    for ts_ in range(NTS):
                        mm = te.matmul(banks[ts_][:, 0:CD],
                                  wact[:, fc * TC + ts_ * 128: fc * TC + (ts_ + 1) * 128],
                                  w2t[wslot][:, :], start=(fc == 0), stop=False)
                    self.c[f"_w2u{wslot}"] = self.inc(mm, "pe")
                    if fc % 8 == 3:
                        emit_qkv_unit()
                self.wait(te, "dve", self.c["_r2wevac2"])
                for ts_ in range(NTS):
                    for kb in range(3):
                        mm = te.matmul(banks[ts_][:, 0:CD],
                                  r2w_sb[kb][:, ts_ * 128:(ts_ + 1) * 128],
                                  self.b2p_sb[:, kb * D + dcb * CD: kb * D + (dcb + 1) * CD],
                                  start=False, stop=(kb == 2))
                pev = self.inc(mm, "pe")
                for ts_ in range(NTS):
                    sl2 = ts_ % 2
                    self.wait(ve, "pe", pev)
                    stk = f"_dst{sl2}"
                    if stk in self.c:
                        self.wait(ve, f"dstg{sl2}", self.c[stk])
                    self.dw()
                    self.inc(ve.tensor_copy(dstage[sl2][:, :], banks[ts_][:, 0:CD]), "dve")
                    self.wait(sy, "dve")
                    self.inc(sy.dma_start(
                        self.ffn_hbm.ap()[t0 + ts_ * 128: t0 + (ts_ + 1) * 128,
                                          dcb * CD:(dcb + 1) * CD],
                        dstage[sl2][:, :]), f"dstg{sl2}", 16)
                    self.c[stk] = self.cur(f"dstg{sl2}")
                self.c["_devac"] = self.cur("dve")
            if self.taps:
                self.wait(sy, "po", self.cur("po"))
                self.inc(sy.dma_start(self.tap_wact.ap()[tc], wact[:, :]), "tapst", 16)
            self.c["_laneA0"] = self.cur("sc")
            self.c["_laneB0"] = self.cur("dve")
            self.c["_laneA1"] = self.cur("sc")
            self.c["_laneB1"] = self.cur("dve")
            self.c["_w13u0"] = self.cur("pe")
            self.c["_w13u2"] = self.cur("pe")
            self.c["_r2wfree"] = self.cur("pe")
        while emit_qkv_unit():
            pass

        if "a" in self.phases and "x" in self.aparts:
            HPG_, HPT_, DH_, NTT_, QC_ = (self.cfg["HPG"], self.HPT, self.DH,
                                          self.NTT, self.QC)
            for i in range(4):
                if f"qkst{i}" in self.c:
                    self.wait(ge, f"qkst{i}")
            if "_khgfree0" in self.c:
                self.wait(ge, "pe", self.c["_khgfree0"])
            self.inc(ge.dma_start(
                self.khg0[:, :].rearrange("p (c n) -> p c n", c=HPG_ // HPT_),
                self.k_hbm.ap().rearrange("(c p) n -> p c n", p=128)[:, 0:HPG_ // HPT_, :]),
                "khgp", 16)
            self.c["_khgc0"] = self.cur("khgp")
            self._khgsem[0] = "khgp"
            if "_vhgfree" in self.c:
                self.wait(ge, "pe", self.c["_vhgfree"])
            for hh2 in range(HPG_):
                vdst = AP(self.vhg[:, :].tensor, self.vhg[:, :].offset + hh2 * (DH_ + 1),
                          [list(self.vhg[:, :].ap[0]), [HPG_ * (DH_ + 1), NTT_], [1, DH_]])
                vsrc = self.v_hbm.ap().rearrange("(a p) n -> p a n", p=128)[
                    :, :, hh2 * DH_:(hh2 + 1) * DH_]
                self.inc(ge.dma_start(vdst, vsrc), "vhgp", 16)
            self.c["_vhgc"] = self.cur("vhgp")
            self._vhgsem = "vhgp"
            self._pf_attn = True

        if self.taps:
            self.wait(sy, "dve")
            self.inc(sy.dma_start(self.tap_wT.ap(), self.w_T[:, :]), "tapst", 16)
            self.inc(sy.dma_start(self.tap_r13.ap(), r13[:, :]), "tapst", 16)

    def phase_attn(self):
        nc, te, ve, se, sy, ge = self.nc, self.te, self.ve, self.se, self.sy, self.ge
        S, D, H, DH, E = self.S, self.D, self.H, self.DH, self.E
        NDC, NTT, NMQ, HPT = self.NDC, self.NTT, self.NMQ, self.HPT
        QC, NQC, NKC, HPG, NHG = self.QC, self.NQC, self.NKC, self.HPG, self.NHG
        WBR, LTP = self.WBR, self.LTP
        banks, seq = self.banks, self.seq
        HDx = self.HDx
        sq = f"s{seq}"
        C4 = min(512, S)
        NT4 = S // C4
        CD = min(512, D)

        adm = self.sb("adm", [128, NMQ * S], BF16)
        lrec_f = [self.sb(f"lrec_f{i}", [1, QC], F32) for i in range(HPG)]
        lrec = [self.sb(f"lrec{i}", [1, QC], BF16) for i in range(HPG)]
        lbc = [self.sb(f"lbc{i}", [64, QC], BF16) for i in range(HPG)]
        smb = self.sb("smb", [128, 16], BF16)
        nmrow = self.sb("nmrow", [1, H], BF16)
        owT_sb = self.sb("owT_sb", [128, NMQ * D], BF16)
        self.wait(sy, "wcow", 16)
        self.inc(sy.dma_start(owT_sb[:, :].rearrange("p (c n) -> p c n", c=NMQ),
                              self.owT_bf.ap().rearrange("(c p) n -> p c n", p=128)),
                 "owl", 16)
        xres = [self.sb(f"xres{i}", [128, CD]) for i in range(2)]
        fres = [self.sb(f"fres{i}", [128, CD], BF16) for i in range(2)]
        ores = [self.sb(f"ores{i}", [128, CD]) for i in range(2)]
        ost = [self.sb(f"ost{i}", [128, CD]) for i in range(2)]
        attn_outer_st = self.st

        if "b" not in self.aparts:
            self.c["_epi_done"] = self.cur("dve")
            self.c["_skipflash"] = 1
        sm = self.small
        if "b" in self.aparts:
            self.wait(ve, "sc")
            self.dw()
            self.inc(ve.tensor_tensor(sm[0:H, 0:1], self.qm2[0:H, 0:1],
                                      self.qm2[0:H, 1:2], op=ALU.mult), "dve")
            self.wait(se, "dve")
            self.sw()
            self.inc(se.activation(sm[0:H, 1:2], sm[0:H, 0:1], AF.Ln,
                                   bias=self.eps[0:H, 2:3], scale=1.0), "sc")
            self.sw()
            self.inc(se.activation(sm[0:H, 1:2], sm[0:H, 1:2], AF.Exp,
                                   bias=self.eps[0:H, 3:4], scale=0.5), "sc")
            self.wait(ve, "sc")
            self.dw()
            self.inc(ve.tensor_scalar(smb[0:H, 0:1], sm[0:H, 1:2], -1.0, None,
                                      op0=ALU.mult), "dve")
            bkt = banks[0][:, :].bitcast(BF16)
            self.wait(te, "dve")
            mm = te.transpose(bkt[0:16, 0:128], smb[:, :], self.id_sb[:, :])
            self.inc(mm, "pe")
            self.wait(ve, "pe")
            self.dw()
            self.inc(ve.tensor_copy(nmrow[:, :], bkt[0:1, 0:H]), "dve")
            self.wait(te, "dve")
            mm = te.matmul(banks[1][:, 0:H], self.ones1[:, :], nmrow[:, :], start=True, stop=True)
            self.inc(mm, "pe")
            self.wait(ve, "pe")
            self.dw()
            self.inc(ve.tensor_copy(self.biasmat[:, :], banks[1][:, 0:H]), "dve")
            self.c["_epi_done"] = self.cur("dve")

        fl_scope = contextlib.ExitStack()
        self.st = fl_scope
        vhg = self.vhg
        khg = [self.khg0, self.sb("khg1", [128, (HPG // HPT) * S], BF16)]
        qsl = [self.sb(f"qsl{i}", [128, (HPG // HPT) * S], BF16) for i in range(2)]

        def emit_qsl_load(hg_i):
            if hg_i >= NHG:
                return
            qs_ = hg_i % 2
            h0_ = hg_i * HPG
            kfk = f"_qslfree{qs_}"
            if kfk in self.c:
                self.wait(sy, "pe", self.c[kfk])
            self.inc(sy.dma_start(
                qsl[qs_][:, :].rearrange("p (c n) -> p c n", c=HPG // HPT),
                self.q_hbm.ap().rearrange("(c p) n -> p c n", p=128)[
                    :, h0_ // HPT: (h0_ + HPG) // HPT, :]),
                f"qsll{qs_}", 16)
            self.c[f"_qslc{qs_}"] = self.cur(f"qsll{qs_}")

        def emit_hg_loads(hg_i):
            if hg_i >= NHG:
                return
            sl_ = hg_i % 2
            h0_ = hg_i * HPG
            kfk = f"_khgfree{sl_}"
            if kfk in self.c:
                self.wait(sy, "pe", self.c[kfk])
            self.inc(sy.dma_start(
                khg[sl_][:, :].rearrange("p (c n) -> p c n", c=HPG // HPT),
                self.k_hbm.ap().rearrange("(c p) n -> p c n", p=128)[
                    :, h0_ // HPT: (h0_ + HPG) // HPT, :]),
                f"khgl{sl_}", 16)
            self.c[f"_khgc{sl_}"] = self.cur(f"khgl{sl_}")
            self._khgsem[sl_] = f"khgl{sl_}"

        def emit_vhg_loads(hg_i):
            if hg_i >= NHG:
                return
            h0_ = hg_i * HPG
            self.wait(sy, "pe", self.c["_vhgfree"])
            for hh2 in range(HPG):
                vdst = AP(vhg[:, :].tensor, vhg[:, :].offset + hh2 * (DH + 1),
                          [list(vhg[:, :].ap[0]), [HPG * (DH + 1), NTT], [1, DH]])
                vsrc = self.v_hbm.ap().rearrange("(a p) n -> p a n", p=128)[
                    :, :, (h0_ + hh2) * DH:(h0_ + hh2 + 1) * DH]
                self.inc(sy.dma_start(vdst, vsrc), "vhgl", 16)
            self.c["_vhgc"] = self.cur("vhgl")
            self._vhgsem = "vhgl"

        brw = [self.sb(f"brw{i}", [128, WBR], BF16) for i in range(HPG)]
        ptil = [self.sb(f"ptil{i}", [128, QC], BF16) for i in range(4)]
        self.st = attn_outer_st
        self.wait(ge, "ebr", 16)
        self.wait(ge, "pe")
        self.wait(ge, "dve")
        for nm in ("qkst0", "qkst1", "xnst0", "xnst1"):
            if nm in self.c:
                self.wait(ge, nm, self.c[nm])
                self.wait(sy, nm, self.c[nm])
        self.wait(sy, "pe")
        self.wait(sy, "dve")
        self.wait(sy, "sc")
        for i in range(4):
            if f"qkst{i}" in self.c:
                self.wait(sy, f"qkst{i}")
        if "x" in self.aparts and NHG > 0:
            emit_qsl_load(0)
        for hg in range(NHG if "x" in self.aparts else 0):
            h0 = hg * HPG
            sl = hg % 2
            qs = hg % 2
            if hg == 0 and not self._pf_attn:
                emit_hg_loads(0)
            emit_qsl_load(hg + 1)
            if hg > 0 and "_brwfree" in self.c:
                self.wait(ge, "dve", self.c["_brwfree"])
            for hh in range(HPG):
                h = h0 + hh
                win = AP(self.ebr_hbm.ap().tensor, 0, [[1, H * LTP], [1, 1]])
                self.inc(ge.indirect_dma_start(
                    brw[hh][:, :], None, win,
                    IndirectOffsetOnAxis(self.gidx_sb[:, :], 0),
                    element_offset=h * LTP), f"brwg{hh}", 16)
                self.c[f"_brwc{hh}"] = self.cur(f"brwg{hh}")
            for qc in range(NQC):
                q0 = qc * QC
                if qc == 0:
                    emit_hg_loads(hg + 1)
                self.wait(te, f"qsll{qs}", self.c[f"_qslc{qs}"])
                self.wait(te, self._khgsem[sl], self.c[f"_khgc{sl}"])
                self.wait(te, self._vhgsem, self.c["_vhgc"])
                self.wait(te, "dve", self.c["_epi_done"])
                self.wait(te, "dve", self.c["_vones"])
                tiles = [(kc, hh) for hh in range(HPG) for kc in range(NKC)]
                LA = 2
                for i in range(len(tiles) + LA):
                    if i < len(tiles):
                        kc, hh = tiles[i]
                        h = h0 + hh
                        sbk = i % 4
                        bk = banks[sbk]
                        key = f"_exp{sbk}"
                        if f"_pext{sbk}" in self.c:
                            self.wait(te, "dve", self.c[f"_pext{sbk}"])
                        if kc == 0 and "_devac" in self.c:
                            self.wait(te, "dve", self.c["_devac"])
                        if self.taps and "_tapw" in self.c:
                            self.wait(se, "tapst", self.c["_tapw"])
                            self.wait(ve, "tapst", self.c["_tapw"])
                        jt = hh // HPT
                        roff = 64 * (hh % HPT)
                        mm = te.matmul(bk[:, 0:QC],
                                  khg[sl][roff:roff + DH, jt * S + kc * 128: jt * S + (kc + 1) * 128],
                                  qsl[qs][roff:roff + DH, jt * S + q0: jt * S + q0 + QC],
                                  start=True, stop=True)
                        pmm = self.inc(mm, "pe")
                        self.wait(se, "pe", pmm)
                        if f"_pext{sbk}" in self.c:
                            self.wait(se, "dve", self.c[f"_pext{sbk}"])
                        self.sw()
                        self.c[key] = self.inc(
                            se.activation(bk[:, 0:QC], bk[:, 0:QC], AF.Exp,
                                          bias=self.biasmat[:, h:h + 1], scale=1.0),
                            "sc")
                        self.wait(ve, "sc", self.c[key])
                        c0 = (S - 128) - kc * 128 + q0
                        if kc == 0:
                            self.wait(ve, f"brwg{hh}", self.c[f"_brwc{hh}"])
                        if f"_ptu{sbk}" in self.c:
                            self.wait(ve, "pe", self.c[f"_ptu{sbk}"])
                        self.dw()
                        self.c[f"_pext{sbk}"] = self.inc(
                            ve.tensor_tensor(ptil[sbk][:, :], bk[:, 0:QC],
                                             brw[hh][:, c0:c0 + QC], op=ALU.mult),
                            "dve")
                    if i >= LA:
                        kc, hh = tiles[i - LA]
                        sbk = (i - LA) % 4
                        avb = banks[4 + hh]
                        self.wait(te, "dve", self.c[f"_pext{sbk}"])
                        mm = te.matmul(avb[0:DH + 1, 0:QC],
                                  vhg[:, (kc * HPG + hh) * (DH + 1): (kc * HPG + hh + 1) * (DH + 1)],
                                  ptil[sbk][:, :], start=(kc == 0), stop=(kc == NKC - 1))
                        self.c[f"_ptu{sbk}"] = self.inc(mm, "pe")
                if qc == NQC - 1:
                    self.c[f"_qslfree{qs}"] = self.cur("pe")
                if self.taps and seq == 1 and hg == 0 and qc == 0:
                    self.wait(ve, "pe")
                    self.dw()
                    self.inc(ve.tensor_copy(qkv_stage[0][0:65, 0:QC], banks[4][0:65, 0:QC]), "dve")
                    self.wait(sy, "dve")
                    self.inc(sy.dma_start(self.tap_av.ap(), qkv_stage[0][0:65, 0:QC]), "tapst", 16)
                for hh in range(HPG):
                    avb = banks[4 + hh]
                    self.wait(se, "pe")
                    if "_lbccp" in self.c:
                        self.wait(se, "dve", self.c["_lbccp"])
                    self.sw()
                    self.inc(se.activation(lrec_f[hh][:, :], avb[64:65, 0:QC], AF.Ln,
                                           bias=0.0, scale=1.0), "sc")
                    self.sw()
                    self.inc(se.activation(lrec[hh][:, :], lrec_f[hh][:, :], AF.Exp,
                                           bias=0.0, scale=-1.0), "sc")
                    self.c[f"_lr{hh}"] = self.cur("sc")
                for hh in range(HPG):
                    bb = banks[hh % 4]
                    self.wait(te, "sc", self.c[f"_lr{hh}"])
                    self.wait(te, "dve", self.c[f"_pext{hh % 4}"])
                    if hh == 0 and "_lbccp" in self.c:
                        self.wait(te, "dve", self.c["_lbccp"])
                    mm = te.matmul(bb[0:64, 0:QC], self.ones1[:, 0:64], lrec[hh][:, :],
                              start=True, stop=True)
                    self.c[f"_lbcmm{hh}"] = self.inc(mm, "pe")
                for hh in range(HPG):
                    h = h0 + hh
                    avb = banks[4 + hh]
                    bb = banks[hh % 4]
                    self.wait(ve, "pe", self.c[f"_lbcmm{hh}"])
                    self.dw()
                    self.inc(ve.tensor_copy(lbc[hh][:, :], bb[0:64, 0:QC]), "dve")
                    dst_r = 64 * (h % HPT)
                    dst_t = h // HPT
                    self.dw()
                    self.inc(ve.tensor_tensor(
                        adm[dst_r:dst_r + 64, dst_t * S + q0: dst_t * S + q0 + QC],
                        avb[0:64, 0:QC], lbc[hh][:, :], op=ALU.mult), "dve")
                self.c["_lbccp"] = self.cur("dve")
                self.c["_epi_done"] = self.cur("dve")
                if qc == NQC - 1:
                    self.c[f"_khgfree{sl}"] = self.cur("pe")
                    self.c["_vhgfree"] = self.cur("pe")
                    self.c["_brwfree"] = self.cur("dve")
                    emit_vhg_loads(hg + 1)
                for i4 in range(4):
                    self.c[f"_exp{i4}"] = self.cur("sc")
            self.c["_hg_done"] = self.cur("dve")
        self.c["_flash_free"] = self.cur("pe")
        self._pf_attn = False

        if self.taps:
            self.wait(sy, "dve")
            self.inc(sy.dma_start(self.tap_adm.ap(), adm[:, :]), "tapst", 16)
        self.wait(te, "sc")
        self.wait(te, "dve")
        fl_scope.close()
        self.wait(sy, "pe", self.c["_flash_free"])
        self.wait(ge, "pe", self.c["_flash_free"])

        if "l" in self.phases and seq + 1 < self.SEQS:
            saved = self.seq
            self.seq = seq + 1
            self.phase_ln()
            self.seq = saved
        self.wait(sy, "dve")
        for hh_ in range(HPG):
            if f"brwg{hh_}" in self.c:
                self.wait(sy, f"brwg{hh_}")
        for ts_ in range(self.TC // 128):
            if f"dstg{ts_}" in self.c:
                self.wait(sy, f"dstg{ts_}")
        for tt in range(NTT if "o" in self.aparts else 0):
            for dcb in range(D // CD):
                idx = tt * (D // CD) + dcb
                bk = banks[idx % 2]
                slot = idx % 2
                key = f"_ors{slot}"
                if key in self.c:
                    self.wait(ge, f"sto{slot}", self.c[key])
                self.inc(ge.dma_start(xres[slot][:, :],
                                      self.x_sh[seq, tt * 128:(tt + 1) * 128,
                                                dcb * CD:(dcb + 1) * CD]),
                         f"xrg{slot}", 16)
                self.inc(ge.dma_start(fres[slot][:, :],
                                      self.ffn_hbm.ap()[tt * 128:(tt + 1) * 128,
                                                        dcb * CD:(dcb + 1) * CD]),
                         f"xrg{slot}", 16)
                okey = f"_opev{slot}"
                if okey in self.c:
                    self.wait(te, "dve", self.c[okey])
                elif idx < 2:
                    self.wait(te, "dve")
                    self.wait(te, "sc")
                if idx == 0:
                    self.wait(te, "owl", self.cur("owl"))
                for j in range(NMQ):
                    mm = te.matmul(bk[:, 0:CD], adm[:, j * S + tt * 128: j * S + (tt + 1) * 128],
                              owT_sb[:, j * D + dcb * CD: j * D + (dcb + 1) * CD],
                              start=(j == 0), stop=(j == NMQ - 1))
                self.inc(mm, "pe")
                self.wait(ve, "pe")
                self.wait(ve, f"xrg{slot}")
                self.dw()
                self.inc(ve.scalar_tensor_tensor(ores[slot][:, :], bk[:, 0:CD], 0.0,
                                                 xres[slot][:, :], op0=ALU.bypass,
                                                 op1=ALU.add), "dve")
                self.c[okey] = self.cur("dve")
                self.dw()
                self.inc(ve.tensor_tensor(ost[slot][:, :], ores[slot][:, :],
                                          fres[slot][:, :], op=ALU.add), "dve")
                self.wait(sy, "dve")
                self.inc(sy.dma_start(
                    self.out_sh[seq, tt * 128:(tt + 1) * 128, dcb * CD:(dcb + 1) * CD],
                    ost[slot][:, :]), f"sto{slot}", 16)
                self.c[key] = self.cur(f"sto{slot}")


def kernel(L0_attn_x, L0_attn_emb_weight, L0_attn_qkv_w, L0_attn_o_w,
           L0_ffn_norm_w, L0_ffn_w1, L0_ffn_w3, L0_ffn_w2, L0_ffn_gate_w,
           L0_ffn_a1, L0_ffn_b1, L0_ffn_a3, L0_ffn_b3, L0_ffn_a2, L0_ffn_b2):
    cfg = CFG
    weights = dict(qkv_w=L0_attn_qkv_w, o_w=L0_attn_o_w, emb_weight=L0_attn_emb_weight,
                   norm_w=L0_ffn_norm_w, w1=L0_ffn_w1, w3=L0_ffn_w3, w2=L0_ffn_w2,
                   gate_w=L0_ffn_gate_w, a1=L0_ffn_a1, b1=L0_ffn_b1, a3=L0_ffn_a3,
                   b3=L0_ffn_b3, a2=L0_ffn_a2, b2=L0_ffn_b2)
    weights = {k: np.asarray(v, np.float32) for k, v in weights.items()}
    hp = host_prep(cfg, weights)
    x = np.asarray(L0_attn_x, np.float32)
    B = x.shape[0]
    per = B // N_CORES

    nc = B_build_cached(cfg)
    in_maps = []
    for c in range(N_CORES):
        m = dict(hp)
        m["x_sh"] = np.ascontiguousarray(x[c * per:(c + 1) * per])
        in_maps.append(m)
    res = run_bass_kernel_spmd(nc, in_maps, core_ids=list(range(N_CORES)))
    out = np.concatenate([r["out_sh"] for r in res.results], axis=0)
    return out


_cache = {}


def B_build_cached(cfg):
    key = tuple(sorted(cfg.items()))
    if key not in _cache:
        _cache[key] = B(cfg).build()
    return _cache[key]


# version 39
# speedup vs baseline: 1.2028x; 1.1471x over previous
import contextlib
import math

import numpy as np
import ml_dtypes

import concourse.bass as bass
import concourse.mybir as mybir
from concourse.bass import IndirectOffsetOnAxis
from concourse.bass_utils import run_bass_kernel_spmd
from bass_rust import AP

F32 = mybir.dt.float32
BF16 = mybir.dt.bfloat16
I32 = mybir.dt.int32
AF = mybir.ActivationFunctionType
ALU = mybir.AluOpType
AX = mybir.AxisListType
BF = ml_dtypes.bfloat16

N_CORES = 8


def make_cfg(S=2048, D=1024, F=4096, H=16, DH=64, E=8, R=16, SEQS=2,
             TC=512, QC=512, HPG=4):
    c = dict(S=S, D=D, F=F, H=H, DH=DH, E=E, R=R, SEQS=SEQS, NB=32, MAXD=128,
             TC=TC, QC=QC, HPG=HPG)
    c["NDC"] = D // 128
    c["NFC"] = F // 128
    c["NTT"] = S // 128
    c["NTC"] = S // TC
    c["NQC"] = S // QC
    c["NKC"] = S // 128
    c["NHG"] = H // HPG
    c["LT"] = 2 * S - 1
    c["LTP"] = ((2 * S - 1 + 511) // 512) * 512
    c["WBR"] = 2 * S - 128
    c["NMQ"] = H * DH // 128
    c["HPT"] = 128 // DH
    assert TC % 128 == 0 and QC % 128 == 0 and QC <= 512
    assert H % HPG == 0 and HPG % c["HPT"] == 0
    return c


CFG = make_cfg()


def rel_bucket_rev(S, NB=32, MAXD=128):
    half = NB // 2
    max_exact = half // 2
    coeff = (half - max_exact) / math.log(MAXD / max_exact)
    delta = np.arange(S - 1, -S, -1)
    arel = np.abs(delta)
    base = np.where(delta > 0, half, 0)
    distant = max_exact + (np.log(np.maximum(arel, 1).astype(np.float32) / max_exact)
                           * coeff).astype(np.int32)
    distant = np.minimum(distant, half - 1)
    return (base + np.where(arel < max_exact, arel, distant)).astype(np.int64)


def host_prep(cfg, w):
    S, D, F, H, DH, E, R, NB = (cfg[k] for k in "S D F H DH E R NB".split())
    out = {}
    out["qkv_wT"] = np.ascontiguousarray(w["qkv_w"].T)
    out["o_wT"] = np.ascontiguousarray(w["o_w"].T)
    out["w1T"] = np.ascontiguousarray(w["w1"].T)
    out["w3T"] = np.ascontiguousarray(w["w3"].T)
    out["w2T"] = np.ascontiguousarray(w["w2"].T)
    out["gate_wT"] = np.ascontiguousarray(w["gate_w"].T)
    out["emb"] = np.ascontiguousarray(w["emb_weight"])
    out["norm_w"] = np.ascontiguousarray(w["norm_w"].reshape(1, D))

    a1, a3 = w["a1"], w["a3"]
    a13p = np.zeros((D, 768), np.float32)
    a13n = np.zeros((D, 768), np.float32)
    for wi, a in ((0, a1), (1, a3)):
        for e in range(E):
            b = wi * E + e
            base = (b // 3) * 128 + 32 * (b % 3)
            a13p[:, base:base + R] = a[e].T
            if e > 0:
                a13n[:, base + 16:base + 16 + R] = a[e - 1].T
    out["a13P_pos"], out["a13P_neg"] = a13p, a13n

    b1, b3 = w["b1"], w["b3"]
    b13 = np.zeros((2, E, 32, F), np.float32)
    for wi, b in ((0, b1), (1, b3)):
        for e in range(E):
            b13[wi, e, 0:R] = b[e].T
            if e > 0:
                b13[wi, e, 16:16 + R] = b[e - 1].T
    out["b13P"] = np.ascontiguousarray(b13.reshape(2 * E * 32, F))

    out["a2T"] = np.ascontiguousarray(w["a2"].reshape(E * R, F).T)

    b2 = w["b2"]
    b2p = np.zeros((384, D), np.float32)
    for e in range(E):
        k, j = divmod(e, 3)
        b2p[128 * k + 32 * j: 128 * k + 32 * j + R] = b2[e].T
    out["b2T_pad"] = b2p

    LT, LTP = cfg["LT"], cfg["LTP"]
    bt = rel_bucket_rev(S, NB, cfg["MAXD"])
    ohr = np.zeros((NB, LTP), BF)
    ohr[bt, np.arange(LT)] = BF(1.0)
    out["ohr"] = ohr
    out["gidx"] = (127 - np.arange(128, dtype=np.int32)).reshape(128, 1)
    out["id_bf"] = np.eye(128, dtype=BF)
    NMQ = cfg["NMQ"]
    sel = np.zeros((128, NMQ * 16), BF)
    for mc in range(NMQ):
        sel[0:64, mc * 16 + 2 * mc] = BF(1.0)
        sel[64:128, mc * 16 + 2 * mc + 1] = BF(1.0)
    out["sels"] = sel
    sel8 = np.zeros((E, E * 128), BF)
    for e in range(E):
        sel8[e, e * 128:(e + 1) * 128] = BF(1.0)
    out["sel8"] = sel8
    return out


class B:
    taps = False
    _pf_attn = False
    _vhgsem = "vhgl"
    phases = "lfa"
    aparts = "qvbxo"

    def __init__(self, cfg, dbg=False):
        self.cfg = cfg
        self.nc = bass.Bass("TRN2", target_bir_lowering=False, debug=dbg)
        for k, v in cfg.items():
            setattr(self, k, v)
        self.HDx = self.H * self.DH
        self.c = {}
        self.sems = {}
        self.st = None
        self._khgsem = {0: "khgl", 1: "khgl"}
        self._vhgsem = "vhgl"

    def sem(self, name):
        if name not in self.sems:
            self.sems[name] = self.st.enter_context(self.nc.semaphore(name))
            self.c[name] = 0
        return self.sems[name]

    def inc(self, instr, name, n=1):
        instr.then_inc(self.sem(name), n)
        self.c[name] += n
        return self.c[name]

    def cur(self, name):
        self.sem(name)
        return self.c[name]

    def wait(self, eng, name, value=None):
        if value is None:
            value = self.c[name]
        assert value <= self.c.get(name, 0), (name, value)
        if value > 0:
            eng.wait_ge(self.sems[name], value)

    def sb(self, n, sh, dt=F32):
        return self.st.enter_context(
            self.nc.sbuf_tensor(n, sh, dt, allow_name_mangling=True))

    def psb(self, n, sh, dt=F32):
        return self.st.enter_context(self.nc.psum_tensor(n, sh, dt))

    def barrier(self):
        self.drain_stores()
        self.nc.all_engine_barrier()
        self.c = {k: v for k, v in self.c.items() if not k.startswith("_")}

    def soft_barrier(self):
        self.drain_stores()
        for q in (self.sy, self.ge):
            for s in ("pe", "dve", "sc", "po"):
                if self.c.get(s, 0) > 0:
                    q.wait_ge(self.sems[s], self.c[s])

    def drain_stores(self):
        for name in list(self.c):
            if name.startswith(("sto", "dstg", "xnst", "qkst", "w13l", "b13l",
                                "w2l", "qkvwl", "vwl", "khgl", "vhgl", "qsll",
                                "xrl", "brwg", "xld", "tapst")):
                if self.c[name] > 0:
                    self.wait(self.sy, name, self.c[name])

    def dw(self):
        if self.c.get("dve", 0) > 0:
            self.ve.wait_ge(self.sems["dve"], self.c["dve"])

    def sw(self):
        if self.c.get("sc", 0) > 0:
            self.se.wait_ge(self.sems["sc"], self.c["sc"])

    def gw(self):
        if self.c.get("po", 0) > 0:
            self.ge.wait_ge(self.sems["po"], self.c["po"])

    def build(self):
        nc, cfg = self.nc, self.cfg
        S, D, F, H, DH, E, R, NB = (cfg[k] for k in "S D F H DH E R NB".split())
        NDC, NFC, NTT, TC, NTC = (cfg[k] for k in "NDC NFC NTT TC NTC".split())
        LTP, NMQ, SEQS = cfg["LTP"], cfg["NMQ"], cfg["SEQS"]
        HDx = self.HDx

        din = lambda n, sh, dt=F32: nc.dram_tensor(n, sh, dt, kind="ExternalInput").ap()
        self.x_sh = din("x_sh", [SEQS, S, D])
        self.qkv_wT = din("qkv_wT", [D, 3 * HDx])
        self.o_wT = din("o_wT", [self.HDx, D])
        self.w1T = din("w1T", [D, F])
        self.w3T = din("w3T", [D, F])
        self.w2T = din("w2T", [F, D])
        self.gate_wT = din("gate_wT", [D, E])
        self.emb = din("emb", [NB, H])
        self.norm_w = din("norm_w", [1, D])
        self.a13P_pos = din("a13P_pos", [D, 768])
        self.a13P_neg = din("a13P_neg", [D, 768])
        self.b13P = din("b13P", [2 * E * 32, F])
        self.a2T = din("a2T", [F, E * R])
        self.b2T_pad = din("b2T_pad", [384, D])
        self.ohr = din("ohr", [NB, LTP], BF16)
        self.gidx = din("gidx", [128, 1], I32)
        self.id_bf = din("id_bf", [128, 128], BF16)
        self.sels = din("sels", [128, NMQ * 16], BF16)
        self.sel8 = din("sel8", [E, E * 128], BF16)
        self.out_sh = nc.dram_tensor("out_sh", [SEQS, S, D], F32,
                                     kind="ExternalOutput").ap()

        dscr = lambda n, sh, dt=BF16: nc.dram_tensor(n, sh, dt, kind="Internal")
        self.xn_hbm = dscr("xn_hbm", [S, D])
        self.nx_hbm = dscr("nx_hbm", [S, D])
        self.q_hbm = dscr("q_hbm", [HDx, S])
        self.k_hbm = dscr("k_hbm", [HDx, S])
        self.v_hbm = dscr("v_hbm", [S, HDx])
        self.ffn_hbm = dscr("ffn_hbm", [S, D])
        self.ebr_hbm = dscr("ebr_hbm", [H, LTP])
        self.w1_bf = dscr("w1_bf", [D, F])
        self.w3_bf = dscr("w3_bf", [D, F])
        self.w2_bf = dscr("w2_bf", [F, D])
        self.qkvT_bf = dscr("qkvT_bf", [D, 3 * HDx])
        self.owT_bf = dscr("owT_bf", [self.HDx, D])
        self.b13_bf = dscr("b13_bf", [2 * E * 32, F])
        if self.taps:
            self.tap_r13 = dscr("tap_r13", [128, 8 * S])
            self.tap_wT = dscr("tap_wT", [E, S])
            self.tap_adm = dscr("tap_adm", [128, cfg["NMQ"] * S])
            self.tap_wact = dscr("tap_wact", [cfg["NTC"], 128, cfg["NFC"] * cfg["TC"]])
            self.tap_pex = dscr("tap_pex", [128, cfg["QC"]])
            self.tap_ptil = dscr("tap_ptil", [128, cfg["QC"]])
            self.tap_av = dscr("tap_av", [65, cfg["QC"]])
            self.tap_brw = dscr("tap_brw", [128, cfg["WBR"]])
            self.tap_aw = dscr("tap_aw", [cfg["E"], 128, cfg["TC"]])
            self.tap_wb = dscr("tap_wb", [128, cfg["E"] * cfg["TC"]])
            self.tap_h1 = dscr("tap_h1", [cfg["E"], 128, cfg["TC"]], mybir.dt.float32)
            self.tap_h3 = dscr("tap_h3", [cfg["E"], 128, cfg["TC"]], mybir.dt.float32)
            self.tap_r2w = dscr("tap_r2w", [3, 128, cfg["TC"]])

        with contextlib.ExitStack() as st:
            self.st = st
            te, ve, se, sy, ge = nc.tensor, nc.vector, nc.scalar, nc.sync, nc.gpsimd
            self.te, self.ve, self.se, self.sy, self.ge = te, ve, se, sy, ge

            self.ones1 = self.sb("ones1", [1, 128], BF16)
            self.id_sb = self.sb("id_sb", [128, 128], BF16)
            self.sels_sb = self.sb("sels_sb", [128, NMQ * 16], BF16)
            self.sel8_sb = self.sb("sel8_sb", [8, 8 * 128], BF16)
            self.gidx_sb = st.enter_context(nc.sbuf_tensor("gidx_sb", [128, 1], I32))
            self.emb_sb = self.sb("emb_sb", [NB, H], BF16)
            self.nwb = self.sb("nwb", [128, D], BF16)
            self.gate_sb = self.sb("gate_sb", [128, NDC * E], BF16)
            self.a2T_sb = self.sb("a2T_sb", [128, NFC * E * R], BF16)
            self.b2p_sb = self.sb("b2p_sb", [128, 3 * D], BF16)
            self.biasmat = self.sb("biasmat", [128, H], F32)
            self.w_T = self.sb("w_T", [E, self.cfg["TC"]], BF16)
            self.qm2 = self.sb("qm2", [H, 2], F32)
            self.small = self.sb("small", [128, 32], F32)
            self.eps = self.sb("eps", [128, 4], F32)
            self.banks = [self.psb(f"bank{i}", [128, 512]) for i in range(8)]
            cfg_ = self.cfg
            self.khg0 = self.sb("khg0", [128, (cfg_["HPG"] // self.HPT) * self.S], BF16)
            self.vhg = self.sb("vhg", [128, self.NTT * cfg_["HPG"] * (self.DH + 1)], BF16)

            self.prelude()
            if "l" in self.phases:
                self.seq = 0
                self.phase_ln()
                self.soft_barrier()
            for seq in range(SEQS):
                self.seq = seq
                if "f" in self.phases:
                    with contextlib.ExitStack() as ffn_st:
                        outer = self.st
                        self.st = ffn_st
                        self.phase_ffn()
                        self.st = outer
                    self.soft_barrier()
                if "a" in self.phases:
                    with contextlib.ExitStack() as att_st:
                        outer = self.st
                        self.st = att_st
                        self.phase_attn()
                        self.st = outer
                    self.soft_barrier()
            if "sto0" in self.c:
                self.wait(sy, "sto0")
                self.wait(sy, "sto1")
            self.drain_stores()
            if self.taps and "tapst" in self.c:
                self.wait(sy, "tapst")
            self.st = None
        return nc

    def prelude(self):
        nc, te, ve, se, sy, ge = self.nc, self.te, self.ve, self.se, self.sy, self.ge
        D, F, E, NB, H, NDC, NMQ, LTP = (self.D, self.F, self.E, self.NB, self.H,
                                         self.NDC, self.NMQ, self.LTP)
        banks = self.banks

        sy.dma_start(self.id_sb[:, :], self.id_bf).then_inc(self.sem("ldc"), 16)
        sy.dma_start(self.sels_sb[:, :], self.sels).then_inc(self.sem("ldc"), 16)
        sy.dma_start(self.sel8_sb[0:self.E, 0:self.E * 128], self.sel8).then_inc(self.sem("ldc"), 16)
        sy.dma_start(self.gidx_sb[:, :], self.gidx).then_inc(self.sem("ldc"), 16)
        self.c["ldc"] += 64

        ge.dma_start(self.emb_sb[:, :], self.emb).then_inc(self.sem("ldg"), 16)
        ge.dma_start(self.gate_sb[:, :].rearrange("p (c e) -> p c e", c=NDC),
                     self.gate_wT.rearrange("(c p) e -> p c e", p=128)
                     ).then_inc(self.sem("ldg"), 16)
        ge.dma_start(self.a2T_sb[:, :].rearrange("p (c n) -> p c n", c=self.NFC),
                     self.a2T.rearrange("(c p) n -> p c n", p=128)
                     ).then_inc(self.sem("ldg"), 16)
        ge.dma_start(self.b2p_sb[:, :].rearrange("p (k n) -> p k n", k=3),
                     self.b2T_pad.rearrange("(k p) n -> p k n", p=128)
                     ).then_inc(self.sem("ldg"), 16)
        self.c["ldg"] += 64
        nw_row = self.sb("nw_row", [1, D], BF16)
        ge.dma_start(nw_row[:, :], self.norm_w).then_inc(self.sem("ldg"), 16)
        self.c["ldg"] += 16

        FCH = self.F // 4

        def _wc(dst, src, nm):
            ge.dma_start(dst, src).then_inc(self.sem(nm), 16)
            self.c[nm] += 16

        _wc(self.w1_bf.ap()[:, 0:FCH], self.w1T[:, 0:FCH], "wcw10")
        _wc(self.w3_bf.ap()[:, 0:FCH], self.w3T[:, 0:FCH], "wcw30")
        _wc(self.b13_bf.ap(), self.b13P, "wcb13")
        _wc(self.qkvT_bf.ap(), self.qkv_wT, "wcqkv")
        _wc(self.w1_bf.ap()[:, FCH:2 * FCH], self.w1T[:, FCH:2 * FCH], "wcw11")
        _wc(self.w3_bf.ap()[:, FCH:2 * FCH], self.w3T[:, FCH:2 * FCH], "wcw31")
        _wc(self.w2_bf.ap(), self.w2T, "wcw2")
        _wc(self.w1_bf.ap()[:, 2 * FCH:3 * FCH], self.w1T[:, 2 * FCH:3 * FCH], "wcw12")
        _wc(self.w3_bf.ap()[:, 2 * FCH:3 * FCH], self.w3T[:, 2 * FCH:3 * FCH], "wcw32")
        _wc(self.w1_bf.ap()[:, 3 * FCH:], self.w1T[:, 3 * FCH:], "wcw13")
        _wc(self.w3_bf.ap()[:, 3 * FCH:], self.w3T[:, 3 * FCH:], "wcw33")
        _wc(self.owT_bf.ap(), self.o_wT, "wcow")

        self.wait(ve, "ldc")
        self.dw()
        self.inc(ve.memset(self.ones1[:, :], 1.0), "dve")
        self.dw()
        self.inc(ve.memset(self.eps[:, 0:1], 1e-5), "dve")
        self.dw()
        self.inc(ve.memset(self.eps[:, 1:2], 1e-6), "dve")
        self.dw()
        self.inc(ve.memset(self.eps[:, 2:3], 1e-30), "dve")
        self.dw()
        self.inc(ve.memset(self.eps[:, 3:4], -2.0794415416798357), "dve")
        vb = self.vhg
        vones = AP(vb[:, :].tensor, vb[:, :].offset + self.DH,
                   [list(vb[:, :].ap[0]),
                    [self.DH + 1, self.NTT * self.cfg["HPG"]], [1, 1]])
        self.dw()
        self.inc(ve.memset(vones, 1.0), "dve")
        self.c["_vones"] = self.cur("dve")
        self.c["_preeps"] = self.cur("dve")

        self.wait(te, "ldg")
        self.wait(te, "dve")
        for j in range(D // 512):
            bk = banks[j % 2]
            if j >= 2:
                self.wait(te, "dve", self.c[f"_nwb{j % 2}"])
            mm = te.matmul(bk[:, :], self.ones1[:, :], nw_row[:, j * 512:(j + 1) * 512],
                      start=True, stop=True)
            self.inc(mm, "pe")
            self.wait(ve, "pe")
            self.dw()
            self.c[f"_nwb{j % 2}"] = self.inc(
                ve.tensor_copy(self.nwb[:, j * 512:(j + 1) * 512], bk[:, :]), "dve")

        self.a13p_sb = self.sb("a13p_sb", [128, NDC * 768], BF16)
        pre_st = contextlib.ExitStack()
        outer_st = self.st
        self.st = pre_st
        ohr_sb = self.sb("ohr_sb", [NB, LTP], BF16)
        sy.dma_start(ohr_sb[:, :], self.ohr).then_inc(self.sem("ldo"), 16)
        self.c["ldo"] += 16
        ebr_stage = self.sb("ebr_stage", [H, LTP], BF16)
        self.wait(te, "ldo")
        self.wait(te, "dve")
        for j in range(LTP // 512):
            bk = banks[j % 2]
            if j >= 2:
                self.wait(te, "sc", self.c[f"_ebr{j % 2}"])
            mm = te.matmul(bk[0:H, :], self.emb_sb[:, :],
                      ohr_sb[:, j * 512:(j + 1) * 512], start=True, stop=True)
            self.inc(mm, "pe")
            self.wait(se, "pe")
            self.sw()
            self.c[f"_ebr{j % 2}"] = self.inc(
                se.activation(ebr_stage[:, j * 512:(j + 1) * 512], bk[0:H, :],
                              AF.Exp), "sc")
        self.wait(sy, "sc")
        sy.dma_start(self.ebr_hbm.ap(), ebr_stage[:, :]).then_inc(self.sem("ebr"), 16)
        self.c["ebr"] += 16

        a13_stage = self.sb("a13_stage", [128, 1536], F32)
        for dc in range(NDC):
            if dc > 0:
                self.wait(sy, "dve")
            sy.dma_start(a13_stage[:, 0:768],
                         self.a13P_pos[dc * 128:(dc + 1) * 128, :]
                         ).then_inc(self.sem("a13"), 16)
            sy.dma_start(a13_stage[:, 768:1536],
                         self.a13P_neg[dc * 128:(dc + 1) * 128, :]
                         ).then_inc(self.sem("a13"), 16)
            self.c["a13"] += 32
            self.wait(ve, "a13")
            self.dw()
            self.inc(ve.scalar_tensor_tensor(
                self.a13p_sb[:, dc * 768:(dc + 1) * 768],
                a13_stage[:, 768:1536], -1.0, a13_stage[:, 0:768],
                op0=ALU.mult, op1=ALU.add), "dve")
        self.c["_a13p"] = self.cur("dve")

        for b in (4, 5, 6):
            self.dw()
            self.inc(ve.memset(banks[b][:, :], 0.0), "dve")
        self.wait(sy, "ebr", 16)
        self.soft_barrier()
        self.st = outer_st
        pre_st.close()

    def phase_ln(self):
        nc, te, ve, se, sy = self.nc, self.te, self.ve, self.se, self.sy
        D, NTT, seq = self.D, self.NTT, self.seq
        ln_st = contextlib.ExitStack()
        outer_st = self.st
        self.st = ln_st
        xload = [self.sb(f"xload{i}", [128, D]) for i in range(2)]
        xn_stage = [self.sb(f"xn_stage{i}", [128, D], BF16) for i in range(2)]
        nx_stage = [self.sb(f"nx_stage{i}", [128, D], BF16) for i in range(2)]
        sqdump = [self.sb(f"sqdump{i}", [128, D]) for i in range(2)]
        stat = [self.sb(f"stat{i}", [128, 8]) for i in range(2)]
        self.st = outer_st
        nwb = self.nwb

        for tt in range(NTT):
            slot = tt % 2
            stt = stat[slot]
            if f"_lnxv{slot}" in self.c:
                self.wait(sy, "dve", self.c[f"_lnxv{slot}"])
                self.wait(sy, "sc", self.c[f"_lnxs{slot}"])
            sy.dma_start(xload[slot][:, :], self.x_sh[seq, tt * 128:(tt + 1) * 128, :]
                         ).then_inc(self.sem(f"xld{slot}"), 16)
            self.c[f"xld{slot}"] += 16
            self.wait(ve, f"xld{slot}")
            if f"_lnsc{slot}" in self.c:
                self.wait(ve, "sc", self.c[f"_lnsc{slot}"])
            self.dw()
            self.inc(ve.tensor_reduce(stt[:, 0:1], xload[slot][:, :],
                                      axis=AX.X, op=ALU.add), "dve")
            self.dw()
            self.inc(ve.tensor_scalar(stt[:, 0:1], stt[:, 0:1], 1.0 / D, None,
                                      op0=ALU.mult), "dve")
            mn_done = self.cur("dve")
            self.wait(se, f"xld{slot}")
            self.wait(se, "dve", mn_done)
            self.sw()
            self.inc(se.activation(sqdump[slot][:, :], xload[slot][:, :], AF.Square,
                                   accum_out=stt[:, 1:2]), "sc")
            sq_done = self.cur("sc")
            self.c[f"_lnxs{slot}"] = sq_done
            self.wait(ve, "sc", sq_done)
            self.dw()
            self.inc(ve.scalar_tensor_tensor(stt[:, 2:3], stt[:, 0:1], 0.0,
                                             stt[:, 0:1], op0=ALU.bypass,
                                             op1=ALU.mult), "dve")
            self.dw()
            self.inc(ve.scalar_tensor_tensor(stt[:, 2:3], stt[:, 1:2], 1.0 / D,
                                             stt[:, 2:3], op0=ALU.mult,
                                             op1=ALU.subtract), "dve")
            var_done = self.cur("dve")
            self.wait(se, "dve", var_done)
            self.sw()
            self.inc(se.activation(stt[:, 3:4], stt[:, 2:3], AF.Ln,
                                   bias=self.eps[:, 0:1], scale=1.0), "sc")
            self.sw()
            self.inc(se.activation(stt[:, 4:5], stt[:, 3:4], AF.Exp,
                                   bias=0.0, scale=-1.0), "sc")
            self.sw()
            self.inc(se.activation(stt[:, 3:4], stt[:, 3:4], AF.Exp,
                                   bias=0.0, scale=-0.5), "sc")
            rcp_done = self.cur("sc")
            self.wait(ve, "sc", rcp_done)
            self.dw()
            self.inc(ve.scalar_tensor_tensor(stt[:, 5:6], stt[:, 2:3], 0.0,
                                             stt[:, 4:5], op0=ALU.bypass,
                                             op1=ALU.mult), "dve")
            u_done = self.cur("dve")
            self.wait(se, "dve", u_done)
            self.sw()
            self.inc(se.activation(stt[:, 6:7], stt[:, 5:6], AF.Ln,
                                   bias=self.eps[:, 1:2], scale=1.0), "sc")
            self.sw()
            self.inc(se.activation(stt[:, 6:7], stt[:, 6:7], AF.Exp,
                                   bias=0.0, scale=-0.5), "sc")
            s_done = self.cur("sc")
            self.c[f"_lnsc{slot}"] = s_done
            self.wait(ve, "sc", s_done)
            stk = f"_xnst{slot}"
            if stk in self.c:
                self.wait(ve, f"xnst{slot}", self.c[stk])
            self.dw()
            self.inc(ve.tensor_scalar(xn_stage[slot][:, :], xload[slot][:, :],
                                      stt[:, 0:1], stt[:, 3:4],
                                      op0=ALU.subtract, op1=ALU.mult), "dve")
            self.dw()
            self.inc(ve.scalar_tensor_tensor(nx_stage[slot][:, :],
                                             xn_stage[slot][:, :], stt[:, 6:7],
                                             nwb[:, :], op0=ALU.mult,
                                             op1=ALU.mult), "dve")
            self.c[f"_ln{slot}"] = self.cur("dve")
            self.c[f"_lnxv{slot}"] = self.cur("dve")
            self.wait(sy, "dve", self.c[f"_ln{slot}"])

            self.inc(sy.dma_start(self.xn_hbm.ap()[tt * 128:(tt + 1) * 128, :],
                                  xn_stage[slot][:, :]), f"xnst{slot}", 16)
            self.inc(sy.dma_start(self.nx_hbm.ap()[tt * 128:(tt + 1) * 128, :],
                                  nx_stage[slot][:, :]), f"xnst{slot}", 16)
            self.c[stk] = self.cur(f"xnst{slot}")
            self.c[f"_xnsT{tt}"] = (slot, self.cur(f"xnst{slot}"))
        ln_st.close()

    def transpose_in(self, src_hbm, dst_sb, sem_name):
        S, NDC = self.S, self.NDC
        for dc in range(NDC):
            self.inc(self.sy.dma_start_transpose(
                dst_sb[:, dc * S:(dc + 1) * S],
                src_hbm.ap()[:, dc * 128:(dc + 1) * 128]), sem_name, 16)

    def phase_ffn(self):
        nc, te, ve, se, sy, ge = self.nc, self.te, self.ve, self.se, self.sy, self.ge
        S, D, F, E, R, H, DH = self.S, self.D, self.F, self.E, self.R, self.H, self.DH
        NDC, NFC, NTT, TC, NTC = self.NDC, self.NFC, self.NTT, self.TC, self.NTC
        NMQ, HDx = self.NMQ, self.HDx
        banks, seq = self.banks, self.seq
        sq = f"s{seq}"
        CD = min(512, D)
        C4 = min(512, S)
        NT4 = S // C4
        CV = min(512, HDx)
        NVC = HDx // CV

        r13 = self.sb("r13", [128, 6 * TC], BF16)
        wact = self.sb("wact", [128, NFC * TC], BF16)
        aw = [self.sb(f"aw{i}", [128, TC], BF16) for i in range(6)]
        tsum = [self.sb(f"tsum{i}", [128, TC], BF16) for i in range(4)]
        sring = [self.sb(f"sring{i}", [128, TC], BF16) for i in range(4)]
        h3ring = [self.sb(f"h3ring{i}", [128, TC], BF16) for i in range(4)]
        wb_sb = self.sb("wb_sb", [128, E * TC], BF16)
        r2w_sb = [self.sb(f"r2w{i}", [128, TC], BF16) for i in range(3)]
        w13t = [self.sb(f"w13t{i}", [128, 2 * NDC * 128], BF16) for i in range(4)]
        b13t = [self.sb(f"b13t{i}", [128, 2 * E * 128], BF16) for i in range(4)]
        w2t = [self.sb(f"w2t{i}", [128, CD], BF16) for i in range(3)]
        dstage = [self.sb(f"dstage{i}", [128, CD], BF16) for i in range(2)]
        gate_f = self.sb("gate_f", [128, 32], F32)
        wt_bf = self.sb("wt_bf", [128, E], BF16)
        nxt = [self.sb(f"nxt{i}", [128, NDC * TC], BF16) for i in range(2)]
        xnt = [self.sb(f"xnt{i}", [128, NDC * C4], BF16) for i in range(2)]
        qkvw = [self.sb(f"qkvw{i}", [128, NDC * 128], BF16) for i in range(2)]
        vw = self.sb("vw", [128, NDC * CV], BF16)
        qkv_stage = [self.sb(f"qkvs{i}", [128, C4], BF16) for i in range(2)]
        sq_sb = [self.sb("sqsb0", [128, C4], BF16)]

        self.wait(te, "sc")
        self.wait(te, "dve")

        def load_nx(tc_i):
            if tc_i >= NTC:
                return
            sl = tc_i % 2
            key = f"_nxu{sl}"
            if key in self.c:
                self.wait(sy, "pe", self.c[key])
            for tt_ in range(tc_i * (TC // 128), (tc_i + 1) * (TC // 128)):
                slt, cnt = self.c[f"_xnsT{tt_}"]
                self.wait(sy, f"xnst{slt}", cnt)
            for dc in range(NDC):
                self.inc(sy.dma_start_transpose(
                    nxt[sl][:, dc * TC:(dc + 1) * TC],
                    self.nx_hbm.ap()[tc_i * TC:(tc_i + 1) * TC, dc * 128:(dc + 1) * 128]),
                    f"nxl{sl}", 16)
            self.c[f"_nxc{sl}"] = self.cur(f"nxl{sl}")

        def load_xn(t4_i):
            if t4_i >= NT4:
                return
            sl = t4_i % 2
            key = f"_xnu{sl}"
            if key in self.c:
                self.wait(sy, "pe", self.c[key])
            for tt_ in range(t4_i * (C4 // 128), (t4_i + 1) * (C4 // 128)):
                slt, cnt = self.c[f"_xnsT{tt_}"]
                self.wait(sy, f"xnst{slt}", cnt)
            for dc in range(NDC):
                self.inc(sy.dma_start_transpose(
                    xnt[sl][:, dc * C4:(dc + 1) * C4],
                    self.xn_hbm.ap()[t4_i * C4:(t4_i + 1) * C4, dc * 128:(dc + 1) * 128]),
                    f"xnl{sl}", 16)
            self.c[f"_xnc{sl}"] = self.cur(f"xnl{sl}")

        qk_order = []
        for _t4 in range(NT4):
            qk_order.extend(range(2 * NMQ))
        self._qkpf = dict(nload=0)

        def issue_qkvw():
            k = self._qkpf["nload"]
            if k >= len(qk_order):
                return
            mc_i = qk_order[k]
            sl = k % 2
            key = f"_qwu{sl}"
            if k == 0:
                self.wait(sy, "wcqkv", 16)
            if key in self.c:
                self.wait(sy, "pe", self.c[key])
            self.inc(sy.dma_start(
                qkvw[sl][:, :].rearrange("p (c n) -> p c n", c=NDC),
                self.qkvT_bf.ap().rearrange("(c p) n -> p c n", p=128)[:, :, mc_i * 128:(mc_i + 1) * 128]),
                f"qwl{sl}", 16)
            self.c[f"_qwc{sl}"] = self.cur(f"qwl{sl}")
            self._qkpf["nload"] = k + 1

        qkv_units = []
        for t4 in range(NT4):
            for mc in range(2 * NMQ):
                qkv_units.append(("qk", t4, mc))
            for vc in range(NVC):
                for tt in range(C4 // 128):
                    qkv_units.append(("v", t4, t4 * (C4 // 128) + tt, vc))
        self._qkv_state = dict(idx=0, wu=0, t4loaded=-1)

        def emit_qkv_unit():
            st_ = self._qkv_state
            if st_["idx"] >= len(qkv_units):
                return False
            u = qkv_units[st_["idx"]]
            bk = banks[7]
            if u[0] == "qk":
                _, t4, mc = u
                kq = self._qkv_state.setdefault("nqk", 0)
                self._qkv_state["nqk"] = kq + 1
                if mc == 0:
                    load_xn(t4 + 1)
                wsl = kq % 2
                self.wait(te, f"xnl{t4 % 2}", self.c[f"_xnc{t4 % 2}"])
                self.wait(te, f"qwl{wsl}", self.c[f"_qwc{wsl}"])
                if "_qkv7" in self.c:
                    self.wait(te, "dve", self.c["_qkv7"])
                    self.wait(te, "sc", self.c["_qkv7sc"])
                for dc in range(NDC):
                    mm = te.matmul(bk[:, 0:C4], qkvw[wsl][:, dc * 128:(dc + 1) * 128],
                              xnt[t4 % 2][:, dc * C4:(dc + 1) * C4],
                              start=(dc == 0), stop=(dc == NDC - 1))
                self.c[f"_qwu{wsl}"] = self.inc(mm, "pe")
                pq = self.c[f"_qwu{wsl}"]
                self.c[f"_xnu{t4 % 2}"] = pq
                issue_qkvw()
                isq = mc < NMQ
                est = st_["idx"] % 2
                self.wait(ve, "pe", pq)
                if f"_qst{est}" in self.c:
                    self.wait(ve, f"qkst{est}", self.c[f"_qst{est}"])
                self.dw()
                if isq:
                    self.inc(ve.tensor_scalar(qkv_stage[est][:, :], bk[:, 0:C4], 0.125,
                                              None, op0=ALU.mult), "dve")
                else:
                    self.inc(ve.tensor_copy(qkv_stage[est][:, :], bk[:, 0:C4]), "dve")
                self.c["_qkv7"] = self.cur("dve")
                self.wait(sy, "dve")
                dst = self.q_hbm if isq else self.k_hbm
                r0 = (mc % NMQ) * 128
                self.inc(sy.dma_start(dst.ap()[r0:r0 + 128, t4 * C4:(t4 + 1) * C4],
                                      qkv_stage[est][:, :]), f"qkst{est}", 16)
                self.c[f"_qst{est}"] = self.cur(f"qkst{est}")
                ssl = 0
                self.wait(se, "pe", pq)
                self.wait(se, "dve", self.c["_qkv7"])
                if f"_sqr{ssl}" in self.c:
                    self.wait(se, "pe", self.c[f"_sqr{ssl}"])
                self.sw()
                self.c["_qkv7sc"] = self.inc(
                    se.activation(sq_sb[ssl][:, :], bk[:, 0:C4], AF.Square), "sc")
                sel = self.sels_sb
                self.wait(te, "sc", self.c["_qkv7sc"])
                self.wait(te, "dve", self.c["_qkv7"])
                mm = te.matmul(bk[0:H, 0:C4], sel[:, (mc % NMQ) * 16:(mc % NMQ) * 16 + H],
                          sq_sb[ssl][:, :], start=True, stop=True)
                self.c[f"_sqr{ssl}"] = self.inc(mm, "pe")
                self.wait(ve, "pe", self.c[f"_sqr{ssl}"])
                ii = 0 if isq else 1
                self.dw()
                self.inc(ve.tensor_reduce(self.small[0:H, 8 + ii:9 + ii], bk[0:H, 0:C4],
                                          axis=AX.X, op=ALU.max), "dve")
                self.dw()
                if t4 == 0 and mc % NMQ == 0:
                    self.inc(ve.tensor_copy(self.qm2[0:H, ii:ii + 1],
                                            self.small[0:H, 8 + ii:9 + ii]), "dve")
                else:
                    self.inc(ve.tensor_tensor(self.qm2[0:H, ii:ii + 1],
                                              self.qm2[0:H, ii:ii + 1],
                                              self.small[0:H, 8 + ii:9 + ii],
                                              op=ALU.max), "dve")
                self.c["_qkv7"] = self.cur("dve")
            else:
                _, t4, tt, vc = u
                load_vw(vc)
                self.wait(te, f"xnl{t4 % 2}", self.c[f"_xnc{t4 % 2}"])
                self.wait(te, "vwl", self.cur("vwl"))
                if "_qkv7" in self.c:
                    self.wait(te, "dve", self.c["_qkv7"])
                    self.wait(te, "sc", self.c["_qkv7sc"])
                lt = tt - t4 * (C4 // 128)
                for dc in range(NDC):
                    mm = te.matmul(bk[:, 0:CV],
                              xnt[t4 % 2][:, dc * C4 + lt * 128: dc * C4 + lt * 128 + 128],
                              vw[:, dc * CV:(dc + 1) * CV],
                              start=(dc == 0), stop=(dc == NDC - 1))
                pq = self.inc(mm, "pe")
                self.c[f"_xnu{t4 % 2}"] = pq
                self.c["_vwu"] = pq
                est = st_["idx"] % 2
                self.wait(ve, "pe", pq)
                if f"_qst{est}" in self.c:
                    self.wait(ve, f"qkst{est}", self.c[f"_qst{est}"])
                self.dw()
                self.inc(ve.tensor_copy(qkv_stage[est][:, 0:CV], bk[:, 0:CV]), "dve")
                self.c["_qkv7"] = self.cur("dve")
                self.wait(sy, "dve")
                self.inc(sy.dma_start(
                    self.v_hbm.ap()[tt * 128:(tt + 1) * 128, vc * CV:(vc + 1) * CV],
                    qkv_stage[est][:, 0:CV]), f"qkst{est}", 16)
                self.c[f"_qst{est}"] = self.cur(f"qkst{est}")
            st_["idx"] += 1
            return True

        self._vwst = dict(loaded=None)

        def load_vw(vc_i):
            if self._vwst["loaded"] == vc_i:
                return
            if self._vwst["loaded"] is None:
                self.wait(sy, "wcqkv", 16)
            if "_vwu" in self.c:
                self.wait(sy, "pe", self.c["_vwu"])
            self.inc(sy.dma_start(
                vw[:, :].rearrange("p (c n) -> p c n", c=NDC),
                self.qkvT_bf.ap().rearrange("(c p) n -> p c n", p=128)[:, :, 2 * HDx + vc_i * CV: 2 * HDx + (vc_i + 1) * CV]),
                "vwl", 16)
            self._vwst["loaded"] = vc_i

        load_nx(0)
        load_xn(0)
        issue_qkvw()
        issue_qkvw()

        lane = {0: (0, 1), 1: (2, 3)}
        r2w_banks = (4, 5, 6)
        for tc in range(NTC):
            t0 = tc * TC
            sl = tc % 2
            load_nx(tc + 1)
            if self.taps and self.cur("tapst") > 0:
                self.wait(ve, "tapst", self.cur("tapst"))
            self.wait(te, f"nxl{sl}", self.c[f"_nxc{sl}"])
            if "_devac" in self.c:
                self.wait(te, "dve", self.c["_devac"])
            if "_r2wevac2" in self.c:
                self.wait(te, "dve", self.c["_r2wevac2"])
            if "_laneA0" in self.c:
                self.wait(te, "sc", self.c["_laneA0"])
            for lt in range(TC // 128):
                bk = banks[6]
                if "_gate_done" in self.c:
                    self.wait(te, "dve", self.c["_gate_done"])
                if "_gate_sc" in self.c:
                    self.wait(te, "sc", self.c["_gate_sc"])
                for dc in range(NDC):
                    mm = te.matmul(bk[:, 0:E],
                              nxt[sl][:, dc * TC + lt * 128: dc * TC + (lt + 1) * 128],
                              self.gate_sb[:, dc * E:(dc + 1) * E],
                              start=(dc == 0), stop=(dc == NDC - 1))
                self.inc(mm, "pe")
                self.wait(ve, "pe")
                self.wait(ve, "sc")
                self.dw()
                self.inc(ve.tensor_reduce(gate_f[:, 0:1], bk[:, 0:E], axis=AX.X,
                                          op=ALU.max), "dve")
                self.dw()
                self.inc(ve.tensor_scalar(gate_f[:, 1:2], gate_f[:, 0:1], -1.0, None,
                                          op0=ALU.mult), "dve")
                self.wait(se, "dve")
                self.sw()
                self.c["_gate_sc"] = self.inc(
                    se.activation(gate_f[:, 2:2 + E], bk[:, 0:E], AF.Exp,
                                  bias=gate_f[:, 1:2], scale=1.0), "sc")
                self.wait(ve, "sc")
                eg = gate_f[:, 2:2 + E]
                self.dw()
                self.inc(ve.tensor_reduce(gate_f[:, 10:11], eg, axis=AX.X, op=ALU.max), "dve")
                self.dw()
                self.inc(ve.tensor_scalar(gate_f[:, 12:12 + E], eg, gate_f[:, 10:11],
                                          None, op0=ALU.is_equal), "dve")
                self.dw()
                self.inc(ve.tensor_scalar(gate_f[:, 20:20 + E], gate_f[:, 12:12 + E],
                                          -1.0, 1.0, op0=ALU.mult, op1=ALU.add), "dve")
                self.dw()
                self.inc(ve.tensor_tensor(gate_f[:, 20:20 + E], eg, gate_f[:, 20:20 + E],
                                          op=ALU.mult), "dve")
                self.dw()
                self.inc(ve.tensor_reduce(gate_f[:, 11:12], gate_f[:, 20:20 + E],
                                          axis=AX.X, op=ALU.max), "dve")
                self.dw()
                self.inc(ve.tensor_scalar(gate_f[:, 20:20 + E], gate_f[:, 20:20 + E],
                                          gate_f[:, 11:12], None, op0=ALU.is_equal), "dve")
                self.dw()
                self.inc(ve.tensor_tensor(gate_f[:, 12:12 + E], gate_f[:, 12:12 + E],
                                          gate_f[:, 20:20 + E], op=ALU.add), "dve")
                self.dw()
                self.inc(ve.scalar_tensor_tensor(gate_f[:, 30:31], gate_f[:, 10:11], 0.0,
                                                 gate_f[:, 11:12], op0=ALU.bypass,
                                                 op1=ALU.add), "dve")
                self.dw()
                self.inc(ve.reciprocal(gate_f[:, 31:32], gate_f[:, 30:31]), "dve")
                self.dw()
                self.inc(ve.tensor_tensor(gate_f[:, 12:12 + E], eg, gate_f[:, 12:12 + E],
                                          op=ALU.mult), "dve")
                self.dw()
                self.inc(ve.tensor_scalar(wt_bf[:, :], gate_f[:, 12:12 + E],
                                          gate_f[:, 31:32], None, op0=ALU.mult), "dve")
                self.c["_gate_done"] = self.cur("dve")
                bkt = banks[6][:, :].bitcast(BF16)
                self.wait(te, "dve")
                mm = te.transpose(bkt[0:E, 0:128], wt_bf[:, :], self.id_sb[:, :])
                self.inc(mm, "pe")
                self.wait(ve, "pe")
                self.dw()
                self.c["_wTe"] = self.inc(ve.tensor_copy(
                    self.w_T[:, lt * 128: (lt + 1) * 128], bkt[0:E, 0:128]), "dve")
                self.c["_gate_done"] = self.cur("dve")
                emit_qkv_unit()
            for j in range(6):
                bk = banks[j % 2]
                key = f"_r13b{j % 2}"
                if key in self.c:
                    self.wait(te, "dve", self.c[key])
                if tc == 0 and j < 2 and f"_lane{'AB'[j % 2]}" not in self.c:
                    pass
                for dc in range(NDC):
                    mm = te.matmul(bk[:, 0:TC],
                              self.a13p_sb[:, dc * 768 + j * 128: dc * 768 + (j + 1) * 128],
                              nxt[sl][:, dc * TC: dc * TC + TC],
                              start=(dc == 0), stop=(dc == NDC - 1))
                self.inc(mm, "pe")
                self.wait(ve, "pe")
                self.dw()
                self.c[key] = self.inc(
                    ve.tensor_scalar(r13[:, j * TC: (j + 1) * TC],
                                     bk[:, 0:TC], 2.0, None, op0=ALU.mult), "dve")
                emit_qkv_unit()
            self.wait(te, "dve")
            self.wait(te, "sc")
            for e in range(E):
                bk = banks[6]
                key = "_wb6"
                if key in self.c:
                    self.wait(te, "dve", self.c[key])
                mm = te.matmul(bk[:, 0:TC], self.sel8_sb[0:E, e * 128:(e + 1) * 128],
                          self.w_T[0:E, 0:TC], start=True, stop=True)
                self.inc(mm, "pe")
                self.wait(ve, "pe")
                self.dw()
                self.c[key] = self.inc(
                    ve.tensor_copy(wb_sb[:, e * TC:(e + 1) * TC], bk[:, 0:TC]), "dve")

            for b in r2w_banks:
                self.dw()
                self.wait(ve, "pe", self.c.get("_r2wfree", 0))
                self.inc(ve.memset(banks[b][:, 0:TC], 0.0), "dve")

            def issue_w13(fcp_i, ch_i):
                fc_i = 2 * fcp_i + ch_i
                if fc_i >= NFC:
                    return
                sl2 = ch_i * 2 + fcp_i % 2
                k1 = f"_w13u{sl2}"
                wch = fc_i // (self.NFC // 4)
                self.wait(sy, f"wcw1{wch}", 16)
                self.wait(sy, f"wcw3{wch}", 16)
                if fc_i == 0:
                    self.wait(sy, "wcb13", 16)
                if k1 in self.c:
                    self.wait(sy, "pe", self.c[k1])
                self.inc(sy.dma_start(
                    w13t[sl2][:, 0:NDC * 128].rearrange("p (c n) -> p c n", c=NDC),
                    self.w1_bf.ap().rearrange("(c p) n -> p c n", p=128)[:, :, fc_i * 128:(fc_i + 1) * 128]),
                    f"w13l{sl2}", 16)
                self.inc(sy.dma_start(
                    w13t[sl2][:, NDC * 128:].rearrange("p (c n) -> p c n", c=NDC),
                    self.w3_bf.ap().rearrange("(c p) n -> p c n", p=128)[:, :, fc_i * 128:(fc_i + 1) * 128]),
                    f"w13l{sl2}", 16)
                b13src = self.b13_bf.ap().rearrange("(w p) n -> p w n", p=32)[:, :, fc_i * 128:(fc_i + 1) * 128]
                for ro in (0, 32, 64):
                    self.inc(sy.dma_start(
                        b13t[sl2][ro:ro + 32, :].rearrange("p (w n) -> p w n", w=2 * E),
                        b13src), f"w13l{sl2}", 16)

            issue_w13(0, 0)
            issue_w13(0, 1)
            w13cnt = {i: self.cur(f"w13l{i}") for i in range(4)}
            for fcp in range(NFC // 2):
                for ch in (0, 1):
                    fc = 2 * fcp + ch
                    A, Bk = lane[ch]
                    wsl = ch * 2 + fcp % 2
                    issue_w13(fcp + 1, ch)
                    self.wait(te, f"w13l{wsl}", w13cnt[wsl])
                    nsl = ch * 2 + (fcp + 1) % 2
                    w13cnt[nsl] = self.cur(f"w13l{nsl}")
                    if f"_laneA{ch}" in self.c:
                        self.wait(te, "sc", self.c[f"_laneA{ch}"])
                    if f"_laneB{ch}" in self.c:
                        self.wait(te, "dve", self.c[f"_laneB{ch}"])
                    for dc in range(NDC):
                        mm = te.matmul(banks[A][:, 0:TC], w13t[wsl][:, dc * 128:(dc + 1) * 128],
                                  nxt[sl][:, dc * TC: dc * TC + TC],
                                  start=(dc == 0), stop=(dc == NDC - 1))
                    for dc in range(NDC):
                        mm = te.matmul(banks[Bk][:, 0:TC],
                                  w13t[wsl][:, NDC * 128 + dc * 128: NDC * 128 + (dc + 1) * 128],
                                  nxt[sl][:, dc * TC: dc * TC + TC],
                                  start=(dc == 0), stop=(dc == NDC - 1))
                    self.c[f"_w13u{wsl}"] = self.inc(mm, "pe")
                r2w_pending = []

                def emit_r2w():
                    for (fc_, e_, slot_) in r2w_pending:
                        kb_, jj_ = divmod(e_, 3)
                        self.wait(te, "dve", self.c[f"_awd{slot_}"])
                        mm_ = te.matmul(
                            banks[r2w_banks[kb_]][32 * jj_:32 * jj_ + 16, 0:TC],
                            self.a2T_sb[:, fc_ * E * R + e_ * R: fc_ * E * R + (e_ + 1) * R],
                            aw[slot_][:, :], start=False, stop=False,
                            skip_group_check=True, tile_position=(0, 32 * jj_))
                        self.c[f"_awr{slot_}"] = self.inc(mm_, "pe")
                    r2w_pending.clear()

                for e in range(E):
                    pA = {}
                    pB = {}
                    for ch in (0, 1):
                        A, Bk = lane[ch]
                        wsl = ch * 2 + fcp % 2
                        b1i, b3i = e, E + e
                        jt1, ro1 = b1i // 3, 32 * (b1i % 3)
                        jt3, ro3 = b3i // 3, 32 * (b3i % 3)
                        if e > 0:
                            self.wait(te, "sc", self.c[f"_silu{ch}"])
                        mm = te.matmul(banks[A][:, 0:TC],
                                  b13t[wsl][ro1:ro1 + 32,
                                            b1i * 128: b1i * 128 + 128],
                                  r13[ro1:ro1 + 32, jt1 * TC: (jt1 + 1) * TC],
                                  start=False, stop=False, skip_group_check=True)
                        pA[ch] = self.inc(mm, "pe")
                        if e > 0:
                            self.wait(te, "dve", self.c[f"_h3w{ch}"])
                        mm = te.matmul(banks[Bk][:, 0:TC],
                                  b13t[wsl][ro3:ro3 + 32,
                                            (E + e) * 128: (E + e) * 128 + 128],
                                  r13[ro3:ro3 + 32, jt3 * TC: (jt3 + 1) * TC],
                                  start=False, stop=False, skip_group_check=True)
                        pB[ch] = self.inc(mm, "pe")
                        if e == E - 1:
                            self.c[f"_w13u{wsl}"] = self.cur("pe")
                    emit_r2w()
                    for ch in (0, 1):
                        srA = sring[ch * 2 + e % 2]
                        self.wait(se, "pe", pA[ch])
                        if f"_awtt{ch}{e % 2}" in self.c:
                            self.wait(se, "dve", self.c[f"_awtt{ch}{e % 2}"])
                        self.sw()
                        self.c[f"_silu{ch}"] = self.inc(
                            se.activation(srA[:, :], banks[lane[ch][0]][:, 0:TC],
                                          AF.Silu), "sc")
                    for ch in (0, 1):
                        h3A = h3ring[ch * 2 + e % 2]
                        self.wait(ve, "pe", pB[ch])
                        self.dw()
                        self.c[f"_h3w{ch}"] = self.inc(
                            ve.tensor_tensor(h3A[:, :], banks[lane[ch][1]][:, 0:TC],
                                             wb_sb[:, e * TC:(e + 1) * TC],
                                             op=ALU.mult), "dve")
                    for ch in (0, 1):
                        fc = 2 * fcp + ch
                        srA = sring[ch * 2 + e % 2]
                        h3A = h3ring[ch * 2 + e % 2]
                        awslot = ch * 3 + e % 3
                        if f"_awr{awslot}" in self.c:
                            self.wait(ve, "pe", self.c[f"_awr{awslot}"])
                        if f"_awp{awslot}" in self.c:
                            self.wait(ve, "po", self.c[f"_awp{awslot}"])
                        self.wait(ve, "sc", self.c[f"_silu{ch}"])
                        self.dw()
                        self.c[f"_aw{ch}"] = self.inc(
                            ve.tensor_tensor(aw[awslot][:, :], srA[:, :], h3A[:, :],
                                             op=ALU.mult), "dve")
                        self.c[f"_awd{awslot}"] = self.c[f"_aw{ch}"]
                        self.c[f"_awtt{ch}{e % 2}"] = self.c[f"_aw{ch}"]
                        r2w_pending.append((fc, e, awslot))
                        tb = ch * 2
                        aprev = aw[ch * 3 + (e - 1) % 3]
                        if e % 2 == 1:
                            self.wait(ge, "dve", self.c[f"_aw{ch}"])
                        if e == 1 or e == 5:
                            self.inc(ge.tensor_tensor(tsum[tb + (e // 4)][:, :],
                                                      aprev[:, :], aw[awslot][:, :],
                                                      op=ALU.add), "po")
                        if e == 3:
                            self.inc(ge.tensor_tensor(tsum[tb + 1][:, :], aprev[:, :],
                                                      aw[awslot][:, :], op=ALU.add), "po")
                            self.gw()
                            self.inc(ge.tensor_tensor(tsum[tb][:, :], tsum[tb][:, :],
                                                      tsum[tb + 1][:, :], op=ALU.add), "po")
                        if e == 7:
                            wv = wact[:, fc * TC:(fc + 1) * TC]
                            self.inc(ge.tensor_tensor(wv, aprev[:, :], aw[awslot][:, :],
                                                      op=ALU.add), "po")
                            self.gw()
                            self.inc(ge.tensor_tensor(wv, wv, tsum[tb + 1][:, :],
                                                      op=ALU.add), "po")
                            self.gw()
                            self.c[f"_wact{fc}"] = self.inc(
                                ge.tensor_tensor(wv, wv, tsum[tb][:, :], op=ALU.add),
                                "po")
                            self.c[f"_laneB{ch}"] = self.c[f"_h3w{ch}"]
                            self.c[f"_laneA{ch}"] = self.c[f"_silu{ch}"]
                        if e % 2 == 1:
                            self.c[f"_awp{awslot}"] = self.cur("po")
                            self.c[f"_awp{ch * 3 + (e - 1) % 3}"] = self.cur("po")
                emit_r2w()
                emit_qkv_unit()

            self.wait(ve, "pe")
            for kb in range(3):
                self.dw()
                self.c[f"_r2wevac{kb}"] = self.inc(
                    ve.tensor_scalar(r2w_sb[kb][:, :], banks[r2w_banks[kb]][:, 0:TC],
                                     2.0, None, op0=ALU.mult), "dve")
            self.c["_wb6"] = self.c["_r2wevac2"]

            if self.taps:
                self.wait(sy, "dve")
                for kb in range(3):
                    self.inc(sy.dma_start(self.tap_r2w.ap()[kb], r2w_sb[kb][:, :]), "tapst", 16)

            NTS = TC // 128
            self.wait(te, "sc")

            def issue_w2(gi):
                if gi >= (D // CD) * NFC:
                    return
                dcb2, fc2 = divmod(gi, NFC)
                sl2 = gi % 3
                key = f"_w2u{sl2}"
                if gi == 0:
                    self.wait(sy, "wcw2", 16)
                if key in self.c:
                    self.wait(sy, "pe", self.c[key])
                self.inc(sy.dma_start(
                    w2t[sl2][:, :],
                    self.w2_bf.ap()[fc2 * 128:(fc2 + 1) * 128, dcb2 * CD:(dcb2 + 1) * CD]),
                    f"w2l{sl2}", 16)
                self.c[f"_w2c{sl2}"] = self.cur(f"w2l{sl2}")

            issue_w2(0)
            issue_w2(1)
            for dcb in range(D // CD):
                for fc in range(NFC):
                    gi = dcb * NFC + fc
                    wslot = gi % 3
                    issue_w2(gi + 2)
                    self.wait(te, f"w2l{wslot}", self.c[f"_w2c{wslot}"])
                    self.wait(te, "po", self.c[f"_wact{fc}"])
                    if fc == 0 and dcb > 0:
                        self.wait(te, "dve", self.c["_devac"])
                    for ts_ in range(NTS):
                        mm = te.matmul(banks[ts_][:, 0:CD],
                                  wact[:, fc * TC + ts_ * 128: fc * TC + (ts_ + 1) * 128],
                                  w2t[wslot][:, :], start=(fc == 0), stop=False)
                    self.c[f"_w2u{wslot}"] = self.inc(mm, "pe")
                    if fc % 8 == 3:
                        emit_qkv_unit()
                self.wait(te, "dve", self.c["_r2wevac2"])
                for ts_ in range(NTS):
                    for kb in range(3):
                        mm = te.matmul(banks[ts_][:, 0:CD],
                                  r2w_sb[kb][:, ts_ * 128:(ts_ + 1) * 128],
                                  self.b2p_sb[:, kb * D + dcb * CD: kb * D + (dcb + 1) * CD],
                                  start=False, stop=(kb == 2))
                pev = self.inc(mm, "pe")
                for ts_ in range(NTS):
                    sl2 = ts_ % 2
                    self.wait(ve, "pe", pev)
                    stk = f"_dst{sl2}"
                    if stk in self.c:
                        self.wait(ve, f"dstg{sl2}", self.c[stk])
                    self.dw()
                    self.inc(ve.tensor_copy(dstage[sl2][:, :], banks[ts_][:, 0:CD]), "dve")
                    self.wait(sy, "dve")
                    self.inc(sy.dma_start(
                        self.ffn_hbm.ap()[t0 + ts_ * 128: t0 + (ts_ + 1) * 128,
                                          dcb * CD:(dcb + 1) * CD],
                        dstage[sl2][:, :]), f"dstg{sl2}", 16)
                    self.c[stk] = self.cur(f"dstg{sl2}")
                self.c["_devac"] = self.cur("dve")
            if self.taps:
                self.wait(sy, "po", self.cur("po"))
                self.inc(sy.dma_start(self.tap_wact.ap()[tc], wact[:, :]), "tapst", 16)
            self.c["_laneA0"] = self.cur("sc")
            self.c["_laneB0"] = self.cur("dve")
            self.c["_laneA1"] = self.cur("sc")
            self.c["_laneB1"] = self.cur("dve")
            self.c["_w13u0"] = self.cur("pe")
            self.c["_w13u2"] = self.cur("pe")
            self.c["_r2wfree"] = self.cur("pe")
        while emit_qkv_unit():
            pass

        if "a" in self.phases and "x" in self.aparts:
            HPG_, HPT_, DH_, NTT_, QC_ = (self.cfg["HPG"], self.HPT, self.DH,
                                          self.NTT, self.QC)
            for i in range(4):
                if f"qkst{i}" in self.c:
                    self.wait(ge, f"qkst{i}")
            if "_khgfree0" in self.c:
                self.wait(ge, "pe", self.c["_khgfree0"])
            self.inc(ge.dma_start(
                self.khg0[:, :].rearrange("p (c n) -> p c n", c=HPG_ // HPT_),
                self.k_hbm.ap().rearrange("(c p) n -> p c n", p=128)[:, 0:HPG_ // HPT_, :]),
                "khgp", 16)
            self.c["_khgc0"] = self.cur("khgp")
            self._khgsem[0] = "khgp"
            if "_vhgfree" in self.c:
                self.wait(ge, "pe", self.c["_vhgfree"])
            for hh2 in range(HPG_):
                vdst = AP(self.vhg[:, :].tensor, self.vhg[:, :].offset + hh2 * (DH_ + 1),
                          [list(self.vhg[:, :].ap[0]), [HPG_ * (DH_ + 1), NTT_], [1, DH_]])
                vsrc = self.v_hbm.ap().rearrange("(a p) n -> p a n", p=128)[
                    :, :, hh2 * DH_:(hh2 + 1) * DH_]
                self.inc(ge.dma_start(vdst, vsrc), "vhgp", 16)
            self.c["_vhgc"] = self.cur("vhgp")
            self._vhgsem = "vhgp"
            self._pf_attn = True

        if self.taps:
            self.wait(sy, "dve")
            self.inc(sy.dma_start(self.tap_wT.ap(), self.w_T[:, :]), "tapst", 16)
            self.inc(sy.dma_start(self.tap_r13.ap(), r13[:, :]), "tapst", 16)

    def phase_attn(self):
        nc, te, ve, se, sy, ge = self.nc, self.te, self.ve, self.se, self.sy, self.ge
        S, D, H, DH, E = self.S, self.D, self.H, self.DH, self.E
        NDC, NTT, NMQ, HPT = self.NDC, self.NTT, self.NMQ, self.HPT
        QC, NQC, NKC, HPG, NHG = self.QC, self.NQC, self.NKC, self.HPG, self.NHG
        WBR, LTP = self.WBR, self.LTP
        banks, seq = self.banks, self.seq
        HDx = self.HDx
        sq = f"s{seq}"
        C4 = min(512, S)
        NT4 = S // C4
        CD = min(512, D)

        adm = self.sb("adm", [128, NMQ * S], BF16)
        lrec_f = [self.sb(f"lrec_f{i}", [1, QC], F32) for i in range(HPG)]
        lrec = [self.sb(f"lrec{i}", [1, QC], BF16) for i in range(HPG)]
        lbc = [self.sb(f"lbc{i}", [64, QC], BF16) for i in range(HPG)]
        smb = self.sb("smb", [128, 16], BF16)
        nmrow = self.sb("nmrow", [1, H], BF16)
        owT_sb = self.sb("owT_sb", [128, NMQ * D], BF16)
        self.wait(sy, "wcow", 16)
        self.inc(sy.dma_start(owT_sb[:, :].rearrange("p (c n) -> p c n", c=NMQ),
                              self.owT_bf.ap().rearrange("(c p) n -> p c n", p=128)),
                 "owl", 16)
        xres = [self.sb(f"xres{i}", [128, CD]) for i in range(2)]
        fres = [self.sb(f"fres{i}", [128, CD], BF16) for i in range(2)]
        ores = [self.sb(f"ores{i}", [128, CD]) for i in range(2)]
        ost = [self.sb(f"ost{i}", [128, CD]) for i in range(2)]
        attn_outer_st = self.st

        if "b" not in self.aparts:
            self.c["_epi_done"] = self.cur("dve")
            self.c["_skipflash"] = 1
        sm = self.small
        if "b" in self.aparts:
            self.wait(ve, "sc")
            self.dw()
            self.inc(ve.tensor_tensor(sm[0:H, 0:1], self.qm2[0:H, 0:1],
                                      self.qm2[0:H, 1:2], op=ALU.mult), "dve")
            self.wait(se, "dve")
            self.sw()
            self.inc(se.activation(sm[0:H, 1:2], sm[0:H, 0:1], AF.Ln,
                                   bias=self.eps[0:H, 2:3], scale=1.0), "sc")
            self.sw()
            self.inc(se.activation(sm[0:H, 1:2], sm[0:H, 1:2], AF.Exp,
                                   bias=self.eps[0:H, 3:4], scale=0.5), "sc")
            self.wait(ve, "sc")
            self.dw()
            self.inc(ve.tensor_scalar(smb[0:H, 0:1], sm[0:H, 1:2], -1.0, None,
                                      op0=ALU.mult), "dve")
            bkt = banks[0][:, :].bitcast(BF16)
            self.wait(te, "dve")
            mm = te.transpose(bkt[0:16, 0:128], smb[:, :], self.id_sb[:, :])
            self.inc(mm, "pe")
            self.wait(ve, "pe")
            self.dw()
            self.inc(ve.tensor_copy(nmrow[:, :], bkt[0:1, 0:H]), "dve")
            self.wait(te, "dve")
            mm = te.matmul(banks[1][:, 0:H], self.ones1[:, :], nmrow[:, :], start=True, stop=True)
            self.inc(mm, "pe")
            self.wait(ve, "pe")
            self.dw()
            self.inc(ve.tensor_copy(self.biasmat[:, :], banks[1][:, 0:H]), "dve")
            self.c["_epi_done"] = self.cur("dve")

        fl_scope = contextlib.ExitStack()
        self.st = fl_scope
        vhg = self.vhg
        khg = [self.khg0, self.sb("khg1", [128, (HPG // HPT) * S], BF16)]
        qsl = [self.sb(f"qsl{i}", [128, (HPG // HPT) * QC], BF16) for i in range(2)]

        def emit_qsl_load(hg_i, qc_i):
            gb_ = hg_i * NQC + qc_i
            qs_ = gb_ % 2
            h0_ = hg_i * HPG
            q0_ = qc_i * QC
            kfk = f"_qslfree{qs_}"
            if kfk in self.c:
                self.wait(sy, "pe", self.c[kfk])
            self.inc(sy.dma_start(
                qsl[qs_][:, :].rearrange("p (c n) -> p c n", c=HPG // HPT),
                self.q_hbm.ap().rearrange("(c p) n -> p c n", p=128)[
                    :, h0_ // HPT: (h0_ + HPG) // HPT, q0_:q0_ + QC]),
                f"qsll{qs_}", 16)
            self.c[f"_qslc{qs_}"] = self.cur(f"qsll{qs_}")

        def emit_hg_loads(hg_i):
            if hg_i >= NHG:
                return
            sl_ = hg_i % 2
            h0_ = hg_i * HPG
            kfk = f"_khgfree{sl_}"
            if kfk in self.c:
                self.wait(sy, "pe", self.c[kfk])
            self.inc(sy.dma_start(
                khg[sl_][:, :].rearrange("p (c n) -> p c n", c=HPG // HPT),
                self.k_hbm.ap().rearrange("(c p) n -> p c n", p=128)[
                    :, h0_ // HPT: (h0_ + HPG) // HPT, :]),
                f"khgl{sl_}", 16)
            self.c[f"_khgc{sl_}"] = self.cur(f"khgl{sl_}")
            self._khgsem[sl_] = f"khgl{sl_}"

        def emit_vhg_loads(hg_i):
            if hg_i >= NHG:
                return
            h0_ = hg_i * HPG
            self.wait(sy, "pe", self.c["_vhgfree"])
            for hh2 in range(HPG):
                vdst = AP(vhg[:, :].tensor, vhg[:, :].offset + hh2 * (DH + 1),
                          [list(vhg[:, :].ap[0]), [HPG * (DH + 1), NTT], [1, DH]])
                vsrc = self.v_hbm.ap().rearrange("(a p) n -> p a n", p=128)[
                    :, :, (h0_ + hh2) * DH:(h0_ + hh2 + 1) * DH]
                self.inc(sy.dma_start(vdst, vsrc), "vhgl", 16)
            self.c["_vhgc"] = self.cur("vhgl")
            self._vhgsem = "vhgl"

        brw = [self.sb(f"brw{i}", [128, WBR], BF16) for i in range(HPG)]
        pex = [self.sb(f"pex{i}", [128, QC], BF16) for i in range(4)]
        ptil = [self.sb(f"ptil{i}", [128, QC], BF16) for i in range(4)]
        self.st = attn_outer_st
        self.wait(ge, "ebr", 16)
        self.wait(ge, "pe")
        self.wait(ge, "dve")
        for nm in ("qkst0", "qkst1", "xnst0", "xnst1"):
            if nm in self.c:
                self.wait(ge, nm, self.c[nm])
                self.wait(sy, nm, self.c[nm])
        self.wait(sy, "pe")
        self.wait(sy, "dve")
        self.wait(sy, "sc")
        for i in range(4):
            if f"qkst{i}" in self.c:
                self.wait(sy, f"qkst{i}")
        if "x" in self.aparts and NHG > 0:
            emit_qsl_load(0, 0)
        for hg in range(NHG if "x" in self.aparts else 0):
            h0 = hg * HPG
            sl = hg % 2
            if hg == 0 and not self._pf_attn:
                emit_hg_loads(0)
            if hg > 0 and "_brwfree" in self.c:
                self.wait(ge, "dve", self.c["_brwfree"])
            for hh in range(HPG):
                h = h0 + hh
                win = AP(self.ebr_hbm.ap().tensor, 0, [[1, H * LTP], [1, 1]])
                self.inc(ge.indirect_dma_start(
                    brw[hh][:, :], None, win,
                    IndirectOffsetOnAxis(self.gidx_sb[:, :], 0),
                    element_offset=h * LTP), f"brwg{hh}", 16)
                self.c[f"_brwc{hh}"] = self.cur(f"brwg{hh}")
            for qc in range(NQC):
                q0 = qc * QC
                gb = hg * NQC + qc
                qs = gb % 2
                if qc == 0:
                    emit_hg_loads(hg + 1)
                self.wait(te, f"qsll{qs}", self.c[f"_qslc{qs}"])
                self.wait(te, self._khgsem[sl], self.c[f"_khgc{sl}"])
                self.wait(te, self._vhgsem, self.c["_vhgc"])
                self.wait(te, "dve", self.c["_epi_done"])
                self.wait(te, "dve", self.c["_vones"])
                tiles = [(kc, hh) for hh in range(HPG) for kc in range(NKC)]
                LA = 2
                for i in range(len(tiles) + LA):
                    if i < len(tiles):
                        kc, hh = tiles[i]
                        h = h0 + hh
                        sbk = i % 4
                        bk = banks[sbk]
                        key = f"_exp{sbk}"
                        if f"_pext{sbk}" in self.c:
                            self.wait(te, "dve", self.c[f"_pext{sbk}"])
                        if kc == 0 and "_devac" in self.c:
                            self.wait(te, "dve", self.c["_devac"])
                        if self.taps and "_tapw" in self.c:
                            self.wait(se, "tapst", self.c["_tapw"])
                            self.wait(ve, "tapst", self.c["_tapw"])
                        jt = hh // HPT
                        roff = 64 * (hh % HPT)
                        mm = te.matmul(bk[:, 0:QC],
                                  khg[sl][roff:roff + DH, jt * S + kc * 128: jt * S + (kc + 1) * 128],
                                  qsl[qs][roff:roff + DH, jt * QC:(jt + 1) * QC],
                                  start=True, stop=True)
                        pmm = self.inc(mm, "pe")
                        self.wait(se, "pe", pmm)
                        if f"_pext{sbk}" in self.c:
                            self.wait(se, "dve", self.c[f"_pext{sbk}"])
                        self.sw()
                        self.c[key] = self.inc(
                            se.activation(bk[:, 0:QC], bk[:, 0:QC], AF.Exp,
                                          bias=self.biasmat[:, h:h + 1], scale=1.0),
                            "sc")
                        self.wait(ve, "sc", self.c[key])
                        c0 = (S - 128) - kc * 128 + q0
                        if kc == 0:
                            self.wait(ve, f"brwg{hh}", self.c[f"_brwc{hh}"])
                        if f"_ptu{sbk}" in self.c:
                            self.wait(ve, "pe", self.c[f"_ptu{sbk}"])
                        self.dw()
                        self.c[f"_pext{sbk}"] = self.inc(
                            ve.tensor_tensor(ptil[sbk][:, :], bk[:, 0:QC],
                                             brw[hh][:, c0:c0 + QC], op=ALU.mult),
                            "dve")
                    if i >= LA:
                        kc, hh = tiles[i - LA]
                        sbk = (i - LA) % 4
                        avb = banks[4 + hh]
                        self.wait(te, "dve", self.c[f"_pext{sbk}"])
                        mm = te.matmul(avb[0:DH + 1, 0:QC],
                                  vhg[:, (kc * HPG + hh) * (DH + 1): (kc * HPG + hh + 1) * (DH + 1)],
                                  ptil[sbk][:, :], start=(kc == 0), stop=(kc == NKC - 1))
                        self.c[f"_ptu{sbk}"] = self.inc(mm, "pe")
                self.c[f"_qslfree{qs}"] = self.cur("pe")
                if qc + 1 < NQC:
                    emit_qsl_load(hg, qc + 1)
                elif hg + 1 < NHG:
                    emit_qsl_load(hg + 1, 0)
                if self.taps and seq == 1 and hg == 0 and qc == 0:
                    self.wait(ve, "pe")
                    self.dw()
                    self.inc(ve.tensor_copy(qkv_stage[0][0:65, 0:QC], banks[4][0:65, 0:QC]), "dve")
                    self.wait(sy, "dve")
                    self.inc(sy.dma_start(self.tap_av.ap(), qkv_stage[0][0:65, 0:QC]), "tapst", 16)
                for hh in range(HPG):
                    avb = banks[4 + hh]
                    self.wait(se, "pe")
                    if "_lbccp" in self.c:
                        self.wait(se, "dve", self.c["_lbccp"])
                    self.sw()
                    self.inc(se.activation(lrec_f[hh][:, :], avb[64:65, 0:QC], AF.Ln,
                                           bias=0.0, scale=1.0), "sc")
                    self.sw()
                    self.inc(se.activation(lrec[hh][:, :], lrec_f[hh][:, :], AF.Exp,
                                           bias=0.0, scale=-1.0), "sc")
                    self.c[f"_lr{hh}"] = self.cur("sc")
                for hh in range(HPG):
                    bb = banks[hh % 4]
                    self.wait(te, "sc", self.c[f"_lr{hh}"])
                    self.wait(te, "dve", self.c[f"_pext{hh % 4}"])
                    if hh == 0 and "_lbccp" in self.c:
                        self.wait(te, "dve", self.c["_lbccp"])
                    mm = te.matmul(bb[0:64, 0:QC], self.ones1[:, 0:64], lrec[hh][:, :],
                              start=True, stop=True)
                    self.c[f"_lbcmm{hh}"] = self.inc(mm, "pe")
                for hh in range(HPG):
                    h = h0 + hh
                    avb = banks[4 + hh]
                    bb = banks[hh % 4]
                    self.wait(ve, "pe", self.c[f"_lbcmm{hh}"])
                    self.dw()
                    self.inc(ve.tensor_copy(lbc[hh][:, :], bb[0:64, 0:QC]), "dve")
                    dst_r = 64 * (h % HPT)
                    dst_t = h // HPT
                    self.dw()
                    self.inc(ve.tensor_tensor(
                        adm[dst_r:dst_r + 64, dst_t * S + q0: dst_t * S + q0 + QC],
                        avb[0:64, 0:QC], lbc[hh][:, :], op=ALU.mult), "dve")
                self.c["_lbccp"] = self.cur("dve")
                self.c["_epi_done"] = self.cur("dve")
                if qc == NQC - 1:
                    self.c[f"_khgfree{sl}"] = self.cur("pe")
                    self.c["_vhgfree"] = self.cur("pe")
                    self.c["_brwfree"] = self.cur("dve")
                    emit_vhg_loads(hg + 1)
                for i4 in range(4):
                    self.c[f"_exp{i4}"] = self.cur("sc")
            self.c["_hg_done"] = self.cur("dve")
        self.c["_flash_free"] = self.cur("pe")
        self._pf_attn = False

        if "l" in self.phases and seq + 1 < self.SEQS:
            saved = self.seq
            self.seq = seq + 1
            self.phase_ln()
            self.seq = saved

        if self.taps:
            self.wait(sy, "dve")
            self.inc(sy.dma_start(self.tap_adm.ap(), adm[:, :]), "tapst", 16)
        self.wait(te, "sc")
        self.wait(te, "dve")
        fl_scope.close()
        self.wait(sy, "dve")
        for hh_ in range(HPG):
            if f"brwg{hh_}" in self.c:
                self.wait(sy, f"brwg{hh_}")
        for ts_ in range(self.TC // 128):
            if f"dstg{ts_}" in self.c:
                self.wait(sy, f"dstg{ts_}")
        for tt in range(NTT if "o" in self.aparts else 0):
            for dcb in range(D // CD):
                idx = tt * (D // CD) + dcb
                bk = banks[idx % 2]
                slot = idx % 2
                key = f"_ors{slot}"
                if key in self.c:
                    self.wait(ge, f"sto{slot}", self.c[key])
                self.inc(ge.dma_start(xres[slot][:, :],
                                      self.x_sh[seq, tt * 128:(tt + 1) * 128,
                                                dcb * CD:(dcb + 1) * CD]),
                         f"xrg{slot}", 16)
                self.inc(ge.dma_start(fres[slot][:, :],
                                      self.ffn_hbm.ap()[tt * 128:(tt + 1) * 128,
                                                        dcb * CD:(dcb + 1) * CD]),
                         f"xrg{slot}", 16)
                okey = f"_opev{slot}"
                if okey in self.c:
                    self.wait(te, "dve", self.c[okey])
                elif idx < 2:
                    self.wait(te, "dve")
                    self.wait(te, "sc")
                if idx == 0:
                    self.wait(te, "owl", self.cur("owl"))
                for j in range(NMQ):
                    mm = te.matmul(bk[:, 0:CD], adm[:, j * S + tt * 128: j * S + (tt + 1) * 128],
                              owT_sb[:, j * D + dcb * CD: j * D + (dcb + 1) * CD],
                              start=(j == 0), stop=(j == NMQ - 1))
                self.inc(mm, "pe")
                self.wait(ve, "pe")
                self.wait(ve, f"xrg{slot}")
                self.dw()
                self.inc(ve.scalar_tensor_tensor(ores[slot][:, :], bk[:, 0:CD], 0.0,
                                                 xres[slot][:, :], op0=ALU.bypass,
                                                 op1=ALU.add), "dve")
                self.c[okey] = self.cur("dve")
                self.dw()
                self.inc(ve.tensor_tensor(ost[slot][:, :], ores[slot][:, :],
                                          fres[slot][:, :], op=ALU.add), "dve")
                self.wait(sy, "dve")
                self.inc(sy.dma_start(
                    self.out_sh[seq, tt * 128:(tt + 1) * 128, dcb * CD:(dcb + 1) * CD],
                    ost[slot][:, :]), f"sto{slot}", 16)
                self.c[key] = self.cur(f"sto{slot}")


def kernel(L0_attn_x, L0_attn_emb_weight, L0_attn_qkv_w, L0_attn_o_w,
           L0_ffn_norm_w, L0_ffn_w1, L0_ffn_w3, L0_ffn_w2, L0_ffn_gate_w,
           L0_ffn_a1, L0_ffn_b1, L0_ffn_a3, L0_ffn_b3, L0_ffn_a2, L0_ffn_b2):
    cfg = CFG
    weights = dict(qkv_w=L0_attn_qkv_w, o_w=L0_attn_o_w, emb_weight=L0_attn_emb_weight,
                   norm_w=L0_ffn_norm_w, w1=L0_ffn_w1, w3=L0_ffn_w3, w2=L0_ffn_w2,
                   gate_w=L0_ffn_gate_w, a1=L0_ffn_a1, b1=L0_ffn_b1, a3=L0_ffn_a3,
                   b3=L0_ffn_b3, a2=L0_ffn_a2, b2=L0_ffn_b2)
    weights = {k: np.asarray(v, np.float32) for k, v in weights.items()}
    hp = host_prep(cfg, weights)
    x = np.asarray(L0_attn_x, np.float32)
    B = x.shape[0]
    per = B // N_CORES

    nc = B_build_cached(cfg)
    in_maps = []
    for c in range(N_CORES):
        m = dict(hp)
        m["x_sh"] = np.ascontiguousarray(x[c * per:(c + 1) * per])
        in_maps.append(m)
    res = run_bass_kernel_spmd(nc, in_maps, core_ids=list(range(N_CORES)))
    out = np.concatenate([r["out_sh"] for r in res.results], axis=0)
    return out


_cache = {}


def B_build_cached(cfg):
    key = tuple(sorted(cfg.items()))
    if key not in _cache:
        _cache[key] = B(cfg).build()
    return _cache[key]
